# version 05
# baseline (speedup 1.0000x reference)
import numpy as np
import ml_dtypes
from contextlib import ExitStack, contextmanager

import concourse.bass as bass
import concourse.mybir as mybir
from concourse.bass_utils import run_bass_kernel_spmd

F32 = mybir.dt.float32
BF16 = mybir.dt.bfloat16
AF = mybir.ActivationFunctionType
ALU = mybir.AluOpType
AX = mybir.AxisListType
BF = ml_dtypes.bfloat16

B, S, D = 4, 4096, 2048
NCORES = 8
TOK = 2048
NT = TOK // 128
ND = D // 128

ENGS = ("sync", "vector", "scalar", "tensor", "gpsimd")


class Sem:
    __slots__ = ("name", "h", "n", "mark", "kind", "retired", "base")

    def __init__(self, name, h, kind):
        self.name = name
        self.h = h
        self.n = 0
        self.mark = 0
        self.base = 0
        self.kind = kind
        self.retired = False


class _Rec:
    __slots__ = ("op", "args", "kw", "incs", "eng")

    def __init__(self, op, args, kw, eng):
        self.op = op
        self.args = args
        self.kw = kw
        self.eng = eng
        self.incs = []

    def inc(self, sem, n=None):
        assert not sem.retired, f"inc on retired sem {sem.name}"
        if "dma" in self.op:
            cls = "sw" if self.eng == "gpsimd" else "hw"
        elif "collective" in self.op:
            cls = "cc"
        else:
            cls = "c"
        need = {"sw": "sw", "cc": "cc", "hw": "gen", "c": None}[cls]
        assert need is None or need == sem.kind, \
            f"sem {sem.name} kind {sem.kind} inc'd by {cls} ({self.eng}.{self.op})"
        if n is None:
            n = 16 if "dma" in self.op else 1
        sem.n += n
        self.incs.append((sem, n))
        return self


class _Eng:
    def __init__(self, prog, name):
        self._p = prog
        self._name = name

    def __getattr__(self, op):
        def emit(*args, **kw):
            r = _Rec(op, args, kw, self._name)
            self._p.q[self._name].append(r)
            return r
        return emit

    def wait(self, sem, val=None):
        v = sem.n if val is None else sem.base + val
        self._p.q[self._name].append(("wait", sem, v))
        return v

    def wait_abs(self, sem, val):
        self._p.q[self._name].append(("wait", sem, val))
        return val


class Prog:

    def __init__(self, nc, ctx):
        self.nc = nc
        self.ctx = ctx
        self.q = {e: [] for e in ENGS}
        for e in ENGS:
            setattr(self, e, _Eng(self, e))
        self._semnum = 0
        self._scopes = [ctx]
        self._sems = []
        self._pool = {"gen": [], "sw": [], "cc": []}
        self._pending = []
        self._scope_sems = []

    def sem(self, name=None, kind="gen"):
        self._semnum += 1
        name = name or f"s{self._semnum}"
        pool = self._pool[kind]
        if pool:
            s = pool.pop()
            s.name = name
            s.retired = False
            s.base = s.n
        else:
            h = self.ctx.enter_context(self.nc.semaphore(f"{name}_{self._semnum}"))
            s = Sem(name, h, kind)
            self._sems.append(s)
        if self._scope_sems:
            self._scope_sems[-1].append(s)
        return s

    def barrier(self):
        sems = [s for s in self._sems if s.n > s.mark]
        for e in ENGS:
            for s in sems:
                self.q[e].append(("wait", s, s.n))
        for s in sems:
            s.mark = s.n
        for s in self._pending:
            self._pool[s.kind].append(s)
        self._pending = []

    def sb(self, name, shape, dt):
        return self._scopes[-1].enter_context(
            self.nc.sbuf_tensor(name, list(shape), dt))

    def psum(self, name, shape, dt=F32):
        return self._scopes[-1].enter_context(
            self.nc.psum_tensor(name, list(shape), dt))

    @contextmanager
    def scope(self):
        es = ExitStack()
        self._scopes.append(es)
        self._scope_sems.append([])
        try:
            yield
        finally:
            self._scopes.pop()
            es.close()
            for s in self._scope_sems.pop():
                s.retired = True
                self._pending.append(s)

    def finalize(self):
        blk = self.ctx.enter_context(self.nc.Block())
        for ename in ENGS:
            items = self.q[ename]
            if not items:
                continue

            def make(items):
                def body(eng):
                    for it in items:
                        if isinstance(it, tuple):
                            _, sem, v = it
                            eng.wait_ge(sem.h, v)
                        else:
                            instr = getattr(eng, it.op)(*it.args, **it.kw)
                            for sem, n in it.incs:
                                instr.then_inc(sem.h, n)
                return body
            getattr(blk, ename)(make(items))


def vchain(p, sem, fns):
    for fn in fns:
        fn().inc(sem)
        p.vector.wait(sem)


def emit_ln_stats(p, x_tiles, sq_scr, stats, n0, vch, sch, width=D, eps=1e-5):
    n = len(x_tiles)
    s_sum = stats[:, n0 + 0 * n:n0 + 1 * n]
    s_sq = stats[:, n0 + 1 * n:n0 + 2 * n]
    s_nm = stats[:, n0 + 2 * n:n0 + 3 * n]
    s_r = stats[:, n0 + 3 * n:n0 + 4 * n]
    for i, xt in enumerate(x_tiles):
        p.scalar.activation(sq_scr[:, 0:width], xt, AF.Square,
                            accum_out=s_sq[:, i:i + 1]).inc(sch)
        p.scalar.wait(sch)
        p.vector.tensor_reduce(s_sum[:, i:i + 1], xt, AX.X, ALU.add).inc(vch)
        p.vector.wait(vch)
    p.vector.wait(sch)
    p.vector.wait(vch)
    vchain(p, vch, [
        lambda: p.vector.tensor_scalar(s_nm, s_sum, 1.0 / width, None, ALU.mult),
        lambda: p.vector.tensor_tensor(s_r, s_nm, s_nm, ALU.mult),
        lambda: p.vector.tensor_scalar(s_r, s_r, -1.0, eps, ALU.mult, ALU.add),
        lambda: p.vector.scalar_tensor_tensor(s_r, s_sq, 1.0 / width, s_r,
                                              ALU.mult, ALU.add),
    ])
    p.scalar.wait(vch)
    p.scalar.activation(s_r, s_r, AF.Sqrt).inc(sch)
    p.vector.wait(sch)
    vchain(p, vch, [
        lambda: p.vector.reciprocal(s_r, s_r),
        lambda: p.vector.tensor_scalar(s_nm, s_nm, -1.0, None, ALU.mult),
    ])
    return s_nm, s_r


def emit_transpose_tiles(p, src_fn, dst_fn, ntiles, ident, ps_t, mm_t, ev_t,
                         free_sem=None, free_base=0, group=8):
    nb = 2
    for i in range(ntiles):
        b = (i // group) % nb
        slot = i % group
        if i == 0:
            p.tensor.wait_abs(ev_t, ev_t.n)
        if i >= group * nb and slot == 0:
            p.tensor.wait(ev_t, free_base + (i // group - nb + 1))
        p.tensor.transpose(ps_t[:, (b * group + slot) * 128:(b * group + slot + 1) * 128],
                           src_fn(i), ident).inc(mm_t)
        last_in_batch = (slot == group - 1) or (i == ntiles - 1)
        if last_in_batch:
            p.vector.wait(mm_t)
            g0 = (i // group) * group
            for j in range(g0, i + 1):
                bb = (j // group) % nb
                ss = j % group
                p.vector.tensor_copy(dst_fn(j),
                                     ps_t[:, (bb * group + ss) * 128:(bb * group + ss + 1) * 128])
            p.q["vector"][-1].inc(ev_t)


def bcast_col(p, src11, dst_col, ones_row, ps_misc, mm_sem, vch):
    p.tensor.matmul(ps_misc[:, 0:1], ones_row, src11, start=True, stop=True).inc(mm_sem)
    p.vector.wait(mm_sem)
    p.vector.tensor_copy(dst_col, ps_misc[:, 0:1]).inc(vch)
    p.vector.wait(vch)


HFC = 5120
NF_FC = HFC // 128


def emit_l1_ffn(p, x, w_fcT, w_projT, twfc_d, twpj_d, ident_i, ones_col_i,
                ones_row_i, y):
    ident = p.sb("ff_ident", (128, 128), BF16)
    ones_c = p.sb("ff_onesc", (128, 1), F32)
    ones_r = p.sb("ff_onesr", (1, 128), F32)
    scal = p.sb("ff_scal", (128, 8), F32)
    ps_misc = p.psum("ff_psmisc", (128, 16))

    ld_c = p.sem("ld_c")
    vch = p.sem("vch")
    sch = p.sem("sch")
    mm_s = p.sem("mm_s")
    tern_done = p.sem("tern_done")

    p.sync.dma_start(out=ident[:, :], in_=ident_i[:, :]).inc(ld_c)
    p.sync.dma_start(out=ones_c[:, :], in_=ones_col_i[:, :]).inc(ld_c)
    p.sync.dma_start(out=ones_r[:, :], in_=ones_row_i[:, :]).inc(ld_c)

    with p.scope():
        CW = 2560
        wbuf = p.sb("ff_wbuf", (128, 2 * CW), F32)
        t1b = p.sb("ff_t1", (128, CW), F32)
        t2b = p.sb("ff_t2", (128, CW), F32)
        obuf = p.sb("ff_obuf", (128, 2 * CW), BF16)
        tstat = p.sb("ff_tstat", (128, 120), F32)

        ld_w = [p.sem("ld_w0"), p.sem("ld_w1")]
        ab_v = p.sem("ab_v")
        tern_v = p.sem("tern_v")
        tern_st = [p.sem("tern_st0"), p.sem("tern_st1")]

        chunks = []
        for i in range(ND):
            for sbi in range(2):
                chunks.append(("fc", i, sbi,
                               w_fcT[i * 128:(i + 1) * 128, sbi * CW:(sbi + 1) * CW]))
        for i in range(NF_FC):
            chunks.append(("pj", i, 0, w_projT[i * 128:(i + 1) * 128, 0:D]))
        nfc = 2 * ND

        for ci, (kind, i, sbi, src) in enumerate(chunks):
            w = CW if kind == "fc" else D
            slot = ci % 2
            buf = wbuf[:, slot * CW:slot * CW + w]
            if ci >= 2:
                p.sync.wait(ab_v, ci - 1)
            p.sync.dma_start(out=buf, in_=src).inc(ld_w[slot])
            p.scalar.wait(ld_w[slot])
            p.scalar.activation(obuf[:, slot * CW:slot * CW + w], buf, AF.Abs,
                                accum_out=tstat[:, ci:ci + 1]).inc(ab_v)
        p.vector.wait(ab_v)
        vchain(p, vch, [
            lambda: p.vector.tensor_reduce(tstat[:, 112:113], tstat[:, 0:nfc], AX.X, ALU.add),
            lambda: p.vector.tensor_reduce(tstat[:, 113:114], tstat[:, nfc:nfc + NF_FC], AX.X, ALU.add),
        ])
        p.tensor.wait(ld_c)
        p.tensor.wait(vch)
        p.tensor.matmul(ps_misc[0:1, 8:10], ones_c[:, 0:1], tstat[:, 112:114],
                        start=True, stop=True).inc(mm_s)
        p.vector.wait(mm_s)
        vchain(p, vch, [
            lambda: p.vector.tensor_scalar(tstat[0:1, 116:117], ps_misc[0:1, 8:9],
                                           1.0 / (D * HFC), 1e-5, ALU.mult, ALU.max),
            lambda: p.vector.tensor_scalar(tstat[0:1, 117:118], ps_misc[0:1, 9:10],
                                           1.0 / (D * HFC), 1e-5, ALU.mult, ALU.max),
            lambda: p.vector.tensor_scalar(tstat[0:1, 118:119], tstat[0:1, 116:117],
                                           0.7, None, ALU.mult),
            lambda: p.vector.tensor_scalar(tstat[0:1, 119:120], tstat[0:1, 117:118],
                                           0.7, None, ALU.mult),
        ])
        p.tensor.wait(vch)
        p.tensor.matmul(ps_misc[:, 0:1], ones_r[0:1, :], tstat[0:1, 116:117], start=True, stop=True)
        p.tensor.matmul(ps_misc[:, 1:2], ones_r[0:1, :], tstat[0:1, 118:119], start=True, stop=True)
        p.tensor.matmul(ps_misc[:, 2:3], ones_r[0:1, :], tstat[0:1, 117:118], start=True, stop=True)
        p.tensor.matmul(ps_misc[:, 3:4], ones_r[0:1, :], tstat[0:1, 119:120],
                        start=True, stop=True).inc(mm_s)
        p.vector.wait(mm_s)
        vchain(p, vch, [
            lambda: p.vector.tensor_copy(scal[:, 0:1], ps_misc[:, 0:1]),
            lambda: p.vector.tensor_copy(scal[:, 1:2], ps_misc[:, 1:2]),
            lambda: p.vector.tensor_copy(scal[:, 3:4], ps_misc[:, 2:3]),
            lambda: p.vector.tensor_copy(scal[:, 4:5], ps_misc[:, 3:4]),
            lambda: p.vector.tensor_scalar(scal[:, 2:3], scal[:, 1:2], -1.0, None, ALU.mult),
            lambda: p.vector.tensor_scalar(scal[:, 5:6], scal[:, 4:5], -1.0, None, ALU.mult),
        ])

        for ci, (kind, i, sbi, src) in enumerate(chunks):
            w = CW if kind == "fc" else D
            slot = ci % 2
            buf = wbuf[:, slot * CW:slot * CW + w]
            ob = obuf[:, slot * CW:slot * CW + w]
            if ci >= 2:
                p.sync.wait(tern_st[ci % 2], 16 * (ci // 2))
            else:
                p.sync.wait(ab_v)
            p.sync.dma_start(out=buf, in_=src).inc(ld_w[slot])
            thr, nthr, am = ((scal[:, 1:2], scal[:, 2:3], scal[:, 0:1]) if kind == "fc"
                             else (scal[:, 4:5], scal[:, 5:6], scal[:, 3:4]))
            p.vector.wait(ld_w[slot])
            p.vector.wait(tern_v)
            p.vector.tensor_scalar(t1b[:, 0:w], buf, thr, am, ALU.is_gt, ALU.mult)
            p.vector.tensor_scalar(t2b[:, 0:w], buf, nthr, am, ALU.is_lt, ALU.mult).inc(tern_v)
            p.vector.wait(tern_v)
            p.vector.tensor_tensor(ob, t1b[:, 0:w], t2b[:, 0:w], ALU.subtract).inc(tern_v)
            p.scalar.wait(tern_v)
            if kind == "fc":
                f0 = sbi * (CW // 128)
                dst = twfc_d[f0:f0 + CW // 128, i, :, :].rearrange("f p c -> p f c")
                p.scalar.dma_start(out=dst, in_=ob.rearrange("p (f c) -> p f c", c=128)).inc(tern_st[ci % 2])
            else:
                dst = twpj_d[:, i, :, :].rearrange("s p c -> p s c")
                p.scalar.dma_start(out=dst, in_=ob.rearrange("p (s c) -> p s c", c=512)).inc(tern_st[ci % 2])
        p.scalar.wait(tern_st[0])
        p.scalar.wait(tern_st[1])
        p.scalar.nop().inc(tern_done)

    p.barrier()
    with p.scope():
        x_span = p.sb("ff_xspan", (128, 4 * D), F32)
        sq_scr = p.sb("ff_sq", (128, D), BF16)
        stats = p.sb("ff_stats", (128, 16), F32)
        xn_tm = p.sb("ff_xntm", (128, 4 * D), BF16)
        xn_fm = p.sb("ff_xnfm", (128, ND * 512), BF16)
        hbuf = p.sb("ff_h", (128, NF_FC * 512), BF16)
        relu_scr = p.sb("ff_relu", (128, 2 * 512), F32)
        wfc_t = p.sb("ff_wfct", (128, 2 * ND * 128), BF16)
        wpj_t = p.sb("ff_wpjt", (128, NF_FC * 512), BF16)
        yout = p.sb("ff_y", (128, 2 * 512), F32)

        ps_g = p.psum("ff_psg", (128, 2 * 512))
        ps_o = p.psum("ff_pso", (128, 2 * 512))
        ps_t = p.psum("ff_pst", (128, 2 * 8 * 128), BF16)

        ld_x = p.sem("ld_x")
        ld_f = [p.sem("ld_f0", "sw"), p.sem("ld_f1", "sw")]
        ld_p = p.sem("ld_p", "sw")
        mm_g = p.sem("mm_g")
        mm_o = p.sem("mm_o")
        mm_t = p.sem("mm_t")
        ev_t = p.sem("ev_t")
        ev_g = p.sem("ev_g")
        ev_g2 = p.sem("ev_g2")
        ev_o = p.sem("ev_o")
        st_y = [p.sem("st_y0"), p.sem("st_y1")]
        sp_done = p.sem("sp_done")

        ev_t_batches = 0
        for v in range(4):
            if v > 0:
                p.sync.wait(ev_o, v * 16)
            p.sync.wait(tern_done, 1)
            p.sync.dma_start(out=x_span[:, :].rearrange("p (t d) -> p t d", d=D),
                             in_=x[v * 512:(v + 1) * 512, :].rearrange("(t p) d -> p t d", p=128)
                             ).inc(ld_x)
            p.vector.wait(ld_x)
            p.scalar.wait(ld_x)
            xt = [x_span[:, i * D:(i + 1) * D] for i in range(4)]
            s_nm, s_r = emit_ln_stats(p, xt, sq_scr, stats, 0, p.sem(f"lnv{v}"), p.sem(f"lns{v}"))
            if v > 0:
                p.vector.wait(mm_t, 64 * v)
            for i in range(4):
                p.vector.tensor_scalar(xn_tm[:, i * D:(i + 1) * D], xt[i],
                                       s_nm[:, i:i + 1], s_r[:, i:i + 1],
                                       ALU.add, ALU.mult).inc(ev_g)
            p.tensor.wait(ev_g)
            p.tensor.wait(ld_c)

            def srcf(i):
                tt, dt = divmod(i, ND)
                return xn_tm[:, tt * D + dt * 128:tt * D + (dt + 1) * 128]

            def dstf(i):
                tt, dt = divmod(i, ND)
                return xn_fm[:, dt * 512 + tt * 128:dt * 512 + (tt + 1) * 128]
            emit_transpose_tiles(p, srcf, dstf, 64, ident[:, :], ps_t, mm_t, ev_t,
                                 free_base=ev_t_batches)
            ev_t_batches += 8
            for f in range(NF_FC):
                gi = v * NF_FC + f
                slot = f % 2
                if gi == 0:
                    p.gpsimd.wait(tern_done, 1)
                if gi >= 2:
                    p.gpsimd.wait(mm_g, (gi - 1) * ND)
                p.gpsimd.dma_start(
                    out=wfc_t[:, slot * ND * 128:(slot + 1) * ND * 128].rearrange(
                        "p (t c) -> p t c", c=128),
                    in_=twfc_d[f].rearrange("t p c -> p t c")).inc(ld_f[slot])
                p.tensor.wait(ld_f[slot])
                if f == 0:
                    p.tensor.wait(ev_t)
                if gi >= 2:
                    p.tensor.wait(ev_g, v * 4 + 4 + gi - 1)
                for k in range(ND):
                    p.tensor.matmul(ps_g[:, slot * 512:(slot + 1) * 512],
                                    wfc_t[:, (slot * ND + k) * 128:(slot * ND + k + 1) * 128],
                                    xn_fm[:, k * 512:(k + 1) * 512],
                                    start=(k == 0), stop=(k == ND - 1)).inc(mm_g)
                p.vector.wait(mm_g)
                if gi >= 2:
                    p.vector.wait(ev_g2, gi - 1)
                p.vector.tensor_scalar(relu_scr[:, slot * 512:(slot + 1) * 512],
                                       ps_g[:, slot * 512:(slot + 1) * 512],
                                       0.0, None, ALU.max).inc(ev_g)
                p.scalar.wait(ev_g)
                p.scalar.activation(hbuf[:, f * 512:(f + 1) * 512],
                                    relu_scr[:, slot * 512:(slot + 1) * 512],
                                    AF.Square).inc(ev_g2)
            for ds in range(4):
                p.gpsimd.wait(mm_o, (v * 4 + ds) * 4 * NF_FC)
                p.gpsimd.dma_start(out=wpj_t[:, :].rearrange("p (f c) -> p f c", c=512),
                                   in_=twpj_d[ds].rearrange("f p c -> p f c")).inc(ld_p)
                p.tensor.wait(ld_p)
                if ds == 0:
                    p.tensor.wait(ev_g2)
                for tt in range(4):
                    slot = (ds * 4 + tt) % 2
                    if v * 16 + ds * 4 + tt >= 2:
                        p.tensor.wait(ev_o, v * 16 + ds * 4 + tt - 1)
                    for k in range(NF_FC):
                        p.tensor.matmul(ps_o[:, slot * 512:(slot + 1) * 512],
                                        hbuf[:, k * 512 + tt * 128:k * 512 + (tt + 1) * 128],
                                        wpj_t[:, k * 512:(k + 1) * 512],
                                        start=(k == 0), stop=(k == NF_FC - 1)).inc(mm_o)
                    p.vector.wait(mm_o)
                    if v * 16 + ds * 4 + tt >= 2:
                        p.vector.wait(st_y[(ds * 4 + tt) % 2],
                                      16 * ((v * 16 + ds * 4 + tt) // 2))
                    p.vector.tensor_tensor(yout[:, slot * 512:(slot + 1) * 512],
                                           ps_o[:, slot * 512:(slot + 1) * 512],
                                           x_span[:, tt * D + ds * 512:tt * D + (ds + 1) * 512],
                                           ALU.add).inc(ev_o)
                    p.sync.wait(ev_o)
                    p.sync.dma_start(out=y[(v * 4 + tt) * 128:(v * 4 + tt + 1) * 128,
                                           ds * 512:(ds + 1) * 512],
                                     in_=yout[:, slot * 512:(slot + 1) * 512]).inc(st_y[slot])
        p.sync.wait(st_y[0])
        p.sync.wait(st_y[1])


def build_l1_ffn():
    nc = bass.Bass("TRN2", target_bir_lowering=False, debug=False)
    x = nc.dram_tensor("x", [TOK, D], F32, kind="ExternalInput").ap()
    w_fcT = nc.dram_tensor("w_fcT", [D, HFC], F32, kind="ExternalInput").ap()
    w_projT = nc.dram_tensor("w_projT", [HFC, D], F32, kind="ExternalInput").ap()
    ident_i = nc.dram_tensor("ident", [128, 128], BF16, kind="ExternalInput").ap()
    ones_col_i = nc.dram_tensor("ones_col", [128, 1], F32, kind="ExternalInput").ap()
    ones_row_i = nc.dram_tensor("ones_row", [1, 128], F32, kind="ExternalInput").ap()
    y = nc.dram_tensor("y", [TOK, D], F32, kind="ExternalOutput").ap()
    twfc_d = nc.dram_tensor("twfc_d", [NF_FC, ND, 128, 128], BF16).ap()
    twpj_d = nc.dram_tensor("twpj_d", [4, NF_FC, 128, 512], BF16).ap()
    with ExitStack() as ctx:
        p = Prog(nc, ctx)
        emit_l1_ffn(p, x, w_fcT, w_projT, twfc_d, twpj_d, ident_i, ones_col_i,
                    ones_row_i, y)
        p.finalize()
    return nc


def consts():
    return {
        "ident": np.eye(128, dtype=BF),
        "ones_col": np.ones((128, 1), dtype=np.float32),
        "ones_row": np.ones((1, 128), dtype=np.float32),
    }


def shard_rows(full):
    out = []
    for s in range(NCORES):
        b, half = divmod(s, 2)
        out.append(np.ascontiguousarray(full[b, half * TOK:(half + 1) * TOK]))
    return out


def unshard_rows(shards):
    Dd = shards[0].shape[-1]
    full = np.empty((B, S, Dd), dtype=shards[0].dtype)
    for s in range(NCORES):
        b, half = divmod(s, 2)
        full[b, half * TOK:(half + 1) * TOK] = shards[s]
    return full.reshape(B, S, Dd)


def run_l1_ffn(x_full, w_fc, w_proj):
    nc = build_l1_ffn()
    c = consts()
    w_fcT = np.ascontiguousarray(w_fc.T)
    w_projT = np.ascontiguousarray(w_proj.T)
    shards = shard_rows(x_full)
    in_maps = [dict(x=shards[s], w_fcT=w_fcT, w_projT=w_projT, **c)
               for s in range(NCORES)]
    res = run_bass_kernel_spmd(nc, in_maps, core_ids=list(range(NCORES)))
    return unshard_rows([res.results[s]["y"] for s in range(NCORES)])


HG = 5632
NF_G = HG // 128


def emit_l0_ffn(p, x, gate_wT, up_wT, down_wT, gate_s2, up_s2, down_s_row,
                ident_i, ones_col_i, ones_row_i, y):
    ident = p.sb("g_ident", (128, 128), BF16)
    ones_c = p.sb("g_onesc", (128, 1), F32)
    ones_r = p.sb("g_onesr", (1, 128), F32)
    gs_c = p.sb("g_gsc", (128, NF_G), F32)
    us_c = p.sb("g_usc", (128, NF_G), F32)
    ds_row = p.sb("g_dsrow", (1, D), F32)
    ds_b = p.sb("g_dsb", (128, D), F32)

    x_span = p.sb("g_xspan", (128, 4 * D), F32)
    sq_scr = p.sb("g_sq", (128, D), BF16)
    stats = p.sb("g_stats", (128, 16), F32)
    xn_tm = p.sb("g_xntm", (128, 4 * D), BF16)
    xn_fm = p.sb("g_xnfm", (128, ND * 512), BF16)
    hbuf = p.sb("g_h", (128, NF_G * 512), BF16)
    silu_scr = p.sb("g_silu", (128, 2 * 512), F32)
    us_scr = p.sb("g_us", (128, 2 * 512), F32)
    wg_t = p.sb("g_wgt", (128, 2 * ND * 128), BF16)
    wu_t = p.sb("g_wut", (128, 2 * ND * 128), BF16)
    wd_t = p.sb("g_wdt", (128, NF_G * 512), BF16)
    yout = p.sb("g_y", (128, 2 * 512), F32)
    oscr = p.sb("g_oscr", (128, 2 * 512), F32)

    ps_g = p.psum("g_psg", (128, 2 * 512))
    ps_u = p.psum("g_psu", (128, 2 * 512))
    ps_o = p.psum("g_pso", (128, 2 * 512))
    ps_t = p.psum("g_pst", (128, 2 * 8 * 128), BF16)
    ps_m = ps_o

    ld_c = p.sem("ld_c")
    ld_x = p.sem("ld_x")
    ld_g = [p.sem("ld_g0", "sw"), p.sem("ld_g1", "sw")]
    ld_dn = p.sem("ld_dn", "sw")
    vch = p.sem("vch")
    mm_b = p.sem("mm_b")
    mm_g = p.sem("mm_g")
    mm_o = p.sem("mm_o")
    mm_t = p.sem("mm_t")
    ev_t = p.sem("ev_t")
    ev_s = p.sem("ev_s")
    ev_u = p.sem("ev_u")
    ev_h = p.sem("ev_h")
    ev_o1 = p.sem("ev_o1")
    ev_o2 = p.sem("ev_o2")
    st_y = [p.sem("st_y0"), p.sem("st_y1")]

    p.sync.dma_start(out=ident[:, :], in_=ident_i[:, :]).inc(ld_c)
    p.sync.dma_start(out=ones_c[:, :], in_=ones_col_i[:, :]).inc(ld_c)
    p.sync.dma_start(out=ones_r[:, :], in_=ones_row_i[:, :]).inc(ld_c)
    p.sync.dma_start(out=gs_c[:, :], in_=gate_s2[:, :]).inc(ld_c)
    p.sync.dma_start(out=us_c[:, :], in_=up_s2[:, :]).inc(ld_c)
    p.sync.dma_start(out=ds_row[:, :], in_=down_s_row[:, :]).inc(ld_c)
    p.tensor.wait(ld_c)
    for j in range(4):
        p.tensor.matmul(ps_m[:, 0:512], ones_r[0:1, :], ds_row[0:1, j * 512:(j + 1) * 512],
                        start=True, stop=True).inc(mm_b)
        p.vector.wait(mm_b)
        p.vector.tensor_copy(ds_b[:, j * 512:(j + 1) * 512], ps_m[:, 0:512]).inc(vch)
        p.tensor.wait(vch)

    ev_t_batches = 0
    for v in range(4):
        if v > 0:
            p.sync.wait(ev_o2, v * 16)
        p.sync.dma_start(out=x_span[:, :].rearrange("p (t d) -> p t d", d=D),
                         in_=x[v * 512:(v + 1) * 512, :].rearrange("(t p) d -> p t d", p=128)
                         ).inc(ld_x)
        p.vector.wait(ld_x)
        p.scalar.wait(ld_x)
        xt = [x_span[:, i * D:(i + 1) * D] for i in range(4)]
        s_nm, s_r = emit_ln_stats(p, xt, sq_scr, stats, 0, p.sem(f"glnv{v}"), p.sem(f"glns{v}"))
        if v > 0:
            p.vector.wait(mm_t, 64 * v)
        for i in range(4):
            p.vector.tensor_scalar(xn_tm[:, i * D:(i + 1) * D], xt[i],
                                   s_nm[:, i:i + 1], s_r[:, i:i + 1],
                                   ALU.add, ALU.mult).inc(vch)
        p.tensor.wait(vch)

        def srcf(i):
            tt, dt = divmod(i, ND)
            return xn_tm[:, tt * D + dt * 128:tt * D + (dt + 1) * 128]

        def dstf(i):
            tt, dt = divmod(i, ND)
            return xn_fm[:, dt * 512 + tt * 128:dt * 512 + (tt + 1) * 128]
        emit_transpose_tiles(p, srcf, dstf, 64, ident[:, :], ps_t, mm_t, ev_t,
                             free_base=ev_t_batches)
        ev_t_batches += 8

        for f in range(NF_G):
            gi = v * NF_G + f
            slot = f % 2
            if gi >= 2:
                p.gpsimd.wait(mm_g, (gi - 1) * 2 * ND)
            p.gpsimd.dma_start(
                out=wg_t[:, slot * ND * 128:(slot + 1) * ND * 128].rearrange(
                    "p (t c) -> p t c", c=128),
                in_=gate_wT[:, f * 128:(f + 1) * 128].rearrange("(t p) c -> p t c", p=128)
            ).inc(ld_g[slot])
            p.gpsimd.dma_start(
                out=wu_t[:, slot * ND * 128:(slot + 1) * ND * 128].rearrange(
                    "p (t c) -> p t c", c=128),
                in_=up_wT[:, f * 128:(f + 1) * 128].rearrange("(t p) c -> p t c", p=128)
            ).inc(ld_g[slot])
            p.tensor.wait(ld_g[slot])
            if f == 0:
                p.tensor.wait(ev_t)
            if gi >= 2:
                p.tensor.wait(ev_s, gi - 1)
                p.tensor.wait(ev_u, gi - 1)
            for k in range(ND):
                p.tensor.matmul(ps_g[:, slot * 512:(slot + 1) * 512],
                                wg_t[:, (slot * ND + k) * 128:(slot * ND + k + 1) * 128],
                                xn_fm[:, k * 512:(k + 1) * 512],
                                start=(k == 0), stop=(k == ND - 1)).inc(mm_g)
            for k in range(ND):
                p.tensor.matmul(ps_u[:, slot * 512:(slot + 1) * 512],
                                wu_t[:, (slot * ND + k) * 128:(slot * ND + k + 1) * 128],
                                xn_fm[:, k * 512:(k + 1) * 512],
                                start=(k == 0), stop=(k == ND - 1)).inc(mm_g)
            p.scalar.wait(mm_g, (gi * 2 + 1) * ND)
            if gi >= 2:
                p.scalar.wait(ev_h, gi - 1)
            p.scalar.activation(silu_scr[:, slot * 512:(slot + 1) * 512],
                                ps_g[:, slot * 512:(slot + 1) * 512],
                                AF.Silu, scale=gs_c[:, f:f + 1]).inc(ev_s)
            p.vector.wait(mm_g)
            if gi >= 2:
                p.vector.wait(ev_h, gi - 1)
            p.vector.tensor_scalar(us_scr[:, slot * 512:(slot + 1) * 512],
                                   ps_u[:, slot * 512:(slot + 1) * 512],
                                   us_c[:, f:f + 1], None, ALU.mult).inc(ev_u)
            p.vector.wait(ev_s)
            p.vector.wait(ev_u)
            p.vector.tensor_tensor(hbuf[:, f * 512:(f + 1) * 512],
                                   silu_scr[:, slot * 512:(slot + 1) * 512],
                                   us_scr[:, slot * 512:(slot + 1) * 512],
                                   ALU.mult).inc(ev_h)

        for ds in range(4):
            p.gpsimd.wait(mm_o, (v * 4 + ds) * 4 * NF_G)
            p.gpsimd.dma_start(
                out=wd_t[:, :].rearrange("p (f c) -> p f c", c=512),
                in_=down_wT[:, ds * 512:(ds + 1) * 512].rearrange("(f p) c -> p f c", p=128)
            ).inc(ld_dn)
            p.tensor.wait(ld_dn)
            if ds == 0:
                p.tensor.wait(ev_h)
            for tt in range(4):
                idx = v * 16 + ds * 4 + tt
                slot = idx % 2
                if idx >= 2:
                    p.tensor.wait(ev_o1, idx - 1)
                for k in range(NF_G):
                    p.tensor.matmul(ps_o[:, slot * 512:(slot + 1) * 512],
                                    hbuf[:, k * 512 + tt * 128:k * 512 + (tt + 1) * 128],
                                    wd_t[:, k * 512:(k + 1) * 512],
                                    start=(k == 0), stop=(k == NF_G - 1)).inc(mm_o)
                p.vector.wait(mm_o)
                if idx >= 2:
                    p.vector.wait(ev_o2, idx - 1)
                p.vector.tensor_tensor(oscr[:, slot * 512:(slot + 1) * 512],
                                       ps_o[:, slot * 512:(slot + 1) * 512],
                                       ds_b[:, ds * 512:(ds + 1) * 512],
                                       ALU.mult).inc(ev_o1)
                p.vector.wait(ev_o1)
                if idx >= 2:
                    p.vector.wait(st_y[idx % 2], 16 * (idx // 2))
                p.vector.tensor_tensor(yout[:, slot * 512:(slot + 1) * 512],
                                       oscr[:, slot * 512:(slot + 1) * 512],
                                       x_span[:, tt * D + ds * 512:tt * D + (ds + 1) * 512],
                                       ALU.add).inc(ev_o2)
                p.sync.wait(ev_o2)
                p.sync.dma_start(out=y[(v * 4 + tt) * 128:(v * 4 + tt + 1) * 128,
                                       ds * 512:(ds + 1) * 512],
                                 in_=yout[:, slot * 512:(slot + 1) * 512]).inc(st_y[slot])
    p.sync.wait(st_y[0])
    p.sync.wait(st_y[1])


def build_l0_ffn():
    nc = bass.Bass("TRN2", target_bir_lowering=False, debug=False)
    x = nc.dram_tensor("x", [TOK, D], F32, kind="ExternalInput").ap()
    gate_wT = nc.dram_tensor("gate_wT", [D, HG], F32, kind="ExternalInput").ap()
    up_wT = nc.dram_tensor("up_wT", [D, HG], F32, kind="ExternalInput").ap()
    down_wT = nc.dram_tensor("down_wT", [HG, D], F32, kind="ExternalInput").ap()
    gate_s2 = nc.dram_tensor("gate_s2", [128, NF_G], F32, kind="ExternalInput").ap()
    up_s2 = nc.dram_tensor("up_s2", [128, NF_G], F32, kind="ExternalInput").ap()
    down_s_row = nc.dram_tensor("down_s_row", [1, D], F32, kind="ExternalInput").ap()
    ident_i = nc.dram_tensor("ident", [128, 128], BF16, kind="ExternalInput").ap()
    ones_col_i = nc.dram_tensor("ones_col", [128, 1], F32, kind="ExternalInput").ap()
    ones_row_i = nc.dram_tensor("ones_row", [1, 128], F32, kind="ExternalInput").ap()
    y = nc.dram_tensor("y", [TOK, D], F32, kind="ExternalOutput").ap()
    with ExitStack() as ctx:
        p = Prog(nc, ctx)
        emit_l0_ffn(p, x, gate_wT, up_wT, down_wT, gate_s2, up_s2, down_s_row,
                    ident_i, ones_col_i, ones_row_i, y)
        p.finalize()
    return nc


def run_l0_ffn(x_full, gate_w, up_w, down_w, gate_s, up_s, down_s):
    nc = build_l0_ffn()
    c = consts()
    ins = dict(gate_wT=np.ascontiguousarray(gate_w.T),
               up_wT=np.ascontiguousarray(up_w.T),
               down_wT=np.ascontiguousarray(down_w.T),
               gate_s2=np.ascontiguousarray(gate_s.reshape(NF_G, 128).T),
               up_s2=np.ascontiguousarray(up_s.reshape(NF_G, 128).T),
               down_s_row=np.ascontiguousarray(down_s.reshape(1, D)), **c)
    shards = shard_rows(x_full)
    in_maps = [dict(x=shards[s], **ins) for s in range(NCORES)]
    res = run_bass_kernel_spmd(nc, in_maps, core_ids=list(range(NCORES)))
    return unshard_rows([res.results[s]["y"] for s in range(NCORES)])


NH = 16
NHL = 8
HDIM = 128
ALPHA = 1.0 / (30.0 * np.sqrt(128.0))


def emit_l1_attn(p, x2b, x2own, WqTl, WkTl, WvTl, WoT, bq2, bk2, bv_row,
                 bo_row, slopes_row, kpos, qpos_row, ctxidx, masktri_i,
                 ident_i, ones_colb_i, ones_row_i, y,
                 xnfm_d, ctx_d, cc_out, ctxmy_d,
                 fused=False, cc_xnf_in=None, cc_xnf_out=None, xnfidx=None):
    NBLK = S // 128
    ident = p.sb("a_ident", (128, 128), BF16)
    ones_cb = p.sb("a_onescb", (128, 1), BF16)
    ones_r = p.sb("a_onesr", (1, 128), F32)
    masktri = p.sb("a_mask", (128, 128), BF16)
    bq_c = p.sb("a_bqc", (128, NHL), F32)
    bk_c = p.sb("a_bkc", (128, NHL), F32)
    bv_r = p.sb("a_bvr", (1, NHL * 128), F32)
    slopes = p.sb("a_slopes", (1, NHL), F32)
    kpos_sb = p.sb("a_kpos", (128, NBLK), F32)

    ld_c = p.sem("a_ldc")
    st_f = p.sem("a_stf")
    st_c = p.sem("a_stc")
    cc_s = p.sem("a_ccs", "cc")
    st_m = [p.sem("a_stm0"), p.sem("a_stm1")]
    for dst, src_ in ((ident, ident_i), (ones_cb, ones_colb_i), (ones_r, ones_row_i),
                      (masktri, masktri_i), (bq_c, bq2), (bk_c, bk2),
                      (bv_r, bv_row), (slopes, slopes_row),
                      (kpos_sb, kpos)):
        p.sync.dma_start(out=dst[:, :], in_=src_[:, :]).inc(ld_c)

    with p.scope():
        x_span = p.sb("a_xspan", (128, 4 * D), F32)
        sq_scr = p.sb("a_sq", (128, D), BF16)
        stats = p.sb("a_stats", (128, 16), F32)
        xn_tm = p.sb("a_xntm", (128, 4 * D), BF16)
        xn_fm = p.sb("a_xnfm", (128, ND * 512), BF16)
        ps_t = p.psum("a_pst", (128, 2 * 8 * 128), BF16)

        ld_x = p.sem("a_ldx")
        vch = p.sem("a_vch")
        mm_t = p.sem("a_mmt")
        ev_t = p.sem("a_evt")
        st_xn = p.sem("a_stxn")
        cc_xn = p.sem("a_ccxn", "cc")
        g_xn = [p.sem("a_gxn0", "sw"), p.sem("a_gxn1", "sw")]
        st_tm = [p.sem("a_sttm0"), p.sem("a_sttm1")]

        if fused:
            for v in range(4):
                if v > 0:
                    p.sync.wait(vch)
                p.sync.dma_start(out=x_span[:, :].rearrange("p (t d) -> p t d", d=D),
                                 in_=x2own[v * 512:(v + 1) * 512, :].rearrange(
                                     "(t p) d -> p t d", p=128)).inc(ld_x)
                p.vector.wait(ld_x)
                p.scalar.wait(ld_x)
                xt = [x_span[:, i * D:(i + 1) * D] for i in range(4)]
                s_nm, s_r = emit_ln_stats(p, xt, sq_scr, stats, 0,
                                          p.sem(f"afln{v}"), p.sem(f"aflns{v}"))
                for i in range(4):
                    p.vector.wait(st_xn)
                    p.vector.tensor_scalar(xn_tm[:, i * D:(i + 1) * D], xt[i],
                                           s_nm[:, i:i + 1], s_r[:, i:i + 1],
                                           ALU.add, ALU.mult).inc(vch)
                    p.scalar.wait(vch)
                    p.scalar.dma_start(
                        out=cc_xnf_in[(v * 4 + i) * 128:(v * 4 + i + 1) * 128, :],
                        in_=xn_tm[:, i * D:(i + 1) * D]).inc(st_xn)
            p.gpsimd.wait(st_xn)
            p.gpsimd.collective_compute("AllGather", ALU.bypass,
                                        replica_groups=[list(range(8))],
                                        ins=[cc_xnf_in.opt()],
                                        outs=[cc_xnf_out.opt()]).inc(cc_xn)
            idx_xn = p.sb("a_idxxn", (128, 32), mybir.dt.int32)
            ld_ix = p.sem("a_ldix")
            p.sync.dma_start(out=idx_xn[:, :], in_=xnfidx[:, :]).inc(ld_ix)
            p.gpsimd.wait(ld_ix)
            p.gpsimd.wait(cc_xn)
            ev_t_batches = 0
            for v in range(8):
                for i in range(4):
                    g = v * 4 + i
                    if v >= 1:
                        p.gpsimd.wait(mm_t, 64 * v)
                    p.gpsimd.indirect_dma_start(
                        xn_tm[:, i * D:(i + 1) * D],
                        None, cc_xnf_out,
                        bass.IndirectOffsetOnAxis(idx_xn[:, g:g + 1], 0)).inc(g_xn[0])
                p.tensor.wait(g_xn[0])
                p.tensor.wait(ld_c)
                if v > 0:
                    p.vector.wait(st_f, 16 * v)

                def srcf(i):
                    tt, dt = divmod(i, ND)
                    return xn_tm[:, tt * D + dt * 128:tt * D + (dt + 1) * 128]

                def dstf(i):
                    tt, dt = divmod(i, ND)
                    return xn_fm[:, dt * 512 + tt * 128:dt * 512 + (tt + 1) * 128]
                emit_transpose_tiles(p, srcf, dstf, 64, ident[:, :], ps_t, mm_t, ev_t,
                                     free_base=ev_t_batches)
                ev_t_batches += 8
                p.sync.wait(ev_t)
                p.sync.dma_start(
                    out=xnfm_d[:, v * 512:(v + 1) * 512].rearrange("(t p) c -> p t c", p=128),
                    in_=xn_fm[:, :].rearrange("p (t c) -> p t c", c=512)).inc(st_f)
        else:
            ev_t_batches = 0
            for v in range(8):
                if v > 0:
                    p.sync.wait(vch)
                p.sync.dma_start(out=x_span[:, :].rearrange("p (t d) -> p t d", d=D),
                                 in_=x2b[v * 512:(v + 1) * 512, :].rearrange(
                                     "(t p) d -> p t d", p=128)).inc(ld_x)
                p.vector.wait(ld_x)
                p.scalar.wait(ld_x)
                xt = [x_span[:, i * D:(i + 1) * D] for i in range(4)]
                s_nm, s_r = emit_ln_stats(p, xt, sq_scr, stats, 0,
                                          p.sem(f"alnv{v}"), p.sem(f"alns{v}"))
                if v > 0:
                    p.vector.wait(mm_t, 64 * v)
                    p.vector.wait(st_f, 16 * v)
                for i in range(4):
                    p.vector.tensor_scalar(xn_tm[:, i * D:(i + 1) * D], xt[i],
                                           s_nm[:, i:i + 1], s_r[:, i:i + 1],
                                           ALU.add, ALU.mult).inc(vch)
                p.tensor.wait(vch)
                p.tensor.wait(ld_c)

                def srcf(i):
                    tt, dt = divmod(i, ND)
                    return xn_tm[:, tt * D + dt * 128:tt * D + (dt + 1) * 128]

                def dstf(i):
                    tt, dt = divmod(i, ND)
                    return xn_fm[:, dt * 512 + tt * 128:dt * 512 + (tt + 1) * 128]
                emit_transpose_tiles(p, srcf, dstf, 64, ident[:, :], ps_t, mm_t, ev_t,
                                     free_base=ev_t_batches)
                ev_t_batches += 8
                p.sync.wait(ev_t)
                p.sync.dma_start(
                    out=xnfm_d[:, v * 512:(v + 1) * 512].rearrange("(t p) c -> p t c", p=128),
                    in_=xn_fm[:, :].rearrange("p (t c) -> p t c", c=512)).inc(st_f)

    p.barrier()
    with p.scope():
        xnb = p.sb("a_xnb", (128, 2 * ND * 512), BF16)
        wk_g = p.sb("a_wkg", (128, 4 * ND * 128), BF16)
        wq_g = p.sb("a_wqg", (128, 4 * ND * 128), BF16)
        wv_g = p.sb("a_wvg", (128, ND * 512), BF16)
        K_g = p.sb("a_Kg", (128, 4 * S), BF16)
        V_g = p.sb("a_Vg", (128, NBLK * 512), BF16)
        Q_g = p.sb("a_Qg", (128, 4 * S), BF16)
        bv_b = p.sb("a_bvb", (128, 512), F32)
        qterm = p.sb("a_qterm", (128, 512), F32)
        qrow_s = p.sb("a_qrow", (1, 512), F32)
        qpos_sp = p.sb("a_qposs", (1, 512), F32)
        kterm = p.sb("a_kterm", (128, NBLK), F32)
        srow = p.sb("a_srow", (128, 1), F32)
        e1 = p.sb("a_e1", (128, 2 * 512), F32)
        e2 = p.sb("a_e2", (128, 2 * 512), F32)
        pb = p.sb("a_pb", (128, 2 * 512), BF16)
        sums = p.sb("a_sums", (1, 512), F32)
        rec_s = p.sb("a_recs", (128, 512), F32)
        ctx_sb = p.sb("a_ctx", (128, 512), BF16)

        ps_a = p.psum("a_psa", (128, 2 * 512))
        ps_s = p.psum("a_pss", (128, 2 * 512))
        ps_c = p.psum("a_psc", (128, 512))
        ps_r = p.psum("a_psr", (1, 512))
        ps_b = p.psum("a_psb", (128, 512))

        ld_b = [p.sem("a_ldb0"), p.sem("a_ldb1")]
        ld_qp = p.sem("a_ldqp")
        ld_w = p.sem("a_ldw", "sw")
        vch = p.sem("b_vch")
        mm_a = p.sem("a_mma")
        ev_a = p.sem("a_eva")
        mm_s = p.sem("a_mms")
        ev_s1 = p.sem("a_evs1")
        ev_s3 = p.sem("a_evs3")
        sch_t = p.sem("a_scht")
        ev_m = p.sem("a_evm")
        mm_c = p.sem("a_mmc")
        ev_c = p.sem("a_evc")
        mm_b2 = p.sem("a_mmb2")

        ag = 0
        sc_i = 0

        def accum_group(mms_fn, evac_fn):
            nonlocal ag
            slot = ag % 2
            if ag >= 2:
                p.tensor.wait(ev_a, ag - 1)
            mms_fn(slot)
            p.vector.wait(mm_a)
            evac_fn(slot)
            p.q["vector"][-1].inc(ev_a)
            ag += 1

        for hg in range(2):
            heads = [4 * hg + i for i in range(4)]
            p.gpsimd.wait(mm_a)
            for i, h in enumerate(heads):
                p.gpsimd.dma_start(
                    out=wk_g[:, i * ND * 128:(i + 1) * ND * 128].rearrange(
                        "p (t c) -> p t c", c=128),
                    in_=WkTl[:, h * 128:(h + 1) * 128].rearrange(
                        "(t p) c -> p t c", p=128)).inc(ld_w)
                p.gpsimd.dma_start(
                    out=wq_g[:, i * ND * 128:(i + 1) * ND * 128].rearrange(
                        "p (t c) -> p t c", c=128),
                    in_=WqTl[:, h * 128:(h + 1) * 128].rearrange(
                        "(t p) c -> p t c", p=128)).inc(ld_w)
            p.gpsimd.dma_start(
                out=wv_g[:, :].rearrange("p (t c) -> p t c", c=512),
                in_=WvTl[:, hg * 512:(hg + 1) * 512].rearrange(
                    "(t p) c -> p t c", p=128)).inc(ld_w)
            p.tensor.wait(ld_c)
            p.tensor.wait(ev_c)
            p.tensor.matmul(ps_b[:, :], ones_r[0:1, :],
                            bv_r[0:1, hg * 512:(hg + 1) * 512],
                            start=True, stop=True).inc(mm_b2)
            p.vector.wait(mm_b2)
            p.vector.tensor_copy(bv_b[:, :], ps_b[:, :]).inc(vch)

            def issue_load(s_):
                slot = s_ % 2
                p.sync.wait(mm_a)
                if hg == 0 and s_ < 2:
                    p.sync.wait(st_f, 16 * 8)
                p.sync.dma_start(
                    out=xnb[:, slot * ND * 512:(slot + 1) * ND * 512].rearrange(
                        "p (t c) -> p t c", c=512),
                    in_=xnfm_d[:, s_ * 512:(s_ + 1) * 512].rearrange(
                        "(t p) c -> p t c", p=128)).inc(ld_b[slot])
            issue_load(0)
            issue_load(1)
            for s_ in range(8):
                slot = s_ % 2
                p.tensor.wait(ld_b[slot])
                p.tensor.wait(ld_w)
                p.tensor.wait(vch)
                xb = xnb[:, slot * ND * 512:(slot + 1) * ND * 512]
                for i, h in enumerate(heads):
                    def mk(pslot, i=i):
                        for k in range(ND):
                            p.tensor.matmul(ps_a[:, pslot * 512:(pslot + 1) * 512],
                                            wk_g[:, (i * ND + k) * 128:(i * ND + k + 1) * 128],
                                            xb[:, k * 512:(k + 1) * 512],
                                            start=(k == 0), stop=(k == ND - 1)).inc(mm_a)

                    def ek(pslot, i=i, h=h, s_=s_):
                        p.vector.tensor_scalar(
                            K_g[:, i * S + s_ * 512:i * S + (s_ + 1) * 512],
                            ps_a[:, pslot * 512:(pslot + 1) * 512],
                            bk_c[:, h:h + 1], None, ALU.add)
                    accum_group(mk, ek)
                    def mq(pslot, i=i):
                        for k in range(ND):
                            p.tensor.matmul(ps_a[:, pslot * 512:(pslot + 1) * 512],
                                            wq_g[:, (i * ND + k) * 128:(i * ND + k + 1) * 128],
                                            xb[:, k * 512:(k + 1) * 512],
                                            start=(k == 0), stop=(k == ND - 1)).inc(mm_a)

                    def eq(pslot, i=i, h=h, s_=s_):
                        p.vector.tensor_scalar(
                            Q_g[:, i * S + s_ * 512:i * S + (s_ + 1) * 512],
                            ps_a[:, pslot * 512:(pslot + 1) * 512],
                            bq_c[:, h:h + 1], ALPHA, ALU.add, ALU.mult)
                    accum_group(mq, eq)
                for b_ in range(4):
                    def mv(pslot, b_=b_):
                        for k in range(ND):
                            p.tensor.matmul(ps_a[:, pslot * 512:(pslot + 1) * 512],
                                            xb[:, k * 512 + b_ * 128:k * 512 + (b_ + 1) * 128],
                                            wv_g[:, k * 512:(k + 1) * 512],
                                            start=(k == 0), stop=(k == ND - 1)).inc(mm_a)

                    def ev(pslot, b_=b_, s_=s_):
                        p.vector.tensor_tensor(
                            V_g[:, (s_ * 4 + b_) * 512:(s_ * 4 + b_ + 1) * 512],
                            ps_a[:, pslot * 512:(pslot + 1) * 512],
                            bv_b[:, :], ALU.add)
                    accum_group(mv, ev)
                if s_ + 2 < 8:
                    issue_load(s_ + 2)

            for i, h in enumerate(heads):
                hglob = hg * 4 + i
                p.tensor.wait(ev_a)
                p.tensor.wait(ev_c)
                p.tensor.matmul(ps_b[:, 0:1], ones_r[0:1, :], slopes[0:1, hglob:hglob + 1],
                                start=True, stop=True).inc(mm_b2)
                p.vector.wait(mm_b2)
                vchain(p, vch, [
                    lambda: p.vector.tensor_copy(srow[:, 0:1], ps_b[:, 0:1]),
                    lambda: p.vector.tensor_scalar(kterm[:, :], kpos_sb[:, :],
                                                   srow[:, 0:1], ALPHA,
                                                   ALU.mult, ALU.mult),
                ])
                for v in range(8):
                    nk = 4 * v + 4
                    first_ct = True
                    p.sync.wait(vch)
                    p.sync.dma_start(out=qpos_sp[0:1, :],
                                     in_=qpos_row[0:1, v * 512:(v + 1) * 512]).inc(ld_qp)
                    p.vector.wait(ld_qp)
                    vchain(p, vch, [
                        lambda: p.vector.tensor_scalar(
                            qrow_s[0:1, :], qpos_sp[0:1, :],
                            slopes[0:1, hglob:hglob + 1], -ALPHA,
                            ALU.mult, ALU.mult),
                    ])
                    p.tensor.wait(vch)
                    p.tensor.wait(ev_c)
                    p.tensor.matmul(ps_b[:, :], ones_r[0:1, :], qrow_s[0:1, :],
                                    start=True, stop=True).inc(mm_b2)
                    p.vector.wait(mm_b2)
                    if sc_i >= 1:
                        p.vector.wait(ev_s1)
                    p.vector.tensor_copy(qterm[:, :], ps_b[:, :]).inc(vch)
                    p.tensor.wait(vch)
                    for kt in range(nk):
                        i0 = max(0, kt - 4 * v)
                        q0 = i0 * 128
                        sslot = sc_i % 2
                        last_kt = (kt == nk - 1)
                        if sc_i >= 2:
                            p.tensor.wait_abs(ev_s1, ev_s1.n - 1)
                        p.tensor.matmul(
                            ps_s[:, sslot * 512 + q0:sslot * 512 + 512],
                            K_g[:, i * S + kt * 128:i * S + (kt + 1) * 128],
                            Q_g[:, i * S + v * 512 + q0:i * S + (v + 1) * 512],
                            start=True, stop=True).inc(mm_s)
                        p.vector.wait(mm_s)
                        if sc_i >= 2:
                            p.vector.wait_abs(ev_s3, ev_s3.n - 1)
                        p.vector.scalar_tensor_tensor(
                            e1[:, sslot * 512 + q0:sslot * 512 + 512],
                            ps_s[:, sslot * 512 + q0:sslot * 512 + 512],
                            kterm[:, kt:kt + 1],
                            qterm[:, q0:512],
                            ALU.add, ALU.add).inc(ev_s1)
                        p.scalar.wait(ev_s1)
                        p.scalar.activation(e2[:, sslot * 512 + q0:sslot * 512 + 512],
                                            e1[:, sslot * 512 + q0:sslot * 512 + 512],
                                            AF.Tanh).inc(sch_t)
                        if sc_i >= 2:
                            p.scalar.wait_abs(mm_c, mm_c.n - 2)
                        p.scalar.wait(sch_t)
                        p.scalar.activation(pb[:, sslot * 512 + q0:sslot * 512 + 512],
                                            e2[:, sslot * 512 + q0:sslot * 512 + 512],
                                            AF.Exp, scale=30.0).inc(ev_s3)
                        diag = (kt >= 4 * v)
                        if diag:
                            p.vector.wait(ev_s3)
                            p.vector.tensor_tensor(
                                pb[:, sslot * 512 + q0:sslot * 512 + q0 + 128],
                                pb[:, sslot * 512 + q0:sslot * 512 + q0 + 128],
                                masktri[:, :], ALU.mult).inc(ev_m)
                        p.tensor.wait(ev_s3)
                        if diag:
                            p.tensor.wait(ev_m)
                        p.tensor.matmul(ps_c[:, q0:512],
                                        V_g[:, kt * 512 + i * 128:kt * 512 + (i + 1) * 128],
                                        pb[:, sslot * 512 + q0:sslot * 512 + 512],
                                        start=first_ct, stop=last_kt).inc(mm_c)
                        p.tensor.matmul(ps_r[0:1, q0:512], ones_cb[:, 0:1],
                                        pb[:, sslot * 512 + q0:sslot * 512 + 512],
                                        start=first_ct, stop=last_kt).inc(mm_c)
                        first_ct = False
                        sc_i += 1
                    idx_hv = hglob * 8 + v
                    p.vector.wait(mm_c)
                    if idx_hv > 0:
                        p.vector.wait(st_c, 16 * idx_hv)
                    vchain(p, vch, [
                        lambda: p.vector.tensor_copy(sums[0:1, :], ps_r[0:1, :]),
                        lambda: p.vector.reciprocal(sums[0:1, :], sums[0:1, :]),
                    ])
                    p.tensor.wait(vch)
                    p.tensor.wait(ev_c)
                    p.tensor.matmul(ps_b[:, :], ones_r[0:1, :], sums[0:1, :],
                                    start=True, stop=True).inc(mm_b2)
                    p.vector.wait(mm_b2)
                    p.vector.tensor_copy(rec_s[:, :], ps_b[:, :]).inc(ev_c)
                    p.vector.wait(ev_c)
                    p.vector.tensor_tensor(ctx_sb[:, :], ps_c[:, :], rec_s[:, :],
                                           ALU.mult).inc(ev_c)
                    p.scalar.wait(ev_c)
                    p.scalar.dma_start(out=ctx_d[hglob, :, v * 512:(v + 1) * 512],
                                       in_=ctx_sb[:, :]).inc(st_c)

    p.barrier()
    p.gpsimd.wait(st_c)
    for ha in range(4):
        p.gpsimd.collective_compute(
            "AllGather", ALU.bypass,
            replica_groups=[list(range(8))],
            ins=[ctx_d.ap()[ha * 2:(ha + 1) * 2].opt()],
            outs=[cc_out.ap()[ha:ha + 1].opt()]).inc(cc_s)

    with p.scope():
        idx_s = p.sb("s_idx", (128, NH), mybir.dt.int32)
        selO = p.sb("s_o", (128, 2 * TOK), BF16)
        ld_i = p.sem("s_ldi")
        g_s = [p.sem("s_gs0", "sw"), p.sem("s_gs1", "sw")]
        cc_v = cc_out.ap().rearrange("a r h p (t c) -> (a r h p t) c", c=TOK)
        p.sync.dma_start(out=idx_s[:, :], in_=ctxidx[:, :]).inc(ld_i)
        p.gpsimd.wait(ld_i)
        for h in range(NH):
            slot = h % 2
            if h >= 2:
                p.gpsimd.wait(st_m[h % 2], 16 * (h // 2))
            p.gpsimd.wait(cc_s)
            p.gpsimd.indirect_dma_start(
                selO[:, slot * TOK:(slot + 1) * TOK], None, cc_v,
                bass.IndirectOffsetOnAxis(idx_s[:, h:h + 1], 0)).inc(g_s[slot])
            p.scalar.wait(g_s[slot])
            p.scalar.dma_start(out=ctxmy_d[h, :, :],
                               in_=selO[:, slot * TOK:(slot + 1) * TOK]).inc(st_m[h % 2])

    p.barrier()
    with p.scope():
        ctx_l = p.sb("c_ctx", (128, NH * 512), BF16)
        bo_r = p.sb("c_bor", (1, D), F32)
        wo_t = p.sb("c_wo", (128, ND * 512), BF16)
        x_span = p.sb("c_xspan", (128, 4 * D), F32)
        bo_b = p.sb("c_bob", (128, D), F32)
        oscr = p.sb("c_oscr", (128, 2 * 512), F32)
        yout = p.sb("c_y", (128, 2 * 512), F32)
        ps_o = p.psum("c_pso", (128, 2 * 512))
        ps_b2 = p.psum("c_psb", (128, 512))

        ld_ct = p.sem("c_ldct", "sw")
        ld_wo = p.sem("c_ldwo", "sw")
        ld_x2 = p.sem("c_ldx2")
        vch = p.sem("c_vch")
        mm_bb = p.sem("c_mmb")
        mm_o = p.sem("c_mmo")
        ev_o1 = p.sem("c_evo1")
        ev_o2 = p.sem("c_evo2")
        st_y = [p.sem("c_sty0"), p.sem("c_sty1")]

        p.sync.dma_start(out=bo_r[:, :], in_=bo_row[:, :]).inc(ld_c)
        p.tensor.wait(ld_c)
        for j in range(4):
            p.tensor.matmul(ps_b2[:, :], ones_r[0:1, :], bo_r[0:1, j * 512:(j + 1) * 512],
                            start=True, stop=True).inc(mm_bb)
            p.vector.wait(mm_bb)
            p.vector.tensor_copy(bo_b[:, j * 512:(j + 1) * 512], ps_b2[:, :]).inc(vch)
            p.tensor.wait(vch)

        for v in range(4):
            if v > 0:
                p.sync.wait(ev_o2, v * 16)
            p.sync.dma_start(
                out=x_span[:, :].rearrange("p (t d) -> p t d", d=D),
                in_=x2own[v * 512:(v + 1) * 512, :].rearrange(
                    "(t p) d -> p t d", p=128)).inc(ld_x2)
            p.gpsimd.wait(mm_o)
            p.gpsimd.wait(st_m[0])
            p.gpsimd.wait(st_m[1])
            p.gpsimd.dma_start(
                out=ctx_l[:, :].rearrange("p (h c) -> p h c", c=512),
                in_=ctxmy_d[:, :, v * 512:(v + 1) * 512].rearrange("h p c -> p h c")
            ).inc(ld_ct)
            for ds in range(4):
                p.gpsimd.wait(mm_o, (v * 4 + ds) * 4 * NH)
                p.gpsimd.dma_start(
                    out=wo_t[:, :].rearrange("p (t c) -> p t c", c=512),
                    in_=WoT[:, ds * 512:(ds + 1) * 512].rearrange(
                        "(t p) c -> p t c", p=128)).inc(ld_wo)
                p.tensor.wait(ld_wo)
                p.tensor.wait(ld_ct)
                for tt in range(4):
                    idx = v * 16 + ds * 4 + tt
                    slot = idx % 2
                    if idx >= 2:
                        p.tensor.wait(ev_o1, idx - 1)
                    for k in range(NH):
                        p.tensor.matmul(ps_o[:, slot * 512:(slot + 1) * 512],
                                        ctx_l[:, k * 512 + tt * 128:k * 512 + (tt + 1) * 128],
                                        wo_t[:, k * 512:(k + 1) * 512],
                                        start=(k == 0), stop=(k == NH - 1)).inc(mm_o)
                    p.vector.wait(mm_o)
                    p.vector.wait(ld_x2)
                    if idx >= 2:
                        p.vector.wait(ev_o2, idx - 1)
                    p.vector.tensor_tensor(oscr[:, slot * 512:(slot + 1) * 512],
                                           ps_o[:, slot * 512:(slot + 1) * 512],
                                           bo_b[:, ds * 512:(ds + 1) * 512],
                                           ALU.add).inc(ev_o1)
                    p.vector.wait(ev_o1)
                    if idx >= 2:
                        p.vector.wait(st_y[idx % 2], 16 * (idx // 2))
                    p.vector.tensor_tensor(yout[:, slot * 512:(slot + 1) * 512],
                                           oscr[:, slot * 512:(slot + 1) * 512],
                                           x_span[:, tt * D + ds * 512:tt * D + (ds + 1) * 512],
                                           ALU.add).inc(ev_o2)
                    p.sync.wait(ev_o2)
                    p.sync.dma_start(out=y[(v * 4 + tt) * 128:(v * 4 + tt + 1) * 128,
                                           ds * 512:(ds + 1) * 512],
                                     in_=yout[:, slot * 512:(slot + 1) * 512]).inc(st_y[slot])
        p.sync.wait(st_y[0])
        p.sync.wait(st_y[1])


def build_l1_attn(dbg=False):
    nc = bass.Bass("TRN2", target_bir_lowering=False, debug=False)
    x2b = nc.dram_tensor("x2b", [S, D], F32, kind="ExternalInput").ap()
    x2own = nc.dram_tensor("x2own", [TOK, D], F32, kind="ExternalInput").ap()
    WqTl = nc.dram_tensor("WqTl", [D, NHL * 128], F32, kind="ExternalInput").ap()
    WkTl = nc.dram_tensor("WkTl", [D, NHL * 128], F32, kind="ExternalInput").ap()
    WvTl = nc.dram_tensor("WvTl", [D, NHL * 128], F32, kind="ExternalInput").ap()
    WoT = nc.dram_tensor("WoT", [D, D], F32, kind="ExternalInput").ap()
    bq2 = nc.dram_tensor("bq2", [128, NHL], F32, kind="ExternalInput").ap()
    bk2 = nc.dram_tensor("bk2", [128, NHL], F32, kind="ExternalInput").ap()
    bv_row = nc.dram_tensor("bv_row", [1, NHL * 128], F32, kind="ExternalInput").ap()
    bo_row = nc.dram_tensor("bo_row", [1, D], F32, kind="ExternalInput").ap()
    slopes_row = nc.dram_tensor("slopes_row", [1, NHL], F32, kind="ExternalInput").ap()
    kpos = nc.dram_tensor("kpos", [128, S // 128], F32, kind="ExternalInput").ap()
    qpos_row = nc.dram_tensor("qpos_row", [1, S], F32, kind="ExternalInput").ap()
    ctxidx = nc.dram_tensor("ctxidx", [128, NH], mybir.dt.int32, kind="ExternalInput").ap()
    masktri_i = nc.dram_tensor("masktri", [128, 128], BF16, kind="ExternalInput").ap()
    ident_i = nc.dram_tensor("ident", [128, 128], BF16, kind="ExternalInput").ap()
    ones_colb_i = nc.dram_tensor("ones_colb", [128, 1], BF16, kind="ExternalInput").ap()
    ones_row_i = nc.dram_tensor("ones_row", [1, 128], F32, kind="ExternalInput").ap()
    y = nc.dram_tensor("y", [TOK, D], F32, kind="ExternalOutput").ap()
    kind = dict(kind="ExternalOutput") if dbg else {}
    xnfm_d = nc.dram_tensor("xnfm_d", [D, S], BF16, **kind).ap()
    ctx_d = nc.dram_tensor("ctx_d", [NHL, 128, S], BF16)
    ctx_dbg = nc.dram_tensor("ctx_dbg", [NHL, 128, S], BF16, **kind) if dbg else None
    cc_out = nc.dram_tensor("cc_out", [4, NCORES, 2, 128, S], BF16)
    ctxmy_d = nc.dram_tensor("ctxmy_d", [NH, 128, TOK], BF16, **kind).ap()
    with ExitStack() as ctx:
        p = Prog(nc, ctx)
        emit_l1_attn(p, x2b, x2own, WqTl, WkTl, WvTl, WoT, bq2, bk2, bv_row,
                     bo_row, slopes_row, kpos, qpos_row, ctxidx, masktri_i,
                     ident_i, ones_colb_i, ones_row_i, y,
                     xnfm_d, ctx_d, cc_out, ctxmy_d)
        if dbg:
            dsem = p.sem("dbgcp", "sw")
            p.gpsimd.dma_start(out=ctx_dbg[:, :, :], in_=ctx_d[:, :, :]).inc(dsem)
            p.gpsimd.wait(dsem)
        p.finalize()
    return nc


def mk_ctxidx(b, half):
    idx = np.empty((128, NH), dtype=np.int32)
    for h in range(NH):
        hl = h % NHL
        r = 2 * b + h // NHL
        ha = hl // 2
        base = (((ha * NCORES + r) * 2) + hl % 2) * 128
        idx[:, h] = (base + np.arange(128)) * 2 + half
    return idx


def attn_consts():
    mask = np.zeros((128, 128), dtype=BF)
    for jk in range(128):
        mask[jk, jk:] = 1.0
    kpos = np.empty((128, S // 128), dtype=np.float32)
    for g in range(S // 128):
        kpos[:, g] = g * 128 + np.arange(128)
    return {
        "masktri": mask,
        "ones_colb": np.ones((128, 1), dtype=BF),
        "kpos": kpos,
        "qpos_row": np.arange(S, dtype=np.float32).reshape(1, S),
    }


def run_l1_attn(x2_full, Wq, bq, Wk, bk, Wv, bv, Wo, bo, slopes, dbg=False):
    c = consts()
    ca = attn_consts()
    nc = build_l1_attn(dbg)
    in_maps = []
    for s in range(NCORES):
        b, half = divmod(s, 2)
        hs = slice(half * NHL * 128, (half + 1) * NHL * 128)
        hh = slice(half * NHL, (half + 1) * NHL)
        in_maps.append(dict(
            x2b=np.ascontiguousarray(x2_full[b]),
            x2own=np.ascontiguousarray(x2_full[b][half * TOK:(half + 1) * TOK]),
            WqTl=np.ascontiguousarray(Wq.T[:, hs]),
            WkTl=np.ascontiguousarray(Wk.T[:, hs]),
            WvTl=np.ascontiguousarray(Wv.T[:, hs]),
            WoT=np.ascontiguousarray(Wo.T),
            bq2=np.ascontiguousarray(bq[hs].reshape(NHL, 128).T),
            bk2=np.ascontiguousarray(bk[hs].reshape(NHL, 128).T),
            bv_row=np.ascontiguousarray(bv[hs].reshape(1, NHL * 128)),
            bo_row=np.ascontiguousarray(bo.reshape(1, D)),
            slopes_row=np.ascontiguousarray(slopes[hh].reshape(1, NHL)),
            ctxidx=mk_ctxidx(b, half),
            ident=c["ident"], ones_row=c["ones_row"],
            masktri=ca["masktri"], ones_colb=ca["ones_colb"],
            kpos=ca["kpos"], qpos_row=ca["qpos_row"]))
    res = run_bass_kernel_spmd(nc, in_maps, core_ids=list(range(NCORES)))
    if dbg:
        return res
    return unshard_rows([res.results[s]["y"] for s in range(NCORES)])


MSL = 256
NCH = 32
CH = 512
LRM = 0.01


def emit_l0_attn(p, x_own, x_dsl, W0T_s, W1RT_s, W1C_s, b0_2, b1row, lr_wT,
                 fg_wT, lrfgb, oidx, ident_i, ones_colb_i, ones_row_i, y, cc, out_d):
    ident = p.sb("t_ident", (128, 128), BF16)
    ones_cb = p.sb("t_onescb", (128, 1), BF16)
    ones_r = p.sb("t_onesr", (1, 128), F32)

    ld_c = p.sem("t_ldc")
    for dst, src_ in ((ident, ident_i), (ones_cb, ones_colb_i), (ones_r, ones_row_i)):
        p.sync.dma_start(out=dst[:, :], in_=src_[:, :]).inc(ld_c)

    W0T = p.sb("t_W0T", (128, ND * MSL), F32)
    W0Tb = p.sb("t_W0Tb", (128, ND * MSL), BF16)
    W1RT = p.sb("t_W1RT", (128, ND * MSL), F32)
    W1RTb = p.sb("t_W1RTb", (128, ND * MSL), BF16)
    W1Cb = p.sb("t_W1Cb", (128, ND * MSL), BF16)
    b0c = p.sb("t_b0c", (128, 2), F32)
    b1r = p.sb("t_b1r", (1, MSL), F32)
    b1b = p.sb("t_b1b", (128, MSL), F32)
    lrw = p.sb("t_lrw", (128, ND), BF16)
    fgw = p.sb("t_fgw", (128, ND), BF16)
    lrb = p.sb("t_lrb", (1, 2), F32)
    scal = p.sb("t_scal", (128, 2), F32)
    stat_t = p.sb("t_stat", (1, 8), F32)

    ld_w = p.sem("t_ldw")
    ld_wg = p.sem("t_ldwg", "sw")
    p.sync.dma_start(out=W0T[:, :].rearrange("p (t c) -> p t c", c=MSL),
                     in_=W0T_s.rearrange("(t p) c -> p t c", p=128)).inc(ld_w)
    p.sync.dma_start(out=W1RT[:, :].rearrange("p (t c) -> p t c", c=MSL),
                     in_=W1RT_s.rearrange("(t p) c -> p t c", p=128)).inc(ld_w)
    p.gpsimd.dma_start(out=W1Cb[:, :].rearrange("p (t c) -> p t c", c=MSL),
                     in_=W1C_s.rearrange("(t p) c -> p t c", p=128)).inc(ld_wg)
    p.sync.dma_start(out=b0c[:, :], in_=b0_2[:, :]).inc(ld_w)
    p.sync.dma_start(out=b1r[:, :], in_=b1row[:, :]).inc(ld_w)
    p.gpsimd.dma_start(out=lrw[:, :].rearrange("p (t c) -> p t c", c=1),
                        in_=lr_wT.rearrange("(t p) c -> p t c", p=128),
                        allow_slow_non_contiguous=True).inc(ld_wg)
    p.gpsimd.dma_start(out=fgw[:, :].rearrange("p (t c) -> p t c", c=1),
                        in_=fg_wT.rearrange("(t p) c -> p t c", p=128),
                        allow_slow_non_contiguous=True).inc(ld_wg)
    p.sync.dma_start(out=lrb[:, :], in_=lrfgb[:, :]).inc(ld_w)

    vch = p.sem("t_vch")
    sch = p.sem("t_sch")
    mm_b = p.sem("t_mmb")
    ld_x = p.sem("t_ldx")
    st_pre = p.sem("t_stpre")
    cc_pre = p.sem("t_ccpre", "cc")
    mm_f = p.sem("t_mmf")
    ev_zb = p.sem("t_evzb")
    ev_f = p.sem("t_evf")
    st_h = p.sem("t_sth")
    cc_h = p.sem("t_cch", "cc")
    ld_h = p.sem("t_ldh")
    ld_hf = [p.sem("t_ldhf0"), p.sem("t_ldhf1")]
    mm_p = p.sem("t_mmp")
    ev_pp = p.sem("t_evpp")
    ev_g = p.sem("t_evg")
    mm_t = p.sem("t_mmt")
    ev_t = p.sem("t_evt")
    st_g = p.sem("t_stg")
    cc_g = p.sem("t_ccg", "cc")
    ld_gt = p.sem("t_ldgt")
    ld_gf = [p.sem("t_ldgf0"), p.sem("t_ldgf1")]
    mm_w = p.sem("t_mmw")
    ev_w = p.sem("t_evw")
    mm_d = p.sem("t_mmd")
    ev_d = p.sem("t_evd")
    st_o = p.sem("t_sto")
    cc_o = p.sem("t_cco", "cc")
    ld_xn = p.sem("t_ldxn")
    st_y = p.sem("t_sty")

    with p.scope():
        x_span = p.sb("t_xspan", (128, 4 * D), F32)
        sq_scr = p.sb("t_sq", (128, D), BF16)
        stats = p.sb("t_stats", (128, 16), F32)
        spair = p.sb("t_spair", (128, 2), F32)
        xn_sc = p.sb("t_xnsc", (128, D), BF16)

        for v in range(4):
            if v > 0:
                p.sync.wait(vch)
            p.sync.dma_start(out=x_span[:, :].rearrange("p (t d) -> p t d", d=D),
                             in_=x_own[v * 512:(v + 1) * 512, :].rearrange(
                                 "(t p) d -> p t d", p=128)).inc(ld_x)
            p.vector.wait(ld_x)
            p.scalar.wait(ld_x)
            xt = [x_span[:, i * D:(i + 1) * D] for i in range(4)]
            s_nm, s_r = emit_ln_stats(p, xt, sq_scr, stats, 0,
                                      p.sem(f"tlnv{v}"), p.sem(f"tlns{v}"))
            for i in range(4):
                blk = v * 4 + i
                p.vector.wait(st_pre)
                vchain(p, vch, [
                    lambda i=i: p.vector.tensor_copy(spair[:, 0:1], s_nm[:, i:i + 1]),
                    lambda i=i: p.vector.tensor_copy(spair[:, 1:2], s_r[:, i:i + 1]),
                ])
                p.scalar.wait(vch)
                p.scalar.dma_start(out=cc["st_in"][blk * 128:(blk + 1) * 128, :],
                                   in_=spair[:, :]).inc(st_pre)
                p.vector.tensor_scalar(xn_sc[:, :], xt[i],
                                       s_nm[:, i:i + 1], s_r[:, i:i + 1],
                                       ALU.add, ALU.mult).inc(vch)
                p.scalar.wait(vch)
                p.scalar.dma_start(out=cc["xn_in"][blk * 128:(blk + 1) * 128, :],
                                   in_=xn_sc[:, :]).inc(st_pre)
        p.gpsimd.wait(st_pre)
        p.gpsimd.collective_compute("AllGather", ALU.bypass,
                                    replica_groups=[list(range(8))],
                                    ins=[cc["st_in"].opt()],
                                    outs=[cc["st_out"].opt()]).inc(cc_pre)
        p.gpsimd.collective_compute("AllGather", ALU.bypass,
                                    replica_groups=[list(range(8))],
                                    ins=[cc["xn_in"].opt()],
                                    outs=[cc["xn_out"].opt()]).inc(cc_pre)

    p.barrier()

    with p.scope():
        xn_t = p.sb("t_xnt", (128, 4 * D), BF16)
        xn_f = p.sb("t_xnf", (128, 2 * ND * 512), BF16)
        xnd = p.sb("t_xnd", (128, 2 * 4 * MSL), F32)
        strow = p.sb("t_strow", (128, 2 * 8), F32)
        zf = p.sb("t_zf", (128, 2 * 1024), F32)
        hms = p.sb("t_hms", (128, 2 * 1024), BF16)
        hTms = p.sb("t_hTms", (128, 4 * MSL), BF16)
        hfT = p.sb("t_hfT", (128, 4 * D), BF16)
        hfF = p.sb("t_hfF", (128, 2 * ND * 128), BF16)
        gTM = p.sb("t_gTM", (128, 4 * MSL), F32)
        gTMb = p.sb("t_gTMb", (128, 4 * MSL), BF16)
        gFMb = p.sb("t_gFMb", (128, 2 * 512), BF16)
        gfT = p.sb("t_gfT", (128, 4 * D), BF16)
        gfF = p.sb("t_gfF", (128, 2 * 4 * 512), BF16)
        dsil = p.sb("t_dsil", (128, 2 * 512), F32)
        dzF = p.sb("t_dzF", (128, 2 * 512), F32)
        dzFb = p.sb("t_dzFb", (128, 2 * 512), BF16)
        dzT = p.sb("t_dzT", (128, 4 * MSL), BF16)
        tmpu = p.sb("t_tmpu", (128, MSL), F32)
        b0scr = p.sb("t_b0scr", (128, 4), F32)
        outsp = p.sb("t_outsp", (128, MSL), F32)
        rowsc = p.sb("t_rowsc", (1, 512), F32)

        ps_z = p.psum("t_psz", (128, 2 * 512))
        ps_q = p.psum("t_psq", (128, 2 * 512))
        ps_t = p.psum("t_pst", (128, 2048), BF16)
        ps_r = p.psum("t_psr", (1, 512))
        ps_b = p.psum("t_psb", (128, 512))

        tposed = [0]

        def load_chunk(t, slot):
            p.sync.wait(mm_f)
            p.sync.wait(mm_w)
            for b in range(4):
                r = 2 * b + (1 if t >= 16 else 0)
                lt = t % 16
                p.sync.dma_start(
                    out=xn_t[:, b * D:(b + 1) * D],
                    in_=cc["xn_out"][(r * 16 + lt) * 128:(r * 16 + lt + 1) * 128, :]
                ).inc(ld_xn)
                p.sync.dma_start(
                    out=xnd[:, (slot * 4 + b) * MSL:(slot * 4 + b + 1) * MSL],
                    in_=x_dsl[t * CH + b * 128:t * CH + (b + 1) * 128, :]).inc(ld_xn)
                p.sync.dma_start(
                    out=strow[:, slot * 8 + b * 2:slot * 8 + b * 2 + 2],
                    in_=cc["st_out"][(r * 16 + lt) * 128:(r * 16 + lt + 1) * 128, :]
                ).inc(ld_xn)
            p.vector.wait(ld_xn)
            for b in range(4):
                p.vector.tensor_scalar(
                    xnd[:, (slot * 4 + b) * MSL:(slot * 4 + b + 1) * MSL],
                    xnd[:, (slot * 4 + b) * MSL:(slot * 4 + b + 1) * MSL],
                    strow[:, slot * 8 + b * 2:slot * 8 + b * 2 + 1],
                    strow[:, slot * 8 + b * 2 + 1:slot * 8 + b * 2 + 2],
                    ALU.add, ALU.mult).inc(vch)

        def transpose_chunk(slot):
            def srcf(i):
                tt, dt = divmod(i, ND)
                return xn_t[:, tt * D + dt * 128:tt * D + (dt + 1) * 128]

            def dstf(i):
                tt, dt = divmod(i, ND)
                return xn_f[:, slot * ND * 512 + dt * 512 + tt * 128:
                            slot * ND * 512 + dt * 512 + (tt + 1) * 128]
            emit_transpose_tiles(p, srcf, dstf, 64, ident[:, :], ps_t, mm_t, ev_t,
                                 free_base=tposed[0])
            tposed[0] += 8

        def bcast11(src11, dst_col):
            p.tensor.wait(vch)
            p.tensor.matmul(ps_b[:, 0:1], ones_r[0:1, :], src11,
                            start=True, stop=True).inc(mm_b)
            p.vector.wait(mm_b)
            p.vector.tensor_copy(dst_col, ps_b[:, 0:1]).inc(vch)
            p.vector.wait(vch)

        ev_wp = p.sem("t_evwp")

        def wmaster_update(master, nmt, mm_fn):
            p.tensor.wait(ev_pp)
            for mt in range(nmt):
                pslot = mt % 2
                if ev_wp.n >= 1:
                    p.tensor.wait_abs(ev_wp, ev_wp.n - 1)
                mm_fn(mt, pslot)
                p.vector.wait(mm_w)
                p.vector.wait(ev_w)
                p.vector.tensor_scalar(tmpu[:, :], ps_q[:, pslot * 512:pslot * 512 + MSL],
                                       scal[:, 1:2], None, ALU.mult).inc(ev_wp)
                p.vector.wait(ev_wp)
                p.vector.scalar_tensor_tensor(master[:, mt * MSL:(mt + 1) * MSL],
                                              master[:, mt * MSL:(mt + 1) * MSL],
                                              scal[:, 0:1], tmpu[:, :],
                                              ALU.mult, ALU.add).inc(ev_w)

        p.sync.wait(cc_pre)
        load_chunk(0, 0)
        p.tensor.wait(ld_xn)
        p.tensor.wait(ld_c)
        transpose_chunk(0)

        p.vector.memset(hms[:, :], 0.0).inc(vch)
        p.tensor.wait(ld_w)
        p.tensor.wait(ld_wg)
        p.tensor.matmul(ps_b[:, 0:MSL], ones_r[0:1, :], b1r[0:1, :],
                        start=True, stop=True).inc(mm_b)
        p.vector.wait(mm_b)
        p.vector.tensor_copy(b1b[:, :], ps_b[:, 0:MSL]).inc(vch)
        p.vector.wait(ld_w)
        p.vector.tensor_copy(W0Tb[:, :], W0T[:, :]).inc(vch)
        p.vector.tensor_copy(W1RTb[:, :], W1RT[:, :]).inc(vch)

        for t in range(NCH + 1):
            cur = t % 2
            prv = 1 - cur
            has_left = t >= 1
            has_right = t < NCH
            nhalf = (1 if has_left else 0) + (1 if has_right else 0)
            loff = 0
            roff = 512 if has_left else 0

            p.tensor.wait(vch)
            groups = []
            if has_left:
                groups += [(0, prv, loff)]
            if has_right:
                groups += [(1, cur, roff)]
            gj = 0
            for mt in range(2):
                for half, slot, off in groups:
                    pslot = gj % 2
                    if ev_zb.n >= 1:
                        p.tensor.wait_abs(ev_zb, ev_zb.n - 1)
                    for k in range(ND):
                        p.tensor.matmul(
                            ps_z[:, pslot * 512:(pslot + 1) * 512],
                            W0Tb[:, k * MSL + mt * 128:k * MSL + (mt + 1) * 128],
                            xn_f[:, slot * ND * 512 + k * 512:slot * ND * 512 + (k + 1) * 512],
                            start=(k == 0), stop=(k == ND - 1)).inc(mm_f)
                    p.vector.wait(mm_f)
                    p.vector.wait(sch)
                    p.vector.tensor_scalar(
                        zf[:, mt * 1024 + off:mt * 1024 + off + 512],
                        ps_z[:, pslot * 512:(pslot + 1) * 512],
                        b0c[:, mt:mt + 1], None, ALU.add).inc(ev_zb)
                    p.scalar.wait(ev_zb)
                    p.scalar.wait(st_h)
                    p.scalar.activation(
                        hms[:, mt * 1024 + off:mt * 1024 + off + 512],
                        zf[:, mt * 1024 + off:mt * 1024 + off + 512],
                        AF.Silu).inc(ev_f)
                    gj += 1
            p.scalar.wait(ev_f)
            p.scalar.dma_start(
                out=cc["h_in"][:, :].rearrange("(m p) c -> p m c", p=128),
                in_=hms[:, :].rearrange("p (m c) -> p m c", c=1024)).inc(st_h)
            p.gpsimd.wait(st_h)
            p.gpsimd.collective_compute("AllGather", ALU.bypass,
                                        replica_groups=[list(range(8))],
                                        ins=[cc["h_in"].opt()],
                                        outs=[cc["h_out"].opt()]).inc(cc_h)
            if has_right:
                p.tensor.wait(ev_f)

                def hsrc(i):
                    mt, tt = divmod(i, 4)
                    return hms[:, mt * 1024 + roff + tt * 128:mt * 1024 + roff + (tt + 1) * 128]

                def hdst(i):
                    mt, tt = divmod(i, 4)
                    return hTms[:, tt * MSL + mt * 128:tt * MSL + (mt + 1) * 128]
                emit_transpose_tiles(p, hsrc, hdst, 8, ident[:, :], ps_t, mm_t, ev_t,
                                     free_base=tposed[0])
                tposed[0] += 1
            p.sync.wait(cc_h)
            if has_right:
                p.sync.wait(mm_w)
                for tt in range(4):
                    p.sync.dma_start(
                        out=hfT[:, tt * D:(tt + 1) * D],
                        in_=cc["h_out"].rearrange("r m c -> (r m) c")[
                            :, roff + tt * 128:roff + (tt + 1) * 128],
                        transpose=True).inc(ld_h)
            p.tensor.wait(ev_wp)
            hview = cc["h_out"].rearrange("r m c -> (r m) c")
            pt_i = 0
            for half, off, slot in ((0, loff, prv), (1, roff, cur)):
                if (half == 0 and not has_left) or (half == 1 and not has_right):
                    continue
                for tt in range(4):
                    pslot = pt_i % 2
                    fslot = pt_i % 2
                    if mm_p.n >= ND:
                        p.sync.wait_abs(mm_p, mm_p.n - ND)
                    p.sync.dma_start(
                        out=hfF[:, fslot * ND * 128:(fslot + 1) * ND * 128].rearrange(
                            "p (k c) -> p k c", c=128),
                        in_=hview.rearrange("(k p) c -> p k c", p=128)[
                            :, :, off + tt * 128:off + (tt + 1) * 128]).inc(ld_hf[fslot])
                    p.tensor.wait(ld_hf[fslot])
                    if ev_pp.n >= 1:
                        p.tensor.wait_abs(ev_pp, ev_pp.n - 1)
                    for k in range(ND):
                        p.tensor.matmul(
                            ps_q[:, pslot * 512:pslot * 512 + MSL],
                            hfF[:, (fslot * ND + k) * 128:(fslot * ND + k + 1) * 128],
                            W1RTb[:, k * MSL:(k + 1) * MSL],
                            start=(k == 0), stop=(k == ND - 1)).inc(mm_p)
                    p.vector.wait(mm_p)
                    if half == 0:
                        p.vector.wait(st_o)
                        p.vector.tensor_tensor(outsp[:, :],
                                               ps_q[:, pslot * 512:pslot * 512 + MSL],
                                               b1b[:, :], ALU.add).inc(ev_pp)
                        p.scalar.wait(ev_pp)
                        p.scalar.dma_start(
                            out=out_d[(t - 1) * CH + tt * 128:(t - 1) * CH + (tt + 1) * 128, :],
                            in_=outsp[:, :]).inc(st_o)
                    else:
                        p.vector.wait(mm_w)
                        p.vector.wait(mm_b)
                        p.vector.tensor_tensor(gTM[:, tt * MSL:(tt + 1) * MSL],
                                               ps_q[:, pslot * 512:pslot * 512 + MSL],
                                               b1b[:, :], ALU.add).inc(ev_pp)
                        p.vector.wait(vch)
                        p.vector.wait(ev_pp)
                        p.vector.tensor_tensor(gTM[:, tt * MSL:(tt + 1) * MSL],
                                               gTM[:, tt * MSL:(tt + 1) * MSL],
                                               xnd[:, (cur * 4 + tt) * MSL:(cur * 4 + tt + 1) * MSL],
                                               ALU.subtract).inc(ev_g)
                        p.vector.wait(ev_g)
                        p.vector.tensor_scalar(gTM[:, tt * MSL:(tt + 1) * MSL],
                                               gTM[:, tt * MSL:(tt + 1) * MSL],
                                               2.0 / (CH * D), None, ALU.mult).inc(ev_g)
                        p.vector.wait(ev_g)
                        p.vector.tensor_copy(gTMb[:, tt * MSL:(tt + 1) * MSL],
                                             gTM[:, tt * MSL:(tt + 1) * MSL]).inc(ev_g)
                    pt_i += 1
            if not has_right:
                break

            p.vector.wait(st_g)
            p.tensor.wait(ev_g)

            def gsrc(i):
                tt, dt = divmod(i, 2)
                return gTMb[:, tt * MSL + dt * 128:tt * MSL + (dt + 1) * 128]

            def gdst(i):
                tt, dt = divmod(i, 2)
                return gFMb[:, dt * 512 + tt * 128:dt * 512 + (tt + 1) * 128]
            emit_transpose_tiles(p, gsrc, gdst, 8, ident[:, :], ps_t, mm_t, ev_t,
                                 free_base=tposed[0])
            tposed[0] += 1
            p.scalar.wait(ev_t)
            p.scalar.dma_start(
                out=cc["g_in"][:, :].rearrange("(a p) c -> p a c", p=128),
                in_=gFMb[:, :].rearrange("p (a c) -> p a c", c=512)).inc(st_g)
            p.gpsimd.wait(st_g)
            p.gpsimd.collective_compute("AllGather", ALU.bypass,
                                        replica_groups=[list(range(8))],
                                        ins=[cc["g_in"].opt()],
                                        outs=[cc["g_out"].opt()]).inc(cc_g)

            p.tensor.wait(ev_zb)
            for k in range(ND):
                p.tensor.matmul(ps_r[0:1, :], lrw[:, k:k + 1],
                                xn_f[:, cur * ND * 512 + k * 512:cur * ND * 512 + (k + 1) * 512],
                                start=(k == 0), stop=(k == ND - 1)).inc(mm_b)
            p.scalar.wait(mm_b)
            p.scalar.activation(rowsc[0:1, :], ps_r[0:1, :], AF.Sigmoid,
                                bias=lrb[0:1, 0:1], scale=1.0).inc(sch)
            p.vector.wait(sch)
            vchain(p, vch, [
                lambda: p.vector.tensor_scalar(rowsc[0:1, :], rowsc[0:1, :], 1.0, 0.0,
                                               ALU.mult, ALU.add,
                                               accum_out=stat_t[0:1, 0:1]),
                lambda: p.vector.tensor_scalar(stat_t[0:1, 1:2], stat_t[0:1, 0:1],
                                               -LRM / CH, None, ALU.mult),
            ])
            p.tensor.wait(sch)
            p.tensor.wait(vch)
            for k in range(ND):
                p.tensor.matmul(ps_r[0:1, :], fgw[:, k:k + 1],
                                xn_f[:, cur * ND * 512 + k * 512:cur * ND * 512 + (k + 1) * 512],
                                start=(k == 0), stop=(k == ND - 1)).inc(mm_b)
            p.vector.wait(mm_b)
            vchain(p, vch, [
                lambda: p.vector.tensor_copy(rowsc[0:1, :], ps_r[0:1, :]),
                lambda: p.vector.tensor_reduce(stat_t[0:1, 4:5], rowsc[0:1, 0:128], AX.X, ALU.add),
                lambda: p.vector.tensor_reduce(stat_t[0:1, 5:6], rowsc[0:1, 128:256], AX.X, ALU.add),
                lambda: p.vector.tensor_reduce(stat_t[0:1, 6:7], rowsc[0:1, 256:384], AX.X, ALU.add),
                lambda: p.vector.tensor_reduce(stat_t[0:1, 7:8], rowsc[0:1, 384:512], AX.X, ALU.add),
                lambda: p.vector.tensor_scalar(stat_t[0:1, 4:8], stat_t[0:1, 4:8],
                                               1.0 / 128, None, ALU.mult),
            ])
            p.scalar.wait(vch)
            p.scalar.activation(stat_t[0:1, 4:8], stat_t[0:1, 4:8], AF.Sigmoid,
                                bias=lrb[0:1, 1:2], scale=1.0).inc(sch)
            p.vector.wait(sch)
            vchain(p, vch, [
                lambda: p.vector.tensor_reduce(stat_t[0:1, 3:4], stat_t[0:1, 4:8], AX.X, ALU.add),
                lambda: p.vector.tensor_scalar(stat_t[0:1, 3:4], stat_t[0:1, 3:4],
                                               0.25, None, ALU.mult),
            ])
            bcast11(stat_t[0:1, 3:4], scal[:, 0:1])
            bcast11(stat_t[0:1, 1:2], scal[:, 1:2])

            p.tensor.wait(ld_h)

            def mm_w1rt(mt, pslot):
                for tt in range(4):
                    p.tensor.matmul(ps_q[:, pslot * 512:pslot * 512 + MSL],
                                    hfT[:, tt * D + mt * 128:tt * D + (mt + 1) * 128],
                                    gTMb[:, tt * MSL:(tt + 1) * MSL],
                                    start=(tt == 0), stop=(tt == 3)).inc(mm_w)
            wmaster_update(W1RT, ND, mm_w1rt)
            p.vector.wait(ev_w)
            p.vector.wait(mm_p)
            p.vector.tensor_copy(W1RTb[:, :], W1RT[:, :]).inc(vch)

            p.sync.wait(cc_g)
            p.sync.wait(mm_w)
            gview = cc["g_out"]
            gv3 = gview.rearrange("(a p) c -> p a c", p=128)
            for tt in range(4):
                p.sync.dma_start(out=gfT[:, tt * D:(tt + 1) * D],
                                 in_=gview[:, tt * 128:(tt + 1) * 128],
                                 transpose=True).inc(ld_gt)

            p.tensor.wait(ev_zb)
            for grp in range(4):
                gslot = grp % 2
                if grp == 0:
                    p.sync.wait(mm_d)
                else:
                    p.sync.wait_abs(mm_d, mm_d.n - 4)
                p.sync.dma_start(
                    out=gfF[:, gslot * 4 * 512:(gslot + 1) * 4 * 512].rearrange(
                        "p (a c) -> p a c", c=512),
                    in_=gv3[:, grp * 4:(grp + 1) * 4, :]).inc(ld_gf[gslot])
                p.tensor.wait(ld_gf[gslot])
                for mt in range(2):
                    for kk in range(4):
                        k = grp * 4 + kk
                        p.tensor.matmul(ps_z[:, mt * 512:(mt + 1) * 512],
                                        W1Cb[:, k * MSL + mt * 128:k * MSL + (mt + 1) * 128],
                                        gfF[:, (gslot * 4 + kk) * 512:(gslot * 4 + kk + 1) * 512],
                                        start=(k == 0), stop=(k == ND - 1)).inc(mm_d)
            p.scalar.wait(ev_d)
            p.scalar.wait(ev_zb)
            for mt in range(2):
                p.scalar.activation(dsil[:, mt * 512:(mt + 1) * 512],
                                    zf[:, mt * 1024 + roff:mt * 1024 + roff + 512],
                                    AF.Sigmoid).inc(sch)
            p.vector.wait(sch)
            p.vector.wait(mm_w)
            for mt in range(2):
                p.vector.tensor_tensor(dzF[:, mt * 512:(mt + 1) * 512],
                                       zf[:, mt * 1024 + roff:mt * 1024 + roff + 512],
                                       dsil[:, mt * 512:(mt + 1) * 512],
                                       ALU.mult).inc(ev_d)
            p.vector.wait(ev_d)
            for mt in range(2):
                p.vector.tensor_scalar(dzFb[:, mt * 512:(mt + 1) * 512],
                                       dsil[:, mt * 512:(mt + 1) * 512],
                                       -1.0, 1.0, ALU.mult, ALU.add).inc(ev_d)
            p.vector.wait(ev_d)
            for mt in range(2):
                p.vector.tensor_tensor(dzF[:, mt * 512:(mt + 1) * 512],
                                       dzF[:, mt * 512:(mt + 1) * 512],
                                       dzFb[:, mt * 512:(mt + 1) * 512],
                                       ALU.mult).inc(ev_d)
            p.vector.wait(ev_d)
            for mt in range(2):
                p.vector.tensor_tensor(dsil[:, mt * 512:(mt + 1) * 512],
                                       dsil[:, mt * 512:(mt + 1) * 512],
                                       dzF[:, mt * 512:(mt + 1) * 512],
                                       ALU.add).inc(ev_d)
            p.vector.wait(mm_d)
            p.vector.wait(ev_d)
            for mt in range(2):
                p.vector.tensor_tensor(dzF[:, mt * 512:(mt + 1) * 512],
                                       ps_z[:, mt * 512:(mt + 1) * 512],
                                       dsil[:, mt * 512:(mt + 1) * 512],
                                       ALU.mult).inc(ev_d)
            p.vector.wait(ev_d)
            p.vector.tensor_copy(dzFb[:, :], dzF[:, :]).inc(ev_d)
            vchain(p, vch, [
                lambda: p.vector.tensor_reduce(b0scr[:, 0:1], dzF[:, 0:512], AX.X, ALU.add),
                lambda: p.vector.tensor_reduce(b0scr[:, 1:2], dzF[:, 512:1024], AX.X, ALU.add),
                lambda: p.vector.tensor_scalar(b0scr[:, 2:4], b0c[:, 0:2],
                                               scal[:, 0:1], None, ALU.mult),
                lambda: p.vector.scalar_tensor_tensor(b0c[:, 0:2], b0scr[:, 0:2],
                                                      scal[:, 1:2], b0scr[:, 2:4],
                                                      ALU.mult, ALU.add),
            ])
            p.tensor.wait(ev_d)

            def zsrc(i):
                mt, tt = divmod(i, 4)
                return dzFb[:, mt * 512 + tt * 128:mt * 512 + (tt + 1) * 128]

            def zdst(i):
                mt, tt = divmod(i, 4)
                return dzT[:, tt * MSL + mt * 128:tt * MSL + (mt + 1) * 128]
            emit_transpose_tiles(p, zsrc, zdst, 8, ident[:, :], ps_t, mm_t, ev_t,
                                 free_base=tposed[0])
            tposed[0] += 1

            p.tensor.wait(vch)
            p.tensor.wait(ev_g)
            for tt in range(4):
                p.tensor.matmul(ps_r[0:1, 0:MSL], ones_cb[:, 0:1],
                                gTMb[:, tt * MSL:(tt + 1) * MSL],
                                start=(tt == 0), stop=(tt == 3)).inc(mm_b)
            p.vector.wait(mm_b)
            vchain(p, vch, [
                lambda: p.vector.tensor_scalar(rowsc[0:1, 0:MSL], b1r[0:1, :],
                                               stat_t[0:1, 3:4], None, ALU.mult),
                lambda: p.vector.scalar_tensor_tensor(b1r[0:1, :], ps_r[0:1, 0:MSL],
                                                      stat_t[0:1, 1:2],
                                                      rowsc[0:1, 0:MSL],
                                                      ALU.mult, ALU.add),
            ])
            p.tensor.wait(vch)
            p.tensor.matmul(ps_b[:, 0:MSL], ones_r[0:1, :], b1r[0:1, :],
                            start=True, stop=True).inc(mm_b)
            p.vector.wait(mm_b)
            p.vector.wait(ev_pp)
            p.vector.tensor_copy(b1b[:, :], ps_b[:, 0:MSL]).inc(vch)

            p.tensor.wait(ev_t)

            def mm_w0t(dt, pslot):
                for tt in range(4):
                    p.tensor.matmul(ps_q[:, pslot * 512:pslot * 512 + MSL],
                                    xn_t[:, tt * D + dt * 128:tt * D + (dt + 1) * 128],
                                    dzT[:, tt * MSL:(tt + 1) * MSL],
                                    start=(tt == 0), stop=(tt == 3)).inc(mm_w)
            wmaster_update(W0T, ND, mm_w0t)
            p.vector.wait(ev_w)
            p.vector.wait(mm_f)
            p.vector.tensor_copy(W0Tb[:, :], W0T[:, :]).inc(vch)

            if t + 1 < NCH:
                load_chunk(t + 1, prv)
                p.tensor.wait(ld_xn)
                transpose_chunk(prv)

            p.tensor.wait(ld_gt)

            def mm_w1c(dt, pslot):
                for tt in range(4):
                    p.tensor.matmul(ps_q[:, pslot * 512:pslot * 512 + MSL],
                                    gfT[:, tt * D + dt * 128:tt * D + (dt + 1) * 128],
                                    hTms[:, tt * MSL:(tt + 1) * MSL],
                                    start=(tt == 0), stop=(tt == 3)).inc(mm_w)
            p.tensor.wait(mm_d)
            wmaster_update(W1Cb, ND, mm_w1c)

        p.gpsimd.wait(st_o)
        for g4 in range(4):
            p.gpsimd.collective_compute(
                "AllGather", ALU.bypass,
                replica_groups=[list(range(8))],
                ins=[out_d[g4 * 8 * CH:(g4 + 1) * 8 * CH, :].opt()],
                outs=[cc["o_out"][g4:g4 + 1].opt()]).inc(cc_o)

    p.barrier()
    with p.scope():
        x1a = p.sb("t_x1a", (128, 2 * D), F32)
        oget = p.sb("t_oget", (128, 2 * D), F32)
        oidx_s = p.sb("t_oidx", (128, NT * NCORES), mybir.dt.int32)
        g_o = p.sem("t_go")
        ld_i2 = p.sem("t_ldi2")
        p.sync.dma_start(out=oidx_s[:, :], in_=oidx[:, :]).inc(ld_i2)
        p.gpsimd.wait(ld_i2)
        p.gpsimd.wait(cc_o)
        oview = cc["o_out"].rearrange("a r t c -> (a r t) c")
        x1v = p.sem("t_x1v")
        ld_xa = [p.sem("t_ldxa0"), p.sem("t_ldxa1")]
        g_o2 = [p.sem("t_go0", "sw"), p.sem("t_go1", "sw")]
        st_y2 = [p.sem("t_sty0"), p.sem("t_sty1")]
        for L in range(NT):
            slot = L % 2
            if L >= 2:
                p.sync.wait(x1v, L - 1)
                p.sync.wait(st_y2[slot])
                p.gpsimd.wait(x1v, L - 1)
                p.gpsimd.wait(st_y2[slot])
            p.sync.dma_start(out=x1a[:, slot * D:(slot + 1) * D],
                             in_=x_own[L * 128:(L + 1) * 128, :]).inc(ld_xa[slot])
            for r in range(NCORES):
                p.gpsimd.indirect_dma_start(
                    oget[:, slot * D + r * MSL:slot * D + (r + 1) * MSL], None,
                    oview,
                    bass.IndirectOffsetOnAxis(oidx_s[:, L * NCORES + r:L * NCORES + r + 1], 0)
                ).inc(g_o2[slot])
            p.vector.wait(ld_xa[slot])
            p.vector.wait(g_o2[slot])
            p.vector.tensor_tensor(x1a[:, slot * D:(slot + 1) * D],
                                   x1a[:, slot * D:(slot + 1) * D],
                                   oget[:, slot * D:(slot + 1) * D],
                                   ALU.add).inc(x1v)
            p.scalar.wait(x1v)
            p.scalar.dma_start(out=y[L * 128:(L + 1) * 128, :],
                               in_=x1a[:, slot * D:(slot + 1) * D]).inc(st_y2[slot])
        p.scalar.wait(st_y2[0])
        p.scalar.wait(st_y2[1])
def build_l0_attn():
    nc = bass.Bass("TRN2", target_bir_lowering=False, debug=False)
    x_own = nc.dram_tensor("x_own", [TOK, D], F32, kind="ExternalInput").ap()
    x_dsl = nc.dram_tensor("x_dsl", [NCH * CH, MSL], F32, kind="ExternalInput").ap()
    W0T_s = nc.dram_tensor("W0T_s", [D, MSL], F32, kind="ExternalInput").ap()
    W1RT_s = nc.dram_tensor("W1RT_s", [D, MSL], F32, kind="ExternalInput").ap()
    W1C_s = nc.dram_tensor("W1C_s", [D, MSL], F32, kind="ExternalInput").ap()
    b0_2 = nc.dram_tensor("b0_2", [128, 2], F32, kind="ExternalInput").ap()
    b1row = nc.dram_tensor("b1row", [1, MSL], F32, kind="ExternalInput").ap()
    lr_wT = nc.dram_tensor("lr_wT", [D, 1], F32, kind="ExternalInput").ap()
    fg_wT = nc.dram_tensor("fg_wT", [D, 1], F32, kind="ExternalInput").ap()
    lrfgb = nc.dram_tensor("lrfgb", [1, 2], F32, kind="ExternalInput").ap()
    oidx = nc.dram_tensor("oidx", [128, NT * NCORES], mybir.dt.int32,
                          kind="ExternalInput").ap()
    ident_i = nc.dram_tensor("ident", [128, 128], BF16, kind="ExternalInput").ap()
    ones_colb_i = nc.dram_tensor("ones_colb", [128, 1], BF16, kind="ExternalInput").ap()
    ones_row_i = nc.dram_tensor("ones_row", [1, 128], F32, kind="ExternalInput").ap()
    y = nc.dram_tensor("y", [TOK, D], F32, kind="ExternalOutput").ap()
    cc = {
        "st_in": nc.dram_tensor("cc_st_in", [TOK, 2], F32).ap(),
        "st_out": nc.dram_tensor("cc_st_out", [NCORES * TOK, 2], F32).ap(),
        "xn_in": nc.dram_tensor("cc_xn_in", [TOK, D], BF16).ap(),
        "xn_out": nc.dram_tensor("cc_xn_out", [NCORES * TOK, D], BF16).ap(),
        "h_in": nc.dram_tensor("cc_h_in", [2 * 128, 1024], BF16).ap(),
        "h_out": nc.dram_tensor("cc_h_out", [NCORES, 2 * 128, 1024], BF16).ap(),
        "g_in": nc.dram_tensor("cc_g_in", [2 * 128, 512], BF16).ap(),
        "g_out": nc.dram_tensor("cc_g_out", [NCORES * 2 * 128, 512], BF16).ap(),
        "o_out": nc.dram_tensor("cc_o_out", [4, NCORES, 8 * CH, MSL], F32).ap(),
    }
    out_d = nc.dram_tensor("out_d", [NCH * CH, MSL], F32).ap()
    with ExitStack() as ctx:
        p = Prog(nc, ctx)
        emit_l0_attn(p, x_own, x_dsl, W0T_s, W1RT_s, W1C_s, b0_2, b1row, lr_wT,
                     fg_wT, lrfgb, oidx, ident_i, ones_colb_i, ones_row_i, y, cc, out_d)
        p.finalize()
    return nc


def mk_xdsl(x_full, s):
    xr = x_full.reshape(B, NCH, 128, D).transpose(1, 0, 2, 3).reshape(NCH * CH, D)
    return np.ascontiguousarray(xr[:, s * MSL:(s + 1) * MSL])


def mk_oidx(b, half):
    idx = np.empty((128, NT * NCORES), dtype=np.int32)
    for L in range(NT):
        c = 16 * half + L
        for r in range(NCORES):
            base = ((c // 8) * NCORES + r) * (8 * CH) + (c % 8) * CH + b * 128
            idx[:, L * NCORES + r] = base + np.arange(128)
    return idx


def run_l0_attn(x_full, W0, b0, W1, b1, lr_w, lr_b, fg_w, fg_b):
    c = consts()
    ca = attn_consts()
    nc = build_l0_attn()
    W0T = np.ascontiguousarray(W0.T)
    W1T = np.ascontiguousarray(W1.T)
    shards = shard_rows(x_full)
    in_maps = []
    for s in range(NCORES):
        b, half = divmod(s, 2)
        msl = slice(s * MSL, (s + 1) * MSL)
        in_maps.append(dict(
            x_own=shards[s],
            x_dsl=mk_xdsl(x_full, s),
            W0T_s=np.ascontiguousarray(W0T[:, msl]),
            W1RT_s=np.ascontiguousarray(W1T[:, msl]),
            W1C_s=np.ascontiguousarray(W1[:, msl]),
            b0_2=np.ascontiguousarray(b0[msl].reshape(2, 128).T),
            b1row=np.ascontiguousarray(b1[msl].reshape(1, MSL)),
            lr_wT=np.ascontiguousarray(lr_w.reshape(D, 1)),
            fg_wT=np.ascontiguousarray(fg_w.reshape(D, 1)),
            lrfgb=np.array([[float(np.asarray(lr_b).reshape(-1)[0]),
                             float(np.asarray(fg_b).reshape(-1)[0])]], dtype=np.float32),
            oidx=mk_oidx(b, half),
            ident=c["ident"], ones_row=c["ones_row"],
            ones_colb=ca["ones_colb"]))
    res = run_bass_kernel_spmd(nc, in_maps, core_ids=list(range(NCORES)))
    return unshard_rows([res.results[s]["y"] for s in range(NCORES)])


def mk_xnfidx(b):
    idx = np.empty((128, 32), dtype=np.int32)
    for g in range(32):
        r = 2 * b + (1 if g >= 16 else 0)
        lt = g % 16
        idx[:, g] = (r * 16 + lt) * 128 + np.arange(128)
    return idx


def build_fused():
    nc = bass.Bass("TRN2", target_bir_lowering=False, debug=False)
    EI = dict(kind="ExternalInput")
    ap = lambda name, shape, dt=F32, **kw: nc.dram_tensor(name, list(shape), dt, **kw).ap()
    ident_i = ap("ident", (128, 128), BF16, **EI)
    ones_colb_i = ap("ones_colb", (128, 1), BF16, **EI)
    ones_col_i = ap("ones_col", (128, 1), F32, **EI)
    ones_row_i = ap("ones_row", (1, 128), F32, **EI)
    x_own = ap("x_own", (TOK, D), **EI)
    x_dsl = ap("x_dsl", (NCH * CH, MSL), **EI)
    W0T_s = ap("W0T_s", (D, MSL), **EI)
    W1RT_s = ap("W1RT_s", (D, MSL), **EI)
    W1C_s = ap("W1C_s", (D, MSL), **EI)
    b0_2 = ap("b0_2", (128, 2), **EI)
    b1row = ap("b1row", (1, MSL), **EI)
    lr_wT = ap("lr_wT", (D, 1), **EI)
    fg_wT = ap("fg_wT", (D, 1), **EI)
    lrfgb = ap("lrfgb", (1, 2), **EI)
    oidx = ap("oidx", (128, NT * NCORES), mybir.dt.int32, **EI)
    gate_wT = ap("gate_wT", (D, HG), **EI)
    up_wT = ap("up_wT", (D, HG), **EI)
    down_wT = ap("down_wT", (HG, D), **EI)
    gate_s2 = ap("gate_s2", (128, NF_G), **EI)
    up_s2 = ap("up_s2", (128, NF_G), **EI)
    down_s_row = ap("down_s_row", (1, D), **EI)
    WqTl = ap("WqTl", (D, NHL * 128), **EI)
    WkTl = ap("WkTl", (D, NHL * 128), **EI)
    WvTl = ap("WvTl", (D, NHL * 128), **EI)
    WoT = ap("WoT", (D, D), **EI)
    bq2 = ap("bq2", (128, NHL), **EI)
    bk2 = ap("bk2", (128, NHL), **EI)
    bv_row = ap("bv_row", (1, NHL * 128), **EI)
    bo_row = ap("bo_row", (1, D), **EI)
    slopes_row = ap("slopes_row", (1, NHL), **EI)
    kpos = ap("kpos", (128, S // 128), **EI)
    qpos_row = ap("qpos_row", (1, S), **EI)
    ctxidx = ap("ctxidx", (128, NH), mybir.dt.int32, **EI)
    xnfidx = ap("xnfidx", (128, 32), mybir.dt.int32, **EI)
    masktri_i = ap("masktri", (128, 128), BF16, **EI)
    w_fcT = ap("w_fcT", (D, HFC), **EI)
    w_projT = ap("w_projT", (HFC, D), **EI)
    y_out = ap("y", (TOK, D), F32, kind="ExternalOutput")
    x1_d = ap("x1_d", (TOK, D))
    x2_d = ap("x2_d", (TOK, D))
    x3_d = ap("x3_d", (TOK, D))
    cc = {
        "st_in": ap("cc_st_in", (TOK, 2)),
        "st_out": ap("cc_st_out", (NCORES * TOK, 2)),
        "xn_in": ap("cc_xn_in", (TOK, D), BF16),
        "xn_out": ap("cc_xn_out", (NCORES * TOK, D), BF16),
        "h_in": ap("cc_h_in", (2 * 128, 1024), BF16),
        "h_out": ap("cc_h_out", (NCORES, 2 * 128, 1024), BF16),
        "g_in": ap("cc_g_in", (2 * 128, 512), BF16),
        "g_out": ap("cc_g_out", (NCORES * 2 * 128, 512), BF16),
        "o_out": ap("cc_o_out", (4, NCORES, 8 * CH, MSL)),
    }
    out_d = ap("out_d", (NCH * CH, MSL))
    twfc_d = nc.dram_tensor("twfc_d", [NF_FC, ND, 128, 128], BF16).ap()
    twpj_d = nc.dram_tensor("twpj_d", [4, NF_FC, 128, 512], BF16).ap()
    xnfm_d = ap("xnfm_d", (D, S), BF16)
    ctx_d = nc.dram_tensor("ctx_d", [NHL, 128, S], BF16)
    cc_att = nc.dram_tensor("cc_att_out", [4, NCORES, 2, 128, S], BF16)
    ctxmy_d = ap("ctxmy_d", (NH, 128, TOK), BF16)
    cc_xnf_in = ap("cc_xnf_in", (TOK, D), BF16)
    cc_xnf_out = ap("cc_xnf_out", (NCORES * TOK, D), BF16)

    with ExitStack() as ctx:
        p = Prog(nc, ctx)
        with p.scope():
            emit_l0_attn(p, x_own, x_dsl, W0T_s, W1RT_s, W1C_s, b0_2, b1row,
                         lr_wT, fg_wT, lrfgb, oidx, ident_i, ones_colb_i,
                         ones_row_i, x1_d, cc, out_d)
        p.barrier()
        with p.scope():
            emit_l0_ffn(p, x1_d, gate_wT, up_wT, down_wT, gate_s2, up_s2,
                        down_s_row, ident_i, ones_col_i, ones_row_i, x2_d)
        p.barrier()
        with p.scope():
            emit_l1_attn(p, None, x2_d, WqTl, WkTl, WvTl, WoT, bq2, bk2,
                         bv_row, bo_row, slopes_row, kpos, qpos_row, ctxidx,
                         masktri_i, ident_i, ones_colb_i, ones_row_i, x3_d,
                         xnfm_d, ctx_d, cc_att, ctxmy_d,
                         fused=True, cc_xnf_in=cc_xnf_in, cc_xnf_out=cc_xnf_out,
                         xnfidx=xnfidx)
        p.barrier()
        with p.scope():
            emit_l1_ffn(p, x3_d, w_fcT, w_projT, twfc_d, twpj_d, ident_i,
                        ones_col_i, ones_row_i, y_out)
        p.finalize()
    return nc


_FUSED_NC = None


def kernel(L0_attn_x, L0_attn_W0, L0_attn_b0, L0_attn_W1, L0_attn_b1,
           L0_attn_lr_w, L0_attn_lr_b, L0_attn_fg_w, L0_attn_fg_b,
           L0_ffn_gate_w, L0_ffn_up_w, L0_ffn_down_w, L0_ffn_gate_s,
           L0_ffn_up_s, L0_ffn_down_s, L1_attn_Wq, L1_attn_bq, L1_attn_Wk,
           L1_attn_bk, L1_attn_Wv, L1_attn_bv, L1_attn_Wo, L1_attn_bo,
           L1_attn_alibi_slopes, L1_ffn_w_fc, L1_ffn_w_proj):
    global _FUSED_NC
    a = lambda v: np.ascontiguousarray(np.asarray(v, dtype=np.float32))
    x = a(L0_attn_x)
    W0, b0 = a(L0_attn_W0), a(L0_attn_b0)
    W1, b1 = a(L0_attn_W1), a(L0_attn_b1)
    lr_w, fg_w = a(L0_attn_lr_w), a(L0_attn_fg_w)
    Wq, Wk, Wv, Wo = a(L1_attn_Wq), a(L1_attn_Wk), a(L1_attn_Wv), a(L1_attn_Wo)
    bq, bk, bv, bo = a(L1_attn_bq), a(L1_attn_bk), a(L1_attn_bv), a(L1_attn_bo)
    slopes = a(L1_attn_alibi_slopes)
    c = consts()
    ca = attn_consts()
    if _FUSED_NC is None:
        _FUSED_NC = build_fused()
    nc = _FUSED_NC
    W0T = np.ascontiguousarray(W0.T)
    W1T = np.ascontiguousarray(W1.T)
    shards = shard_rows(x)
    shared = dict(
        ident=c["ident"], ones_row=c["ones_row"], ones_colb=ca["ones_colb"],
        ones_col=c["ones_col"],
        masktri=ca["masktri"], kpos=ca["kpos"], qpos_row=ca["qpos_row"],
        gate_wT=np.ascontiguousarray(a(L0_ffn_gate_w).T),
        up_wT=np.ascontiguousarray(a(L0_ffn_up_w).T),
        down_wT=np.ascontiguousarray(a(L0_ffn_down_w).T),
        gate_s2=np.ascontiguousarray(a(L0_ffn_gate_s).reshape(NF_G, 128).T),
        up_s2=np.ascontiguousarray(a(L0_ffn_up_s).reshape(NF_G, 128).T),
        down_s_row=np.ascontiguousarray(a(L0_ffn_down_s).reshape(1, D)),
        WoT=np.ascontiguousarray(Wo.T),
        bo_row=np.ascontiguousarray(bo.reshape(1, D)),
        w_fcT=np.ascontiguousarray(a(L1_ffn_w_fc).T),
        w_projT=np.ascontiguousarray(a(L1_ffn_w_proj).T),
        lr_wT=np.ascontiguousarray(lr_w.reshape(D, 1)),
        fg_wT=np.ascontiguousarray(fg_w.reshape(D, 1)),
        lrfgb=np.array([[float(np.asarray(L0_attn_lr_b).reshape(-1)[0]),
                         float(np.asarray(L0_attn_fg_b).reshape(-1)[0])]],
                       dtype=np.float32),
    )
    in_maps = []
    for s in range(NCORES):
        b, half = divmod(s, 2)
        msl = slice(s * MSL, (s + 1) * MSL)
        hs = slice(half * NHL * 128, (half + 1) * NHL * 128)
        hh = slice(half * NHL, (half + 1) * NHL)
        in_maps.append(dict(
            x_own=shards[s], x_dsl=mk_xdsl(x, s),
            W0T_s=np.ascontiguousarray(W0T[:, msl]),
            W1RT_s=np.ascontiguousarray(W1T[:, msl]),
            W1C_s=np.ascontiguousarray(W1[:, msl]),
            b0_2=np.ascontiguousarray(b0[msl].reshape(2, 128).T),
            b1row=np.ascontiguousarray(b1[msl].reshape(1, MSL)),
            oidx=mk_oidx(b, half),
            WqTl=np.ascontiguousarray(Wq.T[:, hs]),
            WkTl=np.ascontiguousarray(Wk.T[:, hs]),
            WvTl=np.ascontiguousarray(Wv.T[:, hs]),
            bq2=np.ascontiguousarray(bq[hs].reshape(NHL, 128).T),
            bk2=np.ascontiguousarray(bk[hs].reshape(NHL, 128).T),
            bv_row=np.ascontiguousarray(bv[hs].reshape(1, NHL * 128)),
            slopes_row=np.ascontiguousarray(slopes[hh].reshape(1, NHL)),
            ctxidx=mk_ctxidx(b, half),
            xnfidx=mk_xnfidx(b),
            **shared))
    res = run_bass_kernel_spmd(nc, in_maps, core_ids=list(range(NCORES)))
    return unshard_rows([res.results[s]["y"] for s in range(NCORES)])


# version 07
# speedup vs baseline: 1.0470x; 1.0470x over previous
import numpy as np
import ml_dtypes
from contextlib import ExitStack, contextmanager

import concourse.bass as bass
import concourse.mybir as mybir
from concourse.bass_utils import run_bass_kernel_spmd

F32 = mybir.dt.float32
BF16 = mybir.dt.bfloat16
AF = mybir.ActivationFunctionType
ALU = mybir.AluOpType
AX = mybir.AxisListType
BF = ml_dtypes.bfloat16

B, S, D = 4, 4096, 2048
NCORES = 8
TOK = 2048
NT = TOK // 128
ND = D // 128

ENGS = ("sync", "vector", "scalar", "tensor", "gpsimd")


class Sem:
    __slots__ = ("name", "h", "n", "mark", "kind", "retired", "base")

    def __init__(self, name, h, kind):
        self.name = name
        self.h = h
        self.n = 0
        self.mark = 0
        self.base = 0
        self.kind = kind
        self.retired = False


class _Rec:
    __slots__ = ("op", "args", "kw", "incs", "eng")

    def __init__(self, op, args, kw, eng):
        self.op = op
        self.args = args
        self.kw = kw
        self.eng = eng
        self.incs = []

    def inc(self, sem, n=None):
        assert not sem.retired, f"inc on retired sem {sem.name}"
        if "dma" in self.op:
            cls = "sw" if self.eng == "gpsimd" else "hw"
        elif "collective" in self.op:
            cls = "cc"
        else:
            cls = "c"
        need = {"sw": "sw", "cc": "cc", "hw": "gen", "c": None}[cls]
        assert need is None or need == sem.kind, \
            f"sem {sem.name} kind {sem.kind} inc'd by {cls} ({self.eng}.{self.op})"
        if n is None:
            n = 16 if "dma" in self.op else 1
        sem.n += n
        self.incs.append((sem, n))
        return self


class _Eng:
    def __init__(self, prog, name):
        self._p = prog
        self._name = name

    def __getattr__(self, op):
        def emit(*args, **kw):
            r = _Rec(op, args, kw, self._name)
            self._p.q[self._name].append(r)
            return r
        return emit

    def wait(self, sem, val=None):
        v = sem.n if val is None else sem.base + val
        self._p.q[self._name].append(("wait", sem, v))
        return v

    def wait_abs(self, sem, val):
        self._p.q[self._name].append(("wait", sem, val))
        return val


class Prog:

    def __init__(self, nc, ctx):
        self.nc = nc
        self.ctx = ctx
        self.q = {e: [] for e in ENGS}
        for e in ENGS:
            setattr(self, e, _Eng(self, e))
        self._semnum = 0
        self._scopes = [ctx]
        self._sems = []
        self._pool = {"gen": [], "sw": [], "cc": []}
        self._pending = []
        self._scope_sems = []

    def sem(self, name=None, kind="gen"):
        self._semnum += 1
        name = name or f"s{self._semnum}"
        pool = self._pool[kind]
        if pool:
            s = pool.pop()
            s.name = name
            s.retired = False
            s.base = s.n
        else:
            h = self.ctx.enter_context(self.nc.semaphore(f"{name}_{self._semnum}"))
            s = Sem(name, h, kind)
            self._sems.append(s)
        if self._scope_sems:
            self._scope_sems[-1].append(s)
        return s

    def barrier(self):
        sems = [s for s in self._sems if s.n > s.mark]
        for e in ENGS:
            for s in sems:
                self.q[e].append(("wait", s, s.n))
        for s in sems:
            s.mark = s.n
        for s in self._pending:
            self._pool[s.kind].append(s)
        self._pending = []

    def sb(self, name, shape, dt):
        return self._scopes[-1].enter_context(
            self.nc.sbuf_tensor(name, list(shape), dt))

    def psum(self, name, shape, dt=F32):
        return self._scopes[-1].enter_context(
            self.nc.psum_tensor(name, list(shape), dt))

    @contextmanager
    def scope(self):
        es = ExitStack()
        self._scopes.append(es)
        self._scope_sems.append([])
        try:
            yield
        finally:
            self._scopes.pop()
            es.close()
            for s in self._scope_sems.pop():
                s.retired = True
                self._pending.append(s)

    def finalize(self):
        blk = self.ctx.enter_context(self.nc.Block())
        for ename in ENGS:
            items = self.q[ename]
            if not items:
                continue

            def make(items):
                def body(eng):
                    for it in items:
                        if isinstance(it, tuple):
                            _, sem, v = it
                            eng.wait_ge(sem.h, v)
                        else:
                            instr = getattr(eng, it.op)(*it.args, **it.kw)
                            for sem, n in it.incs:
                                instr.then_inc(sem.h, n)
                return body
            getattr(blk, ename)(make(items))


def vchain(p, sem, fns):
    for fn in fns:
        fn().inc(sem)
        p.vector.wait(sem)


def emit_ln_stats(p, x_tiles, sq_scr, stats, n0, vch, sch, width=D, eps=1e-5):
    n = len(x_tiles)
    s_sum = stats[:, n0 + 0 * n:n0 + 1 * n]
    s_sq = stats[:, n0 + 1 * n:n0 + 2 * n]
    s_nm = stats[:, n0 + 2 * n:n0 + 3 * n]
    s_r = stats[:, n0 + 3 * n:n0 + 4 * n]
    for i, xt in enumerate(x_tiles):
        p.scalar.activation(sq_scr[:, 0:width], xt, AF.Square,
                            accum_out=s_sq[:, i:i + 1]).inc(sch)
        p.scalar.wait(sch)
        p.vector.tensor_reduce(s_sum[:, i:i + 1], xt, AX.X, ALU.add).inc(vch)
        p.vector.wait(vch)
    p.vector.wait(sch)
    p.vector.wait(vch)
    vchain(p, vch, [
        lambda: p.vector.tensor_scalar(s_nm, s_sum, 1.0 / width, None, ALU.mult),
        lambda: p.vector.tensor_tensor(s_r, s_nm, s_nm, ALU.mult),
        lambda: p.vector.tensor_scalar(s_r, s_r, -1.0, eps, ALU.mult, ALU.add),
        lambda: p.vector.scalar_tensor_tensor(s_r, s_sq, 1.0 / width, s_r,
                                              ALU.mult, ALU.add),
    ])
    p.scalar.wait(vch)
    p.scalar.activation(s_r, s_r, AF.Sqrt).inc(sch)
    p.vector.wait(sch)
    vchain(p, vch, [
        lambda: p.vector.reciprocal(s_r, s_r),
        lambda: p.vector.tensor_scalar(s_nm, s_nm, -1.0, None, ALU.mult),
    ])
    return s_nm, s_r


def emit_transpose_tiles(p, src_fn, dst_fn, ntiles, ident, ps_t, mm_t, ev_t,
                         free_sem=None, free_base=0, group=8):
    nb = 2
    for i in range(ntiles):
        b = (i // group) % nb
        slot = i % group
        if i == 0:
            p.tensor.wait_abs(ev_t, ev_t.n)
        if i >= group * nb and slot == 0:
            p.tensor.wait(ev_t, free_base + (i // group - nb + 1))
        p.tensor.transpose(ps_t[:, (b * group + slot) * 128:(b * group + slot + 1) * 128],
                           src_fn(i), ident).inc(mm_t)
        last_in_batch = (slot == group - 1) or (i == ntiles - 1)
        if last_in_batch:
            p.vector.wait(mm_t)
            g0 = (i // group) * group
            for j in range(g0, i + 1):
                bb = (j // group) % nb
                ss = j % group
                p.vector.tensor_copy(dst_fn(j),
                                     ps_t[:, (bb * group + ss) * 128:(bb * group + ss + 1) * 128])
            p.q["vector"][-1].inc(ev_t)


def bcast_col(p, src11, dst_col, ones_row, ps_misc, mm_sem, vch):
    p.tensor.matmul(ps_misc[:, 0:1], ones_row, src11, start=True, stop=True).inc(mm_sem)
    p.vector.wait(mm_sem)
    p.vector.tensor_copy(dst_col, ps_misc[:, 0:1]).inc(vch)
    p.vector.wait(vch)


HFC = 5120
NF_FC = HFC // 128


def emit_l1_ffn(p, x, w_fcT, w_projT, twfc_d, twpj_d, ident_i, ones_col_i,
                ones_row_i, y):
    ident = p.sb("ff_ident", (128, 128), BF16)
    ones_c = p.sb("ff_onesc", (128, 1), F32)
    ones_r = p.sb("ff_onesr", (1, 128), F32)
    scal = p.sb("ff_scal", (128, 8), F32)
    ps_misc = p.psum("ff_psmisc", (128, 16))

    ld_c = p.sem("ld_c")
    vch = p.sem("vch")
    sch = p.sem("sch")
    mm_s = p.sem("mm_s")
    tern_done = p.sem("tern_done")

    p.sync.dma_start(out=ident[:, :], in_=ident_i[:, :]).inc(ld_c)
    p.sync.dma_start(out=ones_c[:, :], in_=ones_col_i[:, :]).inc(ld_c)
    p.sync.dma_start(out=ones_r[:, :], in_=ones_row_i[:, :]).inc(ld_c)

    with p.scope():
        CW = 2560
        wbuf = p.sb("ff_wbuf", (128, 2 * CW), F32)
        t1b = p.sb("ff_t1", (128, CW), F32)
        t2b = p.sb("ff_t2", (128, CW), F32)
        obuf = p.sb("ff_obuf", (128, 2 * CW), BF16)
        tstat = p.sb("ff_tstat", (128, 120), F32)

        ld_w = [p.sem("ld_w0"), p.sem("ld_w1")]
        ab_v = p.sem("ab_v")
        tern_v = p.sem("tern_v")
        tern_st = [p.sem("tern_st0"), p.sem("tern_st1")]

        chunks = []
        for i in range(ND):
            for sbi in range(2):
                chunks.append(("fc", i, sbi,
                               w_fcT[i * 128:(i + 1) * 128, sbi * CW:(sbi + 1) * CW]))
        for i in range(NF_FC):
            chunks.append(("pj", i, 0, w_projT[i * 128:(i + 1) * 128, 0:D]))
        nfc = 2 * ND

        for ci, (kind, i, sbi, src) in enumerate(chunks):
            w = CW if kind == "fc" else D
            slot = ci % 2
            buf = wbuf[:, slot * CW:slot * CW + w]
            if ci >= 2:
                p.sync.wait(ab_v, ci - 1)
            p.sync.dma_start(out=buf, in_=src).inc(ld_w[slot])
            p.scalar.wait(ld_w[slot])
            p.scalar.activation(obuf[:, slot * CW:slot * CW + w], buf, AF.Abs,
                                accum_out=tstat[:, ci:ci + 1]).inc(ab_v)
        p.vector.wait(ab_v)
        vchain(p, vch, [
            lambda: p.vector.tensor_reduce(tstat[:, 112:113], tstat[:, 0:nfc], AX.X, ALU.add),
            lambda: p.vector.tensor_reduce(tstat[:, 113:114], tstat[:, nfc:nfc + NF_FC], AX.X, ALU.add),
        ])
        p.tensor.wait(ld_c)
        p.tensor.wait(vch)
        p.tensor.matmul(ps_misc[0:1, 8:10], ones_c[:, 0:1], tstat[:, 112:114],
                        start=True, stop=True).inc(mm_s)
        p.vector.wait(mm_s)
        vchain(p, vch, [
            lambda: p.vector.tensor_scalar(tstat[0:1, 116:117], ps_misc[0:1, 8:9],
                                           1.0 / (D * HFC), 1e-5, ALU.mult, ALU.max),
            lambda: p.vector.tensor_scalar(tstat[0:1, 117:118], ps_misc[0:1, 9:10],
                                           1.0 / (D * HFC), 1e-5, ALU.mult, ALU.max),
            lambda: p.vector.tensor_scalar(tstat[0:1, 118:119], tstat[0:1, 116:117],
                                           0.7, None, ALU.mult),
            lambda: p.vector.tensor_scalar(tstat[0:1, 119:120], tstat[0:1, 117:118],
                                           0.7, None, ALU.mult),
        ])
        p.tensor.wait(vch)
        p.tensor.matmul(ps_misc[:, 0:1], ones_r[0:1, :], tstat[0:1, 116:117], start=True, stop=True)
        p.tensor.matmul(ps_misc[:, 1:2], ones_r[0:1, :], tstat[0:1, 118:119], start=True, stop=True)
        p.tensor.matmul(ps_misc[:, 2:3], ones_r[0:1, :], tstat[0:1, 117:118], start=True, stop=True)
        p.tensor.matmul(ps_misc[:, 3:4], ones_r[0:1, :], tstat[0:1, 119:120],
                        start=True, stop=True).inc(mm_s)
        p.vector.wait(mm_s)
        vchain(p, vch, [
            lambda: p.vector.tensor_copy(scal[:, 0:1], ps_misc[:, 0:1]),
            lambda: p.vector.tensor_copy(scal[:, 1:2], ps_misc[:, 1:2]),
            lambda: p.vector.tensor_copy(scal[:, 3:4], ps_misc[:, 2:3]),
            lambda: p.vector.tensor_copy(scal[:, 4:5], ps_misc[:, 3:4]),
            lambda: p.vector.tensor_scalar(scal[:, 2:3], scal[:, 1:2], -1.0, None, ALU.mult),
            lambda: p.vector.tensor_scalar(scal[:, 5:6], scal[:, 4:5], -1.0, None, ALU.mult),
        ])

        for ci, (kind, i, sbi, src) in enumerate(chunks):
            w = CW if kind == "fc" else D
            slot = ci % 2
            buf = wbuf[:, slot * CW:slot * CW + w]
            ob = obuf[:, slot * CW:slot * CW + w]
            if ci >= 2:
                p.sync.wait(tern_st[ci % 2], 16 * (ci // 2))
            else:
                p.sync.wait(ab_v)
            p.sync.dma_start(out=buf, in_=src).inc(ld_w[slot])
            thr, nthr, am = ((scal[:, 1:2], scal[:, 2:3], scal[:, 0:1]) if kind == "fc"
                             else (scal[:, 4:5], scal[:, 5:6], scal[:, 3:4]))
            p.vector.wait(ld_w[slot])
            p.vector.wait(tern_v)
            p.vector.tensor_scalar(t1b[:, 0:w], buf, thr, am, ALU.is_gt, ALU.mult)
            p.vector.tensor_scalar(t2b[:, 0:w], buf, nthr, am, ALU.is_lt, ALU.mult).inc(tern_v)
            p.vector.wait(tern_v)
            p.vector.tensor_tensor(ob, t1b[:, 0:w], t2b[:, 0:w], ALU.subtract).inc(tern_v)
            p.scalar.wait(tern_v)
            if kind == "fc":
                f0 = sbi * (CW // 128)
                dst = twfc_d[f0:f0 + CW // 128, i, :, :].rearrange("f p c -> p f c")
                p.scalar.dma_start(out=dst, in_=ob.rearrange("p (f c) -> p f c", c=128)).inc(tern_st[ci % 2])
            else:
                dst = twpj_d[:, i, :, :].rearrange("s p c -> p s c")
                p.scalar.dma_start(out=dst, in_=ob.rearrange("p (s c) -> p s c", c=512)).inc(tern_st[ci % 2])
        p.scalar.wait(tern_st[0])
        p.scalar.wait(tern_st[1])
        p.scalar.nop().inc(tern_done)

    p.barrier()
    with p.scope():
        x_span = p.sb("ff_xspan", (128, 4 * D), F32)
        sq_scr = p.sb("ff_sq", (128, D), BF16)
        stats = p.sb("ff_stats", (128, 16), F32)
        xn_tm = p.sb("ff_xntm", (128, 4 * D), BF16)
        xn_fm = p.sb("ff_xnfm", (128, ND * 512), BF16)
        hbuf = p.sb("ff_h", (128, NF_FC * 512), BF16)
        relu_scr = p.sb("ff_relu", (128, 2 * 512), F32)
        wfc_t = p.sb("ff_wfct", (128, 2 * ND * 128), BF16)
        wpj_t = p.sb("ff_wpjt", (128, NF_FC * 512), BF16)
        yout = p.sb("ff_y", (128, 2 * 512), F32)

        ps_g = p.psum("ff_psg", (128, 2 * 512))
        ps_o = p.psum("ff_pso", (128, 2 * 512))
        ps_t = p.psum("ff_pst", (128, 2 * 8 * 128), BF16)

        ld_x = p.sem("ld_x")
        ld_f = [p.sem("ld_f0", "sw"), p.sem("ld_f1", "sw")]
        ld_p = p.sem("ld_p", "sw")
        mm_g = p.sem("mm_g")
        mm_o = p.sem("mm_o")
        mm_t = p.sem("mm_t")
        ev_t = p.sem("ev_t")
        ev_g = p.sem("ev_g")
        ev_g2 = p.sem("ev_g2")
        ev_o = p.sem("ev_o")
        st_y = [p.sem("st_y0"), p.sem("st_y1")]
        sp_done = p.sem("sp_done")

        ev_t_batches = 0
        for v in range(4):
            if v > 0:
                p.sync.wait(ev_o, v * 16)
            p.sync.wait(tern_done, 1)
            p.sync.dma_start(out=x_span[:, :].rearrange("p (t d) -> p t d", d=D),
                             in_=x[v * 512:(v + 1) * 512, :].rearrange("(t p) d -> p t d", p=128)
                             ).inc(ld_x)
            p.vector.wait(ld_x)
            p.scalar.wait(ld_x)
            xt = [x_span[:, i * D:(i + 1) * D] for i in range(4)]
            s_nm, s_r = emit_ln_stats(p, xt, sq_scr, stats, 0, p.sem(f"lnv{v}"), p.sem(f"lns{v}"))
            if v > 0:
                p.vector.wait(mm_t, 64 * v)
            for i in range(4):
                p.vector.tensor_scalar(xn_tm[:, i * D:(i + 1) * D], xt[i],
                                       s_nm[:, i:i + 1], s_r[:, i:i + 1],
                                       ALU.add, ALU.mult).inc(ev_g)
            p.tensor.wait(ev_g)
            p.tensor.wait(ld_c)

            def srcf(i):
                tt, dt = divmod(i, ND)
                return xn_tm[:, tt * D + dt * 128:tt * D + (dt + 1) * 128]

            def dstf(i):
                tt, dt = divmod(i, ND)
                return xn_fm[:, dt * 512 + tt * 128:dt * 512 + (tt + 1) * 128]
            emit_transpose_tiles(p, srcf, dstf, 64, ident[:, :], ps_t, mm_t, ev_t,
                                 free_base=ev_t_batches)
            ev_t_batches += 8
            for f in range(NF_FC):
                gi = v * NF_FC + f
                slot = f % 2
                if gi == 0:
                    p.gpsimd.wait(tern_done, 1)
                if gi >= 2:
                    p.gpsimd.wait(mm_g, (gi - 1) * ND)
                p.gpsimd.dma_start(
                    out=wfc_t[:, slot * ND * 128:(slot + 1) * ND * 128].rearrange(
                        "p (t c) -> p t c", c=128),
                    in_=twfc_d[f].rearrange("t p c -> p t c")).inc(ld_f[slot])
                p.tensor.wait(ld_f[slot])
                if f == 0:
                    p.tensor.wait(ev_t)
                if gi >= 2:
                    p.tensor.wait(ev_g, v * 4 + 4 + gi - 1)
                for k in range(ND):
                    p.tensor.matmul(ps_g[:, slot * 512:(slot + 1) * 512],
                                    wfc_t[:, (slot * ND + k) * 128:(slot * ND + k + 1) * 128],
                                    xn_fm[:, k * 512:(k + 1) * 512],
                                    start=(k == 0), stop=(k == ND - 1)).inc(mm_g)
                p.vector.wait(mm_g)
                if gi >= 2:
                    p.vector.wait(ev_g2, gi - 1)
                p.vector.tensor_scalar(relu_scr[:, slot * 512:(slot + 1) * 512],
                                       ps_g[:, slot * 512:(slot + 1) * 512],
                                       0.0, None, ALU.max).inc(ev_g)
                p.scalar.wait(ev_g)
                p.scalar.activation(hbuf[:, f * 512:(f + 1) * 512],
                                    relu_scr[:, slot * 512:(slot + 1) * 512],
                                    AF.Square).inc(ev_g2)
            for ds in range(4):
                p.gpsimd.wait(mm_o, (v * 4 + ds) * 4 * NF_FC)
                p.gpsimd.dma_start(out=wpj_t[:, :].rearrange("p (f c) -> p f c", c=512),
                                   in_=twpj_d[ds].rearrange("f p c -> p f c")).inc(ld_p)
                p.tensor.wait(ld_p)
                if ds == 0:
                    p.tensor.wait(ev_g2)
                for tt in range(4):
                    slot = (ds * 4 + tt) % 2
                    if v * 16 + ds * 4 + tt >= 2:
                        p.tensor.wait(ev_o, v * 16 + ds * 4 + tt - 1)
                    for k in range(NF_FC):
                        p.tensor.matmul(ps_o[:, slot * 512:(slot + 1) * 512],
                                        hbuf[:, k * 512 + tt * 128:k * 512 + (tt + 1) * 128],
                                        wpj_t[:, k * 512:(k + 1) * 512],
                                        start=(k == 0), stop=(k == NF_FC - 1)).inc(mm_o)
                    p.vector.wait(mm_o)
                    if v * 16 + ds * 4 + tt >= 2:
                        p.vector.wait(st_y[(ds * 4 + tt) % 2],
                                      16 * ((v * 16 + ds * 4 + tt) // 2))
                    p.vector.tensor_tensor(yout[:, slot * 512:(slot + 1) * 512],
                                           ps_o[:, slot * 512:(slot + 1) * 512],
                                           x_span[:, tt * D + ds * 512:tt * D + (ds + 1) * 512],
                                           ALU.add).inc(ev_o)
                    p.sync.wait(ev_o)
                    p.sync.dma_start(out=y[(v * 4 + tt) * 128:(v * 4 + tt + 1) * 128,
                                           ds * 512:(ds + 1) * 512],
                                     in_=yout[:, slot * 512:(slot + 1) * 512]).inc(st_y[slot])
        p.sync.wait(st_y[0])
        p.sync.wait(st_y[1])


def build_l1_ffn():
    nc = bass.Bass("TRN2", target_bir_lowering=False, debug=False)
    x = nc.dram_tensor("x", [TOK, D], F32, kind="ExternalInput").ap()
    w_fcT = nc.dram_tensor("w_fcT", [D, HFC], F32, kind="ExternalInput").ap()
    w_projT = nc.dram_tensor("w_projT", [HFC, D], F32, kind="ExternalInput").ap()
    ident_i = nc.dram_tensor("ident", [128, 128], BF16, kind="ExternalInput").ap()
    ones_col_i = nc.dram_tensor("ones_col", [128, 1], F32, kind="ExternalInput").ap()
    ones_row_i = nc.dram_tensor("ones_row", [1, 128], F32, kind="ExternalInput").ap()
    y = nc.dram_tensor("y", [TOK, D], F32, kind="ExternalOutput").ap()
    twfc_d = nc.dram_tensor("twfc_d", [NF_FC, ND, 128, 128], BF16).ap()
    twpj_d = nc.dram_tensor("twpj_d", [4, NF_FC, 128, 512], BF16).ap()
    with ExitStack() as ctx:
        p = Prog(nc, ctx)
        emit_l1_ffn(p, x, w_fcT, w_projT, twfc_d, twpj_d, ident_i, ones_col_i,
                    ones_row_i, y)
        p.finalize()
    return nc


def consts():
    return {
        "ident": np.eye(128, dtype=BF),
        "ones_col": np.ones((128, 1), dtype=np.float32),
        "ones_row": np.ones((1, 128), dtype=np.float32),
    }


def shard_rows(full):
    out = []
    for s in range(NCORES):
        b, half = divmod(s, 2)
        out.append(np.ascontiguousarray(full[b, half * TOK:(half + 1) * TOK]))
    return out


def unshard_rows(shards):
    Dd = shards[0].shape[-1]
    full = np.empty((B, S, Dd), dtype=shards[0].dtype)
    for s in range(NCORES):
        b, half = divmod(s, 2)
        full[b, half * TOK:(half + 1) * TOK] = shards[s]
    return full.reshape(B, S, Dd)


def run_l1_ffn(x_full, w_fc, w_proj):
    nc = build_l1_ffn()
    c = consts()
    w_fcT = np.ascontiguousarray(w_fc.T)
    w_projT = np.ascontiguousarray(w_proj.T)
    shards = shard_rows(x_full)
    in_maps = [dict(x=shards[s], w_fcT=w_fcT, w_projT=w_projT, **c)
               for s in range(NCORES)]
    res = run_bass_kernel_spmd(nc, in_maps, core_ids=list(range(NCORES)))
    return unshard_rows([res.results[s]["y"] for s in range(NCORES)])


HG = 5632
NF_G = HG // 128


def emit_l0_ffn(p, x, gate_wT, up_wT, down_wT, gate_s2, up_s2, down_s_row,
                ident_i, ones_col_i, ones_row_i, y):
    ident = p.sb("g_ident", (128, 128), BF16)
    ones_c = p.sb("g_onesc", (128, 1), F32)
    ones_r = p.sb("g_onesr", (1, 128), F32)
    gs_c = p.sb("g_gsc", (128, NF_G), F32)
    us_c = p.sb("g_usc", (128, NF_G), F32)
    ds_row = p.sb("g_dsrow", (1, D), F32)
    ds_b = p.sb("g_dsb", (128, D), F32)

    x_span = p.sb("g_xspan", (128, 4 * D), F32)
    sq_scr = p.sb("g_sq", (128, D), BF16)
    stats = p.sb("g_stats", (128, 16), F32)
    xn_tm = p.sb("g_xntm", (128, 4 * D), BF16)
    xn_fm = p.sb("g_xnfm", (128, ND * 512), BF16)
    hbuf = p.sb("g_h", (128, NF_G * 512), BF16)
    silu_scr = p.sb("g_silu", (128, 2 * 512), F32)
    us_scr = p.sb("g_us", (128, 2 * 512), F32)
    wg_t = p.sb("g_wgt", (128, 2 * ND * 128), BF16)
    wu_t = p.sb("g_wut", (128, 2 * ND * 128), BF16)
    wd_t = p.sb("g_wdt", (128, NF_G * 512), BF16)
    yout = p.sb("g_y", (128, 2 * 512), F32)
    oscr = p.sb("g_oscr", (128, 2 * 512), F32)

    ps_g = p.psum("g_psg", (128, 2 * 512))
    ps_u = p.psum("g_psu", (128, 2 * 512))
    ps_o = p.psum("g_pso", (128, 2 * 512))
    ps_t = p.psum("g_pst", (128, 2 * 8 * 128), BF16)
    ps_m = ps_o

    ld_c = p.sem("ld_c")
    ld_x = p.sem("ld_x")
    ld_g = [p.sem("ld_g0", "sw"), p.sem("ld_g1", "sw")]
    ld_dn = p.sem("ld_dn", "sw")
    vch = p.sem("vch")
    mm_b = p.sem("mm_b")
    mm_g = p.sem("mm_g")
    mm_o = p.sem("mm_o")
    mm_t = p.sem("mm_t")
    ev_t = p.sem("ev_t")
    ev_s = p.sem("ev_s")
    ev_u = p.sem("ev_u")
    ev_h = p.sem("ev_h")
    ev_o1 = p.sem("ev_o1")
    ev_o2 = p.sem("ev_o2")
    st_y = [p.sem("st_y0"), p.sem("st_y1")]

    p.sync.dma_start(out=ident[:, :], in_=ident_i[:, :]).inc(ld_c)
    p.sync.dma_start(out=ones_c[:, :], in_=ones_col_i[:, :]).inc(ld_c)
    p.sync.dma_start(out=ones_r[:, :], in_=ones_row_i[:, :]).inc(ld_c)
    p.sync.dma_start(out=gs_c[:, :], in_=gate_s2[:, :]).inc(ld_c)
    p.sync.dma_start(out=us_c[:, :], in_=up_s2[:, :]).inc(ld_c)
    p.sync.dma_start(out=ds_row[:, :], in_=down_s_row[:, :]).inc(ld_c)
    p.tensor.wait(ld_c)
    for j in range(4):
        p.tensor.matmul(ps_m[:, 0:512], ones_r[0:1, :], ds_row[0:1, j * 512:(j + 1) * 512],
                        start=True, stop=True).inc(mm_b)
        p.vector.wait(mm_b)
        p.vector.tensor_copy(ds_b[:, j * 512:(j + 1) * 512], ps_m[:, 0:512]).inc(vch)
        p.tensor.wait(vch)

    ev_t_batches = 0
    for v in range(4):
        if v > 0:
            p.sync.wait(ev_o2, v * 16)
        p.sync.dma_start(out=x_span[:, :].rearrange("p (t d) -> p t d", d=D),
                         in_=x[v * 512:(v + 1) * 512, :].rearrange("(t p) d -> p t d", p=128)
                         ).inc(ld_x)
        p.vector.wait(ld_x)
        p.scalar.wait(ld_x)
        xt = [x_span[:, i * D:(i + 1) * D] for i in range(4)]
        s_nm, s_r = emit_ln_stats(p, xt, sq_scr, stats, 0, p.sem(f"glnv{v}"), p.sem(f"glns{v}"))
        if v > 0:
            p.vector.wait(mm_t, 64 * v)
        for i in range(4):
            p.vector.tensor_scalar(xn_tm[:, i * D:(i + 1) * D], xt[i],
                                   s_nm[:, i:i + 1], s_r[:, i:i + 1],
                                   ALU.add, ALU.mult).inc(vch)
        p.tensor.wait(vch)

        def srcf(i):
            tt, dt = divmod(i, ND)
            return xn_tm[:, tt * D + dt * 128:tt * D + (dt + 1) * 128]

        def dstf(i):
            tt, dt = divmod(i, ND)
            return xn_fm[:, dt * 512 + tt * 128:dt * 512 + (tt + 1) * 128]
        emit_transpose_tiles(p, srcf, dstf, 64, ident[:, :], ps_t, mm_t, ev_t,
                             free_base=ev_t_batches)
        ev_t_batches += 8

        for f in range(NF_G):
            gi = v * NF_G + f
            slot = f % 2
            if gi >= 2:
                p.gpsimd.wait(mm_g, (gi - 1) * 2 * ND)
            p.gpsimd.dma_start(
                out=wg_t[:, slot * ND * 128:(slot + 1) * ND * 128].rearrange(
                    "p (t c) -> p t c", c=128),
                in_=gate_wT[:, f * 128:(f + 1) * 128].rearrange("(t p) c -> p t c", p=128)
            ).inc(ld_g[slot])
            p.gpsimd.dma_start(
                out=wu_t[:, slot * ND * 128:(slot + 1) * ND * 128].rearrange(
                    "p (t c) -> p t c", c=128),
                in_=up_wT[:, f * 128:(f + 1) * 128].rearrange("(t p) c -> p t c", p=128)
            ).inc(ld_g[slot])
            p.tensor.wait(ld_g[slot])
            if f == 0:
                p.tensor.wait(ev_t)
            if gi >= 2:
                p.tensor.wait(ev_s, gi - 1)
                p.tensor.wait(ev_u, gi - 1)
            for k in range(ND):
                p.tensor.matmul(ps_g[:, slot * 512:(slot + 1) * 512],
                                wg_t[:, (slot * ND + k) * 128:(slot * ND + k + 1) * 128],
                                xn_fm[:, k * 512:(k + 1) * 512],
                                start=(k == 0), stop=(k == ND - 1)).inc(mm_g)
            for k in range(ND):
                p.tensor.matmul(ps_u[:, slot * 512:(slot + 1) * 512],
                                wu_t[:, (slot * ND + k) * 128:(slot * ND + k + 1) * 128],
                                xn_fm[:, k * 512:(k + 1) * 512],
                                start=(k == 0), stop=(k == ND - 1)).inc(mm_g)
            p.scalar.wait(mm_g, (gi * 2 + 1) * ND)
            if gi >= 2:
                p.scalar.wait(ev_h, gi - 1)
            p.scalar.activation(silu_scr[:, slot * 512:(slot + 1) * 512],
                                ps_g[:, slot * 512:(slot + 1) * 512],
                                AF.Silu, scale=gs_c[:, f:f + 1]).inc(ev_s)
            p.vector.wait(mm_g)
            if gi >= 2:
                p.vector.wait(ev_h, gi - 1)
            p.vector.tensor_scalar(us_scr[:, slot * 512:(slot + 1) * 512],
                                   ps_u[:, slot * 512:(slot + 1) * 512],
                                   us_c[:, f:f + 1], None, ALU.mult).inc(ev_u)
            p.vector.wait(ev_s)
            p.vector.wait(ev_u)
            p.vector.tensor_tensor(hbuf[:, f * 512:(f + 1) * 512],
                                   silu_scr[:, slot * 512:(slot + 1) * 512],
                                   us_scr[:, slot * 512:(slot + 1) * 512],
                                   ALU.mult).inc(ev_h)

        for ds in range(4):
            p.gpsimd.wait(mm_o, (v * 4 + ds) * 4 * NF_G)
            p.gpsimd.dma_start(
                out=wd_t[:, :].rearrange("p (f c) -> p f c", c=512),
                in_=down_wT[:, ds * 512:(ds + 1) * 512].rearrange("(f p) c -> p f c", p=128)
            ).inc(ld_dn)
            p.tensor.wait(ld_dn)
            if ds == 0:
                p.tensor.wait(ev_h)
            for tt in range(4):
                idx = v * 16 + ds * 4 + tt
                slot = idx % 2
                if idx >= 2:
                    p.tensor.wait(ev_o1, idx - 1)
                for k in range(NF_G):
                    p.tensor.matmul(ps_o[:, slot * 512:(slot + 1) * 512],
                                    hbuf[:, k * 512 + tt * 128:k * 512 + (tt + 1) * 128],
                                    wd_t[:, k * 512:(k + 1) * 512],
                                    start=(k == 0), stop=(k == NF_G - 1)).inc(mm_o)
                p.vector.wait(mm_o)
                if idx >= 2:
                    p.vector.wait(ev_o2, idx - 1)
                p.vector.tensor_tensor(oscr[:, slot * 512:(slot + 1) * 512],
                                       ps_o[:, slot * 512:(slot + 1) * 512],
                                       ds_b[:, ds * 512:(ds + 1) * 512],
                                       ALU.mult).inc(ev_o1)
                p.vector.wait(ev_o1)
                if idx >= 2:
                    p.vector.wait(st_y[idx % 2], 16 * (idx // 2))
                p.vector.tensor_tensor(yout[:, slot * 512:(slot + 1) * 512],
                                       oscr[:, slot * 512:(slot + 1) * 512],
                                       x_span[:, tt * D + ds * 512:tt * D + (ds + 1) * 512],
                                       ALU.add).inc(ev_o2)
                p.sync.wait(ev_o2)
                p.sync.dma_start(out=y[(v * 4 + tt) * 128:(v * 4 + tt + 1) * 128,
                                       ds * 512:(ds + 1) * 512],
                                 in_=yout[:, slot * 512:(slot + 1) * 512]).inc(st_y[slot])
    p.sync.wait(st_y[0])
    p.sync.wait(st_y[1])


def build_l0_ffn():
    nc = bass.Bass("TRN2", target_bir_lowering=False, debug=False)
    x = nc.dram_tensor("x", [TOK, D], F32, kind="ExternalInput").ap()
    gate_wT = nc.dram_tensor("gate_wT", [D, HG], F32, kind="ExternalInput").ap()
    up_wT = nc.dram_tensor("up_wT", [D, HG], F32, kind="ExternalInput").ap()
    down_wT = nc.dram_tensor("down_wT", [HG, D], F32, kind="ExternalInput").ap()
    gate_s2 = nc.dram_tensor("gate_s2", [128, NF_G], F32, kind="ExternalInput").ap()
    up_s2 = nc.dram_tensor("up_s2", [128, NF_G], F32, kind="ExternalInput").ap()
    down_s_row = nc.dram_tensor("down_s_row", [1, D], F32, kind="ExternalInput").ap()
    ident_i = nc.dram_tensor("ident", [128, 128], BF16, kind="ExternalInput").ap()
    ones_col_i = nc.dram_tensor("ones_col", [128, 1], F32, kind="ExternalInput").ap()
    ones_row_i = nc.dram_tensor("ones_row", [1, 128], F32, kind="ExternalInput").ap()
    y = nc.dram_tensor("y", [TOK, D], F32, kind="ExternalOutput").ap()
    with ExitStack() as ctx:
        p = Prog(nc, ctx)
        emit_l0_ffn(p, x, gate_wT, up_wT, down_wT, gate_s2, up_s2, down_s_row,
                    ident_i, ones_col_i, ones_row_i, y)
        p.finalize()
    return nc


def run_l0_ffn(x_full, gate_w, up_w, down_w, gate_s, up_s, down_s):
    nc = build_l0_ffn()
    c = consts()
    ins = dict(gate_wT=np.ascontiguousarray(gate_w.T),
               up_wT=np.ascontiguousarray(up_w.T),
               down_wT=np.ascontiguousarray(down_w.T),
               gate_s2=np.ascontiguousarray(gate_s.reshape(NF_G, 128).T),
               up_s2=np.ascontiguousarray(up_s.reshape(NF_G, 128).T),
               down_s_row=np.ascontiguousarray(down_s.reshape(1, D)), **c)
    shards = shard_rows(x_full)
    in_maps = [dict(x=shards[s], **ins) for s in range(NCORES)]
    res = run_bass_kernel_spmd(nc, in_maps, core_ids=list(range(NCORES)))
    return unshard_rows([res.results[s]["y"] for s in range(NCORES)])


NH = 16
NHL = 8
HDIM = 128
ALPHA = 1.0 / (30.0 * np.sqrt(128.0))


def emit_l1_attn(p, x2b, x2own, WqTl, WkTl, WvTl, WoT, bq2, bk2, bv_row,
                 bo_row, slopes_row, kpos, qpos_row, ctxidx, masktri_i,
                 ident_i, ones_colb_i, ones_row_i, y,
                 xnfm_d, ctx_d, cc_out, ctxmy_d,
                 fused=False, cc_xnf_in=None, cc_xnf_out=None, xnfidx=None):
    NBLK = S // 128
    ident = p.sb("a_ident", (128, 128), BF16)
    ones_cb = p.sb("a_onescb", (128, 1), BF16)
    ones_r = p.sb("a_onesr", (1, 128), F32)
    masktri = p.sb("a_mask", (128, 128), BF16)
    bq_c = p.sb("a_bqc", (128, NHL), F32)
    bk_c = p.sb("a_bkc", (128, NHL), F32)
    bv_r = p.sb("a_bvr", (1, NHL * 128), F32)
    slopes = p.sb("a_slopes", (1, NHL), F32)
    kpos_sb = p.sb("a_kpos", (128, NBLK), F32)

    ld_c = p.sem("a_ldc")
    st_f = p.sem("a_stf")
    st_c = p.sem("a_stc")
    cc_s = p.sem("a_ccs", "cc")
    st_m = [p.sem("a_stm0"), p.sem("a_stm1")]
    for dst, src_ in ((ident, ident_i), (ones_cb, ones_colb_i), (ones_r, ones_row_i),
                      (masktri, masktri_i), (bq_c, bq2), (bk_c, bk2),
                      (bv_r, bv_row), (slopes, slopes_row),
                      (kpos_sb, kpos)):
        p.sync.dma_start(out=dst[:, :], in_=src_[:, :]).inc(ld_c)

    with p.scope():
        x_span = p.sb("a_xspan", (128, 4 * D), F32)
        sq_scr = p.sb("a_sq", (128, D), BF16)
        stats = p.sb("a_stats", (128, 16), F32)
        xn_tm = p.sb("a_xntm", (128, 4 * D), BF16)
        xn_fm = p.sb("a_xnfm", (128, ND * 512), BF16)
        ps_t = p.psum("a_pst", (128, 2 * 8 * 128), BF16)

        ld_x = p.sem("a_ldx")
        vch = p.sem("a_vch")
        mm_t = p.sem("a_mmt")
        ev_t = p.sem("a_evt")
        st_xn = p.sem("a_stxn")
        cc_xn = p.sem("a_ccxn", "cc")
        g_xn = [p.sem("a_gxn0", "sw"), p.sem("a_gxn1", "sw")]
        st_tm = [p.sem("a_sttm0"), p.sem("a_sttm1")]

        if fused:
            for v in range(4):
                if v > 0:
                    p.sync.wait(vch)
                p.sync.dma_start(out=x_span[:, :].rearrange("p (t d) -> p t d", d=D),
                                 in_=x2own[v * 512:(v + 1) * 512, :].rearrange(
                                     "(t p) d -> p t d", p=128)).inc(ld_x)
                p.vector.wait(ld_x)
                p.scalar.wait(ld_x)
                xt = [x_span[:, i * D:(i + 1) * D] for i in range(4)]
                s_nm, s_r = emit_ln_stats(p, xt, sq_scr, stats, 0,
                                          p.sem(f"afln{v}"), p.sem(f"aflns{v}"))
                for i in range(4):
                    p.vector.wait(st_xn)
                    p.vector.tensor_scalar(xn_tm[:, i * D:(i + 1) * D], xt[i],
                                           s_nm[:, i:i + 1], s_r[:, i:i + 1],
                                           ALU.add, ALU.mult).inc(vch)
                    p.scalar.wait(vch)
                    p.scalar.dma_start(
                        out=cc_xnf_in[(v * 4 + i) * 128:(v * 4 + i + 1) * 128, :],
                        in_=xn_tm[:, i * D:(i + 1) * D]).inc(st_xn)
            p.gpsimd.wait(st_xn)
            p.gpsimd.collective_compute("AllGather", ALU.bypass,
                                        replica_groups=[list(range(8))],
                                        ins=[cc_xnf_in.opt()],
                                        outs=[cc_xnf_out.opt()]).inc(cc_xn)
            idx_xn = p.sb("a_idxxn", (128, 32), mybir.dt.int32)
            ld_ix = p.sem("a_ldix")
            p.sync.dma_start(out=idx_xn[:, :], in_=xnfidx[:, :]).inc(ld_ix)
            p.gpsimd.wait(ld_ix)
            p.gpsimd.wait(cc_xn)
            ev_t_batches = 0
            for v in range(8):
                for i in range(4):
                    g = v * 4 + i
                    if v >= 1:
                        p.gpsimd.wait(mm_t, 64 * v)
                    p.gpsimd.indirect_dma_start(
                        xn_tm[:, i * D:(i + 1) * D],
                        None, cc_xnf_out,
                        bass.IndirectOffsetOnAxis(idx_xn[:, g:g + 1], 0)).inc(g_xn[0])
                p.tensor.wait(g_xn[0])
                p.tensor.wait(ld_c)
                if v > 0:
                    p.vector.wait(st_f, 16 * v)

                def srcf(i):
                    tt, dt = divmod(i, ND)
                    return xn_tm[:, tt * D + dt * 128:tt * D + (dt + 1) * 128]

                def dstf(i):
                    tt, dt = divmod(i, ND)
                    return xn_fm[:, dt * 512 + tt * 128:dt * 512 + (tt + 1) * 128]
                emit_transpose_tiles(p, srcf, dstf, 64, ident[:, :], ps_t, mm_t, ev_t,
                                     free_base=ev_t_batches)
                ev_t_batches += 8
                p.sync.wait(ev_t)
                p.sync.dma_start(
                    out=xnfm_d[:, v * 512:(v + 1) * 512].rearrange("(t p) c -> p t c", p=128),
                    in_=xn_fm[:, :].rearrange("p (t c) -> p t c", c=512)).inc(st_f)
        else:
            ev_t_batches = 0
            for v in range(8):
                if v > 0:
                    p.sync.wait(vch)
                p.sync.dma_start(out=x_span[:, :].rearrange("p (t d) -> p t d", d=D),
                                 in_=x2b[v * 512:(v + 1) * 512, :].rearrange(
                                     "(t p) d -> p t d", p=128)).inc(ld_x)
                p.vector.wait(ld_x)
                p.scalar.wait(ld_x)
                xt = [x_span[:, i * D:(i + 1) * D] for i in range(4)]
                s_nm, s_r = emit_ln_stats(p, xt, sq_scr, stats, 0,
                                          p.sem(f"alnv{v}"), p.sem(f"alns{v}"))
                if v > 0:
                    p.vector.wait(mm_t, 64 * v)
                    p.vector.wait(st_f, 16 * v)
                for i in range(4):
                    p.vector.tensor_scalar(xn_tm[:, i * D:(i + 1) * D], xt[i],
                                           s_nm[:, i:i + 1], s_r[:, i:i + 1],
                                           ALU.add, ALU.mult).inc(vch)
                p.tensor.wait(vch)
                p.tensor.wait(ld_c)

                def srcf(i):
                    tt, dt = divmod(i, ND)
                    return xn_tm[:, tt * D + dt * 128:tt * D + (dt + 1) * 128]

                def dstf(i):
                    tt, dt = divmod(i, ND)
                    return xn_fm[:, dt * 512 + tt * 128:dt * 512 + (tt + 1) * 128]
                emit_transpose_tiles(p, srcf, dstf, 64, ident[:, :], ps_t, mm_t, ev_t,
                                     free_base=ev_t_batches)
                ev_t_batches += 8
                p.sync.wait(ev_t)
                p.sync.dma_start(
                    out=xnfm_d[:, v * 512:(v + 1) * 512].rearrange("(t p) c -> p t c", p=128),
                    in_=xn_fm[:, :].rearrange("p (t c) -> p t c", c=512)).inc(st_f)

    p.barrier()
    with p.scope():
        xnb = p.sb("a_xnb", (128, 2 * ND * 512), BF16)
        wk_g = p.sb("a_wkg", (128, 4 * ND * 128), BF16)
        wq_g = p.sb("a_wqg", (128, 4 * ND * 128), BF16)
        wv_g = p.sb("a_wvg", (128, ND * 512), BF16)
        K_g = p.sb("a_Kg", (128, 4 * S), BF16)
        V_g = p.sb("a_Vg", (128, NBLK * 512), BF16)
        Q_g = p.sb("a_Qg", (128, 4 * S), BF16)
        bv_b = p.sb("a_bvb", (128, 512), F32)
        qterm = p.sb("a_qterm", (128, 512), F32)
        qrow_s = p.sb("a_qrow", (1, 512), F32)
        qpos_sp = p.sb("a_qposs", (1, 512), F32)
        kterm = p.sb("a_kterm", (128, NBLK), F32)
        srow = p.sb("a_srow", (128, 1), F32)
        e1 = p.sb("a_e1", (128, 2 * 512), F32)
        e2 = p.sb("a_e2", (128, 2 * 512), F32)
        pb = p.sb("a_pb", (128, 4 * 512), BF16)
        sums = p.sb("a_sums", (1, 512), F32)
        rec_s = p.sb("a_recs", (128, 512), F32)
        ctx_sb = p.sb("a_ctx", (128, 512), BF16)

        ps_a = p.psum("a_psa", (128, 2 * 512))
        ps_s = p.psum("a_pss", (128, 2 * 512))
        ps_c = p.psum("a_psc", (128, 512))
        ps_r = p.psum("a_psr", (1, 512))
        ps_b = p.psum("a_psb", (128, 512))

        ld_b = [p.sem("a_ldb0"), p.sem("a_ldb1")]
        ld_qp = p.sem("a_ldqp")
        ld_w = p.sem("a_ldw", "sw")
        vch = p.sem("b_vch")
        mm_a = p.sem("a_mma")
        ev_a = p.sem("a_eva")
        mm_s = p.sem("a_mms")
        ev_s1 = p.sem("a_evs1")
        ev_s3 = p.sem("a_evs3")
        sch_t = p.sem("a_scht")
        ev_m = p.sem("a_evm")
        mm_c = p.sem("a_mmc")
        ev_c = p.sem("a_evc")
        mm_b2 = p.sem("a_mmb2")

        ag = 0
        sc_i = 0

        def accum_group(mms_fn, evac_fn):
            nonlocal ag
            slot = ag % 2
            if ag >= 2:
                p.tensor.wait(ev_a, ag - 1)
            mms_fn(slot)
            p.vector.wait(mm_a)
            evac_fn(slot)
            p.q["vector"][-1].inc(ev_a)
            ag += 1

        for hg in range(2):
            heads = [4 * hg + i for i in range(4)]
            p.gpsimd.wait(mm_a)
            for i, h in enumerate(heads):
                p.gpsimd.dma_start(
                    out=wk_g[:, i * ND * 128:(i + 1) * ND * 128].rearrange(
                        "p (t c) -> p t c", c=128),
                    in_=WkTl[:, h * 128:(h + 1) * 128].rearrange(
                        "(t p) c -> p t c", p=128)).inc(ld_w)
                p.gpsimd.dma_start(
                    out=wq_g[:, i * ND * 128:(i + 1) * ND * 128].rearrange(
                        "p (t c) -> p t c", c=128),
                    in_=WqTl[:, h * 128:(h + 1) * 128].rearrange(
                        "(t p) c -> p t c", p=128)).inc(ld_w)
            p.gpsimd.dma_start(
                out=wv_g[:, :].rearrange("p (t c) -> p t c", c=512),
                in_=WvTl[:, hg * 512:(hg + 1) * 512].rearrange(
                    "(t p) c -> p t c", p=128)).inc(ld_w)
            p.tensor.wait(ld_c)
            p.tensor.wait(ev_c)
            p.tensor.matmul(ps_b[:, :], ones_r[0:1, :],
                            bv_r[0:1, hg * 512:(hg + 1) * 512],
                            start=True, stop=True).inc(mm_b2)
            p.vector.wait(mm_b2)
            p.vector.tensor_copy(bv_b[:, :], ps_b[:, :]).inc(vch)

            def issue_load(s_):
                slot = s_ % 2
                p.sync.wait(mm_a)
                if hg == 0 and s_ < 2:
                    p.sync.wait(st_f, 16 * 8)
                p.sync.dma_start(
                    out=xnb[:, slot * ND * 512:(slot + 1) * ND * 512].rearrange(
                        "p (t c) -> p t c", c=512),
                    in_=xnfm_d[:, s_ * 512:(s_ + 1) * 512].rearrange(
                        "(t p) c -> p t c", p=128)).inc(ld_b[slot])
            issue_load(0)
            issue_load(1)
            for s_ in range(8):
                slot = s_ % 2
                p.tensor.wait(ld_b[slot])
                p.tensor.wait(ld_w)
                p.tensor.wait(vch)
                p.tensor.wait(ev_s1)
                xb = xnb[:, slot * ND * 512:(slot + 1) * ND * 512]
                for i, h in enumerate(heads):
                    def mk(pslot, i=i):
                        for k in range(ND):
                            p.tensor.matmul(ps_a[:, pslot * 512:(pslot + 1) * 512],
                                            wk_g[:, (i * ND + k) * 128:(i * ND + k + 1) * 128],
                                            xb[:, k * 512:(k + 1) * 512],
                                            start=(k == 0), stop=(k == ND - 1)).inc(mm_a)

                    def ek(pslot, i=i, h=h, s_=s_):
                        p.vector.tensor_scalar(
                            K_g[:, i * S + s_ * 512:i * S + (s_ + 1) * 512],
                            ps_a[:, pslot * 512:(pslot + 1) * 512],
                            bk_c[:, h:h + 1], None, ALU.add)
                    accum_group(mk, ek)
                    def mq(pslot, i=i):
                        for k in range(ND):
                            p.tensor.matmul(ps_a[:, pslot * 512:(pslot + 1) * 512],
                                            wq_g[:, (i * ND + k) * 128:(i * ND + k + 1) * 128],
                                            xb[:, k * 512:(k + 1) * 512],
                                            start=(k == 0), stop=(k == ND - 1)).inc(mm_a)

                    def eq(pslot, i=i, h=h, s_=s_):
                        p.vector.tensor_scalar(
                            Q_g[:, i * S + s_ * 512:i * S + (s_ + 1) * 512],
                            ps_a[:, pslot * 512:(pslot + 1) * 512],
                            bq_c[:, h:h + 1], ALPHA, ALU.add, ALU.mult)
                    accum_group(mq, eq)
                for b_ in range(4):
                    def mv(pslot, b_=b_):
                        for k in range(ND):
                            p.tensor.matmul(ps_a[:, pslot * 512:(pslot + 1) * 512],
                                            xb[:, k * 512 + b_ * 128:k * 512 + (b_ + 1) * 128],
                                            wv_g[:, k * 512:(k + 1) * 512],
                                            start=(k == 0), stop=(k == ND - 1)).inc(mm_a)

                    def ev(pslot, b_=b_, s_=s_):
                        p.vector.tensor_tensor(
                            V_g[:, (s_ * 4 + b_) * 512:(s_ * 4 + b_ + 1) * 512],
                            ps_a[:, pslot * 512:(pslot + 1) * 512],
                            bv_b[:, :], ALU.add)
                    accum_group(mv, ev)
                if s_ + 2 < 8:
                    issue_load(s_ + 2)

            for i, h in enumerate(heads):
                hglob = hg * 4 + i
                p.tensor.wait(ev_a)
                p.tensor.wait(ev_c)
                p.tensor.matmul(ps_b[:, 0:1], ones_r[0:1, :], slopes[0:1, hglob:hglob + 1],
                                start=True, stop=True).inc(mm_b2)
                p.vector.wait(mm_b2)
                vchain(p, vch, [
                    lambda: p.vector.tensor_copy(srow[:, 0:1], ps_b[:, 0:1]),
                    lambda: p.vector.tensor_scalar(kterm[:, :], kpos_sb[:, :],
                                                   srow[:, 0:1], ALPHA,
                                                   ALU.mult, ALU.mult),
                ])
                for v in range(8):
                    nk = 4 * v + 4
                    first_ct = True
                    p.sync.wait(vch)
                    p.sync.dma_start(out=qpos_sp[0:1, :],
                                     in_=qpos_row[0:1, v * 512:(v + 1) * 512]).inc(ld_qp)
                    p.vector.wait(ld_qp)
                    vchain(p, vch, [
                        lambda: p.vector.tensor_scalar(
                            qrow_s[0:1, :], qpos_sp[0:1, :],
                            slopes[0:1, hglob:hglob + 1], -ALPHA,
                            ALU.mult, ALU.mult),
                    ])
                    p.tensor.wait(vch)
                    p.tensor.wait(ev_c)
                    p.tensor.matmul(ps_b[:, :], ones_r[0:1, :], qrow_s[0:1, :],
                                    start=True, stop=True).inc(mm_b2)
                    p.vector.wait(mm_b2)
                    if sc_i >= 1:
                        p.vector.wait(ev_s1)
                    p.vector.tensor_copy(qterm[:, :], ps_b[:, :]).inc(vch)
                    p.tensor.wait(vch)
                    for kt in range(nk):
                        i0 = max(0, kt - 4 * v)
                        q0 = i0 * 128
                        sslot = sc_i % 4
                        eslot = sc_i % 2
                        sps = ps_s if sslot < 2 else ps_a
                        sb0 = (sslot % 2) * 512
                        last_kt = (kt == nk - 1)
                        if sc_i >= 4:
                            p.tensor.wait_abs(ev_s1, ev_s1.n - 3)
                        p.tensor.matmul(
                            sps[:, sb0 + q0:sb0 + 512],
                            K_g[:, i * S + kt * 128:i * S + (kt + 1) * 128],
                            Q_g[:, i * S + v * 512 + q0:i * S + (v + 1) * 512],
                            start=True, stop=True).inc(mm_s)
                        p.vector.wait(mm_s)
                        if sc_i >= 2:
                            p.vector.wait_abs(sch_t, sch_t.n - 1)
                        p.vector.scalar_tensor_tensor(
                            e1[:, eslot * 512 + q0:eslot * 512 + 512],
                            sps[:, sb0 + q0:sb0 + 512],
                            kterm[:, kt:kt + 1],
                            qterm[:, q0:512],
                            ALU.add, ALU.add).inc(ev_s1)
                        p.scalar.wait(ev_s1)
                        p.scalar.activation(e2[:, eslot * 512 + q0:eslot * 512 + 512],
                                            e1[:, eslot * 512 + q0:eslot * 512 + 512],
                                            AF.Tanh).inc(sch_t)
                        if sc_i >= 4:
                            p.scalar.wait_abs(mm_c, mm_c.n - 6)
                        p.scalar.wait(sch_t)
                        p.scalar.activation(pb[:, sslot * 512 + q0:sslot * 512 + 512],
                                            e2[:, eslot * 512 + q0:eslot * 512 + 512],
                                            AF.Exp, scale=30.0).inc(ev_s3)
                        diag = (kt >= 4 * v)
                        if diag:
                            p.vector.wait(ev_s3)
                            p.vector.tensor_tensor(
                                pb[:, sslot * 512 + q0:sslot * 512 + q0 + 128],
                                pb[:, sslot * 512 + q0:sslot * 512 + q0 + 128],
                                masktri[:, :], ALU.mult).inc(ev_m)
                        p.tensor.wait(ev_s3)
                        if diag:
                            p.tensor.wait(ev_m)
                        p.tensor.matmul(ps_c[:, q0:512],
                                        V_g[:, kt * 512 + i * 128:kt * 512 + (i + 1) * 128],
                                        pb[:, sslot * 512 + q0:sslot * 512 + 512],
                                        start=first_ct, stop=last_kt).inc(mm_c)
                        p.tensor.matmul(ps_r[0:1, q0:512], ones_cb[:, 0:1],
                                        pb[:, sslot * 512 + q0:sslot * 512 + 512],
                                        start=first_ct, stop=last_kt).inc(mm_c)
                        first_ct = False
                        sc_i += 1
                    idx_hv = hglob * 8 + v
                    p.vector.wait(mm_c)
                    if idx_hv > 0:
                        p.vector.wait(st_c, 16 * idx_hv)
                    vchain(p, vch, [
                        lambda: p.vector.tensor_copy(sums[0:1, :], ps_r[0:1, :]),
                        lambda: p.vector.reciprocal(sums[0:1, :], sums[0:1, :]),
                    ])
                    p.tensor.wait(vch)
                    p.tensor.wait(ev_c)
                    p.tensor.matmul(ps_b[:, :], ones_r[0:1, :], sums[0:1, :],
                                    start=True, stop=True).inc(mm_b2)
                    p.vector.wait(mm_b2)
                    p.vector.tensor_copy(rec_s[:, :], ps_b[:, :]).inc(ev_c)
                    p.vector.wait(ev_c)
                    p.vector.tensor_tensor(ctx_sb[:, :], ps_c[:, :], rec_s[:, :],
                                           ALU.mult).inc(ev_c)
                    p.scalar.wait(ev_c)
                    p.scalar.dma_start(out=ctx_d[hglob, :, v * 512:(v + 1) * 512],
                                       in_=ctx_sb[:, :]).inc(st_c)

    p.barrier()
    p.gpsimd.wait(st_c)
    for ha in range(4):
        p.gpsimd.collective_compute(
            "AllGather", ALU.bypass,
            replica_groups=[list(range(8))],
            ins=[ctx_d.ap()[ha * 2:(ha + 1) * 2].opt()],
            outs=[cc_out.ap()[ha:ha + 1].opt()]).inc(cc_s)

    with p.scope():
        idx_s = p.sb("s_idx", (128, NH), mybir.dt.int32)
        selO = p.sb("s_o", (128, 2 * TOK), BF16)
        ld_i = p.sem("s_ldi")
        g_s = [p.sem("s_gs0", "sw"), p.sem("s_gs1", "sw")]
        cc_v = cc_out.ap().rearrange("a r h p (t c) -> (a r h p t) c", c=TOK)
        p.sync.dma_start(out=idx_s[:, :], in_=ctxidx[:, :]).inc(ld_i)
        p.gpsimd.wait(ld_i)
        for h in range(NH):
            slot = h % 2
            if h >= 2:
                p.gpsimd.wait(st_m[h % 2], 16 * (h // 2))
            p.gpsimd.wait(cc_s)
            p.gpsimd.indirect_dma_start(
                selO[:, slot * TOK:(slot + 1) * TOK], None, cc_v,
                bass.IndirectOffsetOnAxis(idx_s[:, h:h + 1], 0)).inc(g_s[slot])
            p.scalar.wait(g_s[slot])
            p.scalar.dma_start(out=ctxmy_d[h, :, :],
                               in_=selO[:, slot * TOK:(slot + 1) * TOK]).inc(st_m[h % 2])

    p.barrier()
    with p.scope():
        ctx_l = p.sb("c_ctx", (128, NH * 512), BF16)
        bo_r = p.sb("c_bor", (1, D), F32)
        wo_t = p.sb("c_wo", (128, ND * 512), BF16)
        x_span = p.sb("c_xspan", (128, 4 * D), F32)
        bo_b = p.sb("c_bob", (128, D), F32)
        oscr = p.sb("c_oscr", (128, 2 * 512), F32)
        yout = p.sb("c_y", (128, 2 * 512), F32)
        ps_o = p.psum("c_pso", (128, 2 * 512))
        ps_b2 = p.psum("c_psb", (128, 512))

        ld_ct = p.sem("c_ldct", "sw")
        ld_wo = p.sem("c_ldwo", "sw")
        ld_x2 = p.sem("c_ldx2")
        vch = p.sem("c_vch")
        mm_bb = p.sem("c_mmb")
        mm_o = p.sem("c_mmo")
        ev_o1 = p.sem("c_evo1")
        ev_o2 = p.sem("c_evo2")
        st_y = [p.sem("c_sty0"), p.sem("c_sty1")]

        p.sync.dma_start(out=bo_r[:, :], in_=bo_row[:, :]).inc(ld_c)
        p.tensor.wait(ld_c)
        for j in range(4):
            p.tensor.matmul(ps_b2[:, :], ones_r[0:1, :], bo_r[0:1, j * 512:(j + 1) * 512],
                            start=True, stop=True).inc(mm_bb)
            p.vector.wait(mm_bb)
            p.vector.tensor_copy(bo_b[:, j * 512:(j + 1) * 512], ps_b2[:, :]).inc(vch)
            p.tensor.wait(vch)

        for v in range(4):
            if v > 0:
                p.sync.wait(ev_o2, v * 16)
            p.sync.dma_start(
                out=x_span[:, :].rearrange("p (t d) -> p t d", d=D),
                in_=x2own[v * 512:(v + 1) * 512, :].rearrange(
                    "(t p) d -> p t d", p=128)).inc(ld_x2)
            p.gpsimd.wait(mm_o)
            p.gpsimd.wait(st_m[0])
            p.gpsimd.wait(st_m[1])
            p.gpsimd.dma_start(
                out=ctx_l[:, :].rearrange("p (h c) -> p h c", c=512),
                in_=ctxmy_d[:, :, v * 512:(v + 1) * 512].rearrange("h p c -> p h c")
            ).inc(ld_ct)
            for ds in range(4):
                p.gpsimd.wait(mm_o, (v * 4 + ds) * 4 * NH)
                p.gpsimd.dma_start(
                    out=wo_t[:, :].rearrange("p (t c) -> p t c", c=512),
                    in_=WoT[:, ds * 512:(ds + 1) * 512].rearrange(
                        "(t p) c -> p t c", p=128)).inc(ld_wo)
                p.tensor.wait(ld_wo)
                p.tensor.wait(ld_ct)
                for tt in range(4):
                    idx = v * 16 + ds * 4 + tt
                    slot = idx % 2
                    if idx >= 2:
                        p.tensor.wait(ev_o1, idx - 1)
                    for k in range(NH):
                        p.tensor.matmul(ps_o[:, slot * 512:(slot + 1) * 512],
                                        ctx_l[:, k * 512 + tt * 128:k * 512 + (tt + 1) * 128],
                                        wo_t[:, k * 512:(k + 1) * 512],
                                        start=(k == 0), stop=(k == NH - 1)).inc(mm_o)
                    p.vector.wait(mm_o)
                    p.vector.wait(ld_x2)
                    if idx >= 2:
                        p.vector.wait(ev_o2, idx - 1)
                    p.vector.tensor_tensor(oscr[:, slot * 512:(slot + 1) * 512],
                                           ps_o[:, slot * 512:(slot + 1) * 512],
                                           bo_b[:, ds * 512:(ds + 1) * 512],
                                           ALU.add).inc(ev_o1)
                    p.vector.wait(ev_o1)
                    if idx >= 2:
                        p.vector.wait(st_y[idx % 2], 16 * (idx // 2))
                    p.vector.tensor_tensor(yout[:, slot * 512:(slot + 1) * 512],
                                           oscr[:, slot * 512:(slot + 1) * 512],
                                           x_span[:, tt * D + ds * 512:tt * D + (ds + 1) * 512],
                                           ALU.add).inc(ev_o2)
                    p.sync.wait(ev_o2)
                    p.sync.dma_start(out=y[(v * 4 + tt) * 128:(v * 4 + tt + 1) * 128,
                                           ds * 512:(ds + 1) * 512],
                                     in_=yout[:, slot * 512:(slot + 1) * 512]).inc(st_y[slot])
        p.sync.wait(st_y[0])
        p.sync.wait(st_y[1])


def build_l1_attn(dbg=False):
    nc = bass.Bass("TRN2", target_bir_lowering=False, debug=False)
    x2b = nc.dram_tensor("x2b", [S, D], F32, kind="ExternalInput").ap()
    x2own = nc.dram_tensor("x2own", [TOK, D], F32, kind="ExternalInput").ap()
    WqTl = nc.dram_tensor("WqTl", [D, NHL * 128], F32, kind="ExternalInput").ap()
    WkTl = nc.dram_tensor("WkTl", [D, NHL * 128], F32, kind="ExternalInput").ap()
    WvTl = nc.dram_tensor("WvTl", [D, NHL * 128], F32, kind="ExternalInput").ap()
    WoT = nc.dram_tensor("WoT", [D, D], F32, kind="ExternalInput").ap()
    bq2 = nc.dram_tensor("bq2", [128, NHL], F32, kind="ExternalInput").ap()
    bk2 = nc.dram_tensor("bk2", [128, NHL], F32, kind="ExternalInput").ap()
    bv_row = nc.dram_tensor("bv_row", [1, NHL * 128], F32, kind="ExternalInput").ap()
    bo_row = nc.dram_tensor("bo_row", [1, D], F32, kind="ExternalInput").ap()
    slopes_row = nc.dram_tensor("slopes_row", [1, NHL], F32, kind="ExternalInput").ap()
    kpos = nc.dram_tensor("kpos", [128, S // 128], F32, kind="ExternalInput").ap()
    qpos_row = nc.dram_tensor("qpos_row", [1, S], F32, kind="ExternalInput").ap()
    ctxidx = nc.dram_tensor("ctxidx", [128, NH], mybir.dt.int32, kind="ExternalInput").ap()
    masktri_i = nc.dram_tensor("masktri", [128, 128], BF16, kind="ExternalInput").ap()
    ident_i = nc.dram_tensor("ident", [128, 128], BF16, kind="ExternalInput").ap()
    ones_colb_i = nc.dram_tensor("ones_colb", [128, 1], BF16, kind="ExternalInput").ap()
    ones_row_i = nc.dram_tensor("ones_row", [1, 128], F32, kind="ExternalInput").ap()
    y = nc.dram_tensor("y", [TOK, D], F32, kind="ExternalOutput").ap()
    kind = dict(kind="ExternalOutput") if dbg else {}
    xnfm_d = nc.dram_tensor("xnfm_d", [D, S], BF16, **kind).ap()
    ctx_d = nc.dram_tensor("ctx_d", [NHL, 128, S], BF16)
    ctx_dbg = nc.dram_tensor("ctx_dbg", [NHL, 128, S], BF16, **kind) if dbg else None
    cc_out = nc.dram_tensor("cc_out", [4, NCORES, 2, 128, S], BF16)
    ctxmy_d = nc.dram_tensor("ctxmy_d", [NH, 128, TOK], BF16, **kind).ap()
    with ExitStack() as ctx:
        p = Prog(nc, ctx)
        emit_l1_attn(p, x2b, x2own, WqTl, WkTl, WvTl, WoT, bq2, bk2, bv_row,
                     bo_row, slopes_row, kpos, qpos_row, ctxidx, masktri_i,
                     ident_i, ones_colb_i, ones_row_i, y,
                     xnfm_d, ctx_d, cc_out, ctxmy_d)
        if dbg:
            dsem = p.sem("dbgcp", "sw")
            p.gpsimd.dma_start(out=ctx_dbg[:, :, :], in_=ctx_d[:, :, :]).inc(dsem)
            p.gpsimd.wait(dsem)
        p.finalize()
    return nc


def mk_ctxidx(b, half):
    idx = np.empty((128, NH), dtype=np.int32)
    for h in range(NH):
        hl = h % NHL
        r = 2 * b + h // NHL
        ha = hl // 2
        base = (((ha * NCORES + r) * 2) + hl % 2) * 128
        idx[:, h] = (base + np.arange(128)) * 2 + half
    return idx


def attn_consts():
    mask = np.zeros((128, 128), dtype=BF)
    for jk in range(128):
        mask[jk, jk:] = 1.0
    kpos = np.empty((128, S // 128), dtype=np.float32)
    for g in range(S // 128):
        kpos[:, g] = g * 128 + np.arange(128)
    return {
        "masktri": mask,
        "ones_colb": np.ones((128, 1), dtype=BF),
        "kpos": kpos,
        "qpos_row": np.arange(S, dtype=np.float32).reshape(1, S),
    }


def run_l1_attn(x2_full, Wq, bq, Wk, bk, Wv, bv, Wo, bo, slopes, dbg=False):
    c = consts()
    ca = attn_consts()
    nc = build_l1_attn(dbg)
    in_maps = []
    for s in range(NCORES):
        b, half = divmod(s, 2)
        hs = slice(half * NHL * 128, (half + 1) * NHL * 128)
        hh = slice(half * NHL, (half + 1) * NHL)
        in_maps.append(dict(
            x2b=np.ascontiguousarray(x2_full[b]),
            x2own=np.ascontiguousarray(x2_full[b][half * TOK:(half + 1) * TOK]),
            WqTl=np.ascontiguousarray(Wq.T[:, hs]),
            WkTl=np.ascontiguousarray(Wk.T[:, hs]),
            WvTl=np.ascontiguousarray(Wv.T[:, hs]),
            WoT=np.ascontiguousarray(Wo.T),
            bq2=np.ascontiguousarray(bq[hs].reshape(NHL, 128).T),
            bk2=np.ascontiguousarray(bk[hs].reshape(NHL, 128).T),
            bv_row=np.ascontiguousarray(bv[hs].reshape(1, NHL * 128)),
            bo_row=np.ascontiguousarray(bo.reshape(1, D)),
            slopes_row=np.ascontiguousarray(slopes[hh].reshape(1, NHL)),
            ctxidx=mk_ctxidx(b, half),
            ident=c["ident"], ones_row=c["ones_row"],
            masktri=ca["masktri"], ones_colb=ca["ones_colb"],
            kpos=ca["kpos"], qpos_row=ca["qpos_row"]))
    res = run_bass_kernel_spmd(nc, in_maps, core_ids=list(range(NCORES)))
    if dbg:
        return res
    return unshard_rows([res.results[s]["y"] for s in range(NCORES)])


MSL = 256
NCH = 32
CH = 512
LRM = 0.01


def emit_l0_attn(p, x_own, x_dsl, W0T_s, W1RT_s, W1C_s, b0_2, b1row, lr_wT,
                 fg_wT, lrfgb, oidx, ident_i, ones_colb_i, ones_row_i, y, cc, out_d):
    ident = p.sb("t_ident", (128, 128), BF16)
    ones_cb = p.sb("t_onescb", (128, 1), BF16)
    ones_r = p.sb("t_onesr", (1, 128), F32)

    ld_c = p.sem("t_ldc")
    for dst, src_ in ((ident, ident_i), (ones_cb, ones_colb_i), (ones_r, ones_row_i)):
        p.sync.dma_start(out=dst[:, :], in_=src_[:, :]).inc(ld_c)

    W0T = p.sb("t_W0T", (128, ND * MSL), F32)
    W0Tb = p.sb("t_W0Tb", (128, ND * MSL), BF16)
    W1RT = p.sb("t_W1RT", (128, ND * MSL), F32)
    W1RTb = p.sb("t_W1RTb", (128, ND * MSL), BF16)
    W1Cb = p.sb("t_W1Cb", (128, ND * MSL), BF16)
    b0c = p.sb("t_b0c", (128, 2), F32)
    b1r = p.sb("t_b1r", (1, MSL), F32)
    b1b = p.sb("t_b1b", (128, MSL), F32)
    lrw = p.sb("t_lrw", (128, ND), BF16)
    fgw = p.sb("t_fgw", (128, ND), BF16)
    lrb = p.sb("t_lrb", (1, 2), F32)
    scal = p.sb("t_scal", (128, 2), F32)
    stat_t = p.sb("t_stat", (1, 8), F32)

    ld_w = p.sem("t_ldw")
    ld_wg = p.sem("t_ldwg", "sw")
    p.sync.dma_start(out=W0T[:, :].rearrange("p (t c) -> p t c", c=MSL),
                     in_=W0T_s.rearrange("(t p) c -> p t c", p=128)).inc(ld_w)
    p.sync.dma_start(out=W1RT[:, :].rearrange("p (t c) -> p t c", c=MSL),
                     in_=W1RT_s.rearrange("(t p) c -> p t c", p=128)).inc(ld_w)
    p.gpsimd.dma_start(out=W1Cb[:, :].rearrange("p (t c) -> p t c", c=MSL),
                     in_=W1C_s.rearrange("(t p) c -> p t c", p=128)).inc(ld_wg)
    p.sync.dma_start(out=b0c[:, :], in_=b0_2[:, :]).inc(ld_w)
    p.sync.dma_start(out=b1r[:, :], in_=b1row[:, :]).inc(ld_w)
    p.gpsimd.dma_start(out=lrw[:, :].rearrange("p (t c) -> p t c", c=1),
                        in_=lr_wT.rearrange("(t p) c -> p t c", p=128),
                        allow_slow_non_contiguous=True).inc(ld_wg)
    p.gpsimd.dma_start(out=fgw[:, :].rearrange("p (t c) -> p t c", c=1),
                        in_=fg_wT.rearrange("(t p) c -> p t c", p=128),
                        allow_slow_non_contiguous=True).inc(ld_wg)
    p.sync.dma_start(out=lrb[:, :], in_=lrfgb[:, :]).inc(ld_w)

    vch = p.sem("t_vch")
    sch = p.sem("t_sch")
    mm_b = p.sem("t_mmb")
    ld_x = p.sem("t_ldx")
    st_pre = p.sem("t_stpre")
    cc_pre = p.sem("t_ccpre", "cc")
    mm_f = p.sem("t_mmf")
    ev_zb = p.sem("t_evzb")
    ev_f = p.sem("t_evf")
    st_h = p.sem("t_sth")
    cc_h = p.sem("t_cch", "cc")
    ld_h = p.sem("t_ldh")
    ld_hf = [p.sem("t_ldhf0"), p.sem("t_ldhf1")]
    mm_p = p.sem("t_mmp")
    ev_pp = p.sem("t_evpp")
    ev_g = p.sem("t_evg")
    mm_t = p.sem("t_mmt")
    ev_t = p.sem("t_evt")
    st_g = p.sem("t_stg")
    cc_g = p.sem("t_ccg", "cc")
    ld_gt = p.sem("t_ldgt")
    ld_gf = [p.sem("t_ldgf0"), p.sem("t_ldgf1")]
    mm_w = p.sem("t_mmw")
    ev_w = p.sem("t_evw")
    mm_d = p.sem("t_mmd")
    ev_d = p.sem("t_evd")
    st_o = p.sem("t_sto")
    cc_o = p.sem("t_cco", "cc")
    ld_xn = p.sem("t_ldxn")
    st_y = p.sem("t_sty")

    with p.scope():
        x_span = p.sb("t_xspan", (128, 4 * D), F32)
        sq_scr = p.sb("t_sq", (128, D), BF16)
        stats = p.sb("t_stats", (128, 16), F32)
        spair = p.sb("t_spair", (128, 2), F32)
        xn_sc = p.sb("t_xnsc", (128, D), BF16)

        for v in range(4):
            if v > 0:
                p.sync.wait(vch)
            p.sync.dma_start(out=x_span[:, :].rearrange("p (t d) -> p t d", d=D),
                             in_=x_own[v * 512:(v + 1) * 512, :].rearrange(
                                 "(t p) d -> p t d", p=128)).inc(ld_x)
            p.vector.wait(ld_x)
            p.scalar.wait(ld_x)
            xt = [x_span[:, i * D:(i + 1) * D] for i in range(4)]
            s_nm, s_r = emit_ln_stats(p, xt, sq_scr, stats, 0,
                                      p.sem(f"tlnv{v}"), p.sem(f"tlns{v}"))
            for i in range(4):
                blk = v * 4 + i
                p.vector.wait(st_pre)
                vchain(p, vch, [
                    lambda i=i: p.vector.tensor_copy(spair[:, 0:1], s_nm[:, i:i + 1]),
                    lambda i=i: p.vector.tensor_copy(spair[:, 1:2], s_r[:, i:i + 1]),
                ])
                p.scalar.wait(vch)
                p.scalar.dma_start(out=cc["st_in"][blk * 128:(blk + 1) * 128, :],
                                   in_=spair[:, :]).inc(st_pre)
                p.vector.tensor_scalar(xn_sc[:, :], xt[i],
                                       s_nm[:, i:i + 1], s_r[:, i:i + 1],
                                       ALU.add, ALU.mult).inc(vch)
                p.scalar.wait(vch)
                p.scalar.dma_start(out=cc["xn_in"][blk * 128:(blk + 1) * 128, :],
                                   in_=xn_sc[:, :]).inc(st_pre)
        p.gpsimd.wait(st_pre)
        p.gpsimd.collective_compute("AllGather", ALU.bypass,
                                    replica_groups=[list(range(8))],
                                    ins=[cc["st_in"].opt()],
                                    outs=[cc["st_out"].opt()]).inc(cc_pre)
        p.gpsimd.collective_compute("AllGather", ALU.bypass,
                                    replica_groups=[list(range(8))],
                                    ins=[cc["xn_in"].opt()],
                                    outs=[cc["xn_out"].opt()]).inc(cc_pre)

    p.barrier()

    with p.scope():
        xn_t = p.sb("t_xnt", (128, 4 * D), BF16)
        xn_f = p.sb("t_xnf", (128, 2 * ND * 512), BF16)
        xnd = p.sb("t_xnd", (128, 2 * 4 * MSL), F32)
        strow = p.sb("t_strow", (128, 2 * 8), F32)
        zf = p.sb("t_zf", (128, 2 * 1024), F32)
        hms = p.sb("t_hms", (128, 2 * 1024), BF16)
        hTms = p.sb("t_hTms", (128, 4 * MSL), BF16)
        hfT = p.sb("t_hfT", (128, 4 * D), BF16)
        hfF = p.sb("t_hfF", (128, 2 * ND * 128), BF16)
        gTM = p.sb("t_gTM", (128, 4 * MSL), F32)
        gTMb = p.sb("t_gTMb", (128, 4 * MSL), BF16)
        gFMb = p.sb("t_gFMb", (128, 2 * 512), BF16)
        gfT = p.sb("t_gfT", (128, 4 * D), BF16)
        gfF = p.sb("t_gfF", (128, 2 * 4 * 512), BF16)
        dsil = p.sb("t_dsil", (128, 2 * 512), F32)
        dzF = p.sb("t_dzF", (128, 2 * 512), F32)
        dzFb = p.sb("t_dzFb", (128, 2 * 512), BF16)
        dzT = p.sb("t_dzT", (128, 4 * MSL), BF16)
        tmpu = p.sb("t_tmpu", (128, MSL), F32)
        b0scr = p.sb("t_b0scr", (128, 4), F32)
        outsp = p.sb("t_outsp", (128, MSL), F32)
        rowsc = p.sb("t_rowsc", (1, 512), F32)

        ps_z = p.psum("t_psz", (128, 2 * 512))
        ps_q = p.psum("t_psq", (128, 2 * 512))
        ps_t = p.psum("t_pst", (128, 2048), BF16)
        ps_r = p.psum("t_psr", (1, 512))
        ps_b = p.psum("t_psb", (128, 512))

        tposed = [0]

        def load_chunk(t, slot):
            p.sync.wait(mm_f)
            p.sync.wait(mm_w)
            for b in range(4):
                r = 2 * b + (1 if t >= 16 else 0)
                lt = t % 16
                p.sync.dma_start(
                    out=xn_t[:, b * D:(b + 1) * D],
                    in_=cc["xn_out"][(r * 16 + lt) * 128:(r * 16 + lt + 1) * 128, :]
                ).inc(ld_xn)
                p.sync.dma_start(
                    out=xnd[:, (slot * 4 + b) * MSL:(slot * 4 + b + 1) * MSL],
                    in_=x_dsl[t * CH + b * 128:t * CH + (b + 1) * 128, :]).inc(ld_xn)
                p.sync.dma_start(
                    out=strow[:, slot * 8 + b * 2:slot * 8 + b * 2 + 2],
                    in_=cc["st_out"][(r * 16 + lt) * 128:(r * 16 + lt + 1) * 128, :]
                ).inc(ld_xn)
            p.vector.wait(ld_xn)
            for b in range(4):
                p.vector.tensor_scalar(
                    xnd[:, (slot * 4 + b) * MSL:(slot * 4 + b + 1) * MSL],
                    xnd[:, (slot * 4 + b) * MSL:(slot * 4 + b + 1) * MSL],
                    strow[:, slot * 8 + b * 2:slot * 8 + b * 2 + 1],
                    strow[:, slot * 8 + b * 2 + 1:slot * 8 + b * 2 + 2],
                    ALU.add, ALU.mult).inc(vch)

        def transpose_chunk(slot):
            def srcf(i):
                tt, dt = divmod(i, ND)
                return xn_t[:, tt * D + dt * 128:tt * D + (dt + 1) * 128]

            def dstf(i):
                tt, dt = divmod(i, ND)
                return xn_f[:, slot * ND * 512 + dt * 512 + tt * 128:
                            slot * ND * 512 + dt * 512 + (tt + 1) * 128]
            emit_transpose_tiles(p, srcf, dstf, 64, ident[:, :], ps_t, mm_t, ev_t,
                                 free_base=tposed[0])
            tposed[0] += 8

        def bcast11(src11, dst_col):
            p.tensor.wait(vch)
            p.tensor.matmul(ps_b[:, 0:1], ones_r[0:1, :], src11,
                            start=True, stop=True).inc(mm_b)
            p.vector.wait(mm_b)
            p.vector.tensor_copy(dst_col, ps_b[:, 0:1]).inc(vch)
            p.vector.wait(vch)

        ev_wp = p.sem("t_evwp")

        def wmaster_update(master, nmt, mm_fn):
            p.tensor.wait(ev_pp)
            for mt in range(nmt):
                pslot = mt % 2
                if ev_wp.n >= 1:
                    p.tensor.wait_abs(ev_wp, ev_wp.n - 1)
                mm_fn(mt, pslot)
                p.vector.wait(mm_w)
                p.vector.wait(ev_w)
                p.vector.tensor_scalar(tmpu[:, :], ps_q[:, pslot * 512:pslot * 512 + MSL],
                                       scal[:, 1:2], None, ALU.mult).inc(ev_wp)
                p.vector.wait(ev_wp)
                p.vector.scalar_tensor_tensor(master[:, mt * MSL:(mt + 1) * MSL],
                                              master[:, mt * MSL:(mt + 1) * MSL],
                                              scal[:, 0:1], tmpu[:, :],
                                              ALU.mult, ALU.add).inc(ev_w)

        p.sync.wait(cc_pre)
        load_chunk(0, 0)
        p.tensor.wait(ld_xn)
        p.tensor.wait(ld_c)
        transpose_chunk(0)

        p.vector.memset(hms[:, :], 0.0).inc(vch)
        p.tensor.wait(ld_w)
        p.tensor.wait(ld_wg)
        p.tensor.matmul(ps_b[:, 0:MSL], ones_r[0:1, :], b1r[0:1, :],
                        start=True, stop=True).inc(mm_b)
        p.vector.wait(mm_b)
        p.vector.tensor_copy(b1b[:, :], ps_b[:, 0:MSL]).inc(vch)
        p.vector.wait(ld_w)
        p.vector.tensor_copy(W0Tb[:, :], W0T[:, :]).inc(vch)
        p.vector.tensor_copy(W1RTb[:, :], W1RT[:, :]).inc(vch)

        for t in range(NCH + 1):
            cur = t % 2
            prv = 1 - cur
            has_left = t >= 1
            has_right = t < NCH
            nhalf = (1 if has_left else 0) + (1 if has_right else 0)
            loff = 0
            roff = 512 if has_left else 0

            p.tensor.wait(vch)
            groups = []
            if has_left:
                groups += [(0, prv, loff)]
            if has_right:
                groups += [(1, cur, roff)]
            gj = 0
            for mt in range(2):
                for half, slot, off in groups:
                    pslot = gj % 2
                    if ev_zb.n >= 1:
                        p.tensor.wait_abs(ev_zb, ev_zb.n - 1)
                    for k in range(ND):
                        p.tensor.matmul(
                            ps_z[:, pslot * 512:(pslot + 1) * 512],
                            W0Tb[:, k * MSL + mt * 128:k * MSL + (mt + 1) * 128],
                            xn_f[:, slot * ND * 512 + k * 512:slot * ND * 512 + (k + 1) * 512],
                            start=(k == 0), stop=(k == ND - 1)).inc(mm_f)
                    p.vector.wait(mm_f)
                    p.vector.wait(sch)
                    p.vector.tensor_scalar(
                        zf[:, mt * 1024 + off:mt * 1024 + off + 512],
                        ps_z[:, pslot * 512:(pslot + 1) * 512],
                        b0c[:, mt:mt + 1], None, ALU.add).inc(ev_zb)
                    p.scalar.wait(ev_zb)
                    p.scalar.wait(st_h)
                    p.scalar.activation(
                        hms[:, mt * 1024 + off:mt * 1024 + off + 512],
                        zf[:, mt * 1024 + off:mt * 1024 + off + 512],
                        AF.Silu).inc(ev_f)
                    gj += 1
            p.scalar.wait(ev_f)
            p.scalar.dma_start(
                out=cc["h_in"][:, :].rearrange("(m p) c -> p m c", p=128),
                in_=hms[:, :].rearrange("p (m c) -> p m c", c=1024)).inc(st_h)
            p.gpsimd.wait(st_h)
            p.gpsimd.collective_compute("AllGather", ALU.bypass,
                                        replica_groups=[list(range(8))],
                                        ins=[cc["h_in"].opt()],
                                        outs=[cc["h_out"].opt()]).inc(cc_h)
            if has_right:
                p.tensor.wait(ev_f)

                def hsrc(i):
                    mt, tt = divmod(i, 4)
                    return hms[:, mt * 1024 + roff + tt * 128:mt * 1024 + roff + (tt + 1) * 128]

                def hdst(i):
                    mt, tt = divmod(i, 4)
                    return hTms[:, tt * MSL + mt * 128:tt * MSL + (mt + 1) * 128]
                emit_transpose_tiles(p, hsrc, hdst, 8, ident[:, :], ps_t, mm_t, ev_t,
                                     free_base=tposed[0])
                tposed[0] += 1
            p.sync.wait(cc_h)
            if has_right:
                p.sync.wait(mm_w)
                for tt in range(4):
                    p.sync.dma_start(
                        out=hfT[:, tt * D:(tt + 1) * D],
                        in_=cc["h_out"].rearrange("r m c -> (r m) c")[
                            :, roff + tt * 128:roff + (tt + 1) * 128],
                        transpose=True).inc(ld_h)
            p.tensor.wait(ev_wp)
            hview = cc["h_out"].rearrange("r m c -> (r m) c")
            pt_i = 0
            for half, off, slot in ((0, loff, prv), (1, roff, cur)):
                if (half == 0 and not has_left) or (half == 1 and not has_right):
                    continue
                for tt in range(4):
                    pslot = pt_i % 2
                    fslot = pt_i % 2
                    if mm_p.n >= ND:
                        p.sync.wait_abs(mm_p, mm_p.n - ND)
                    p.sync.dma_start(
                        out=hfF[:, fslot * ND * 128:(fslot + 1) * ND * 128].rearrange(
                            "p (k c) -> p k c", c=128),
                        in_=hview.rearrange("(k p) c -> p k c", p=128)[
                            :, :, off + tt * 128:off + (tt + 1) * 128]).inc(ld_hf[fslot])
                    p.tensor.wait(ld_hf[fslot])
                    if ev_pp.n >= 1:
                        p.tensor.wait_abs(ev_pp, ev_pp.n - 1)
                    for k in range(ND):
                        p.tensor.matmul(
                            ps_q[:, pslot * 512:pslot * 512 + MSL],
                            hfF[:, (fslot * ND + k) * 128:(fslot * ND + k + 1) * 128],
                            W1RTb[:, k * MSL:(k + 1) * MSL],
                            start=(k == 0), stop=(k == ND - 1)).inc(mm_p)
                    p.vector.wait(mm_p)
                    if half == 0:
                        p.vector.wait(st_o)
                        p.vector.tensor_tensor(outsp[:, :],
                                               ps_q[:, pslot * 512:pslot * 512 + MSL],
                                               b1b[:, :], ALU.add).inc(ev_pp)
                        p.scalar.wait(ev_pp)
                        p.scalar.dma_start(
                            out=out_d[(t - 1) * CH + tt * 128:(t - 1) * CH + (tt + 1) * 128, :],
                            in_=outsp[:, :]).inc(st_o)
                    else:
                        p.vector.wait(mm_w)
                        p.vector.wait(mm_b)
                        p.vector.tensor_tensor(gTM[:, tt * MSL:(tt + 1) * MSL],
                                               ps_q[:, pslot * 512:pslot * 512 + MSL],
                                               b1b[:, :], ALU.add).inc(ev_pp)
                        p.vector.wait(vch)
                        p.vector.wait(ev_pp)
                        p.vector.tensor_tensor(gTM[:, tt * MSL:(tt + 1) * MSL],
                                               gTM[:, tt * MSL:(tt + 1) * MSL],
                                               xnd[:, (cur * 4 + tt) * MSL:(cur * 4 + tt + 1) * MSL],
                                               ALU.subtract).inc(ev_g)
                        p.vector.wait(ev_g)
                        p.vector.tensor_scalar(gTM[:, tt * MSL:(tt + 1) * MSL],
                                               gTM[:, tt * MSL:(tt + 1) * MSL],
                                               2.0 / (CH * D), None, ALU.mult).inc(ev_g)
                        p.vector.wait(ev_g)
                        p.vector.tensor_copy(gTMb[:, tt * MSL:(tt + 1) * MSL],
                                             gTM[:, tt * MSL:(tt + 1) * MSL]).inc(ev_g)
                    pt_i += 1
            if not has_right:
                break

            p.vector.wait(st_g)
            p.tensor.wait(ev_g)

            def gsrc(i):
                tt, dt = divmod(i, 2)
                return gTMb[:, tt * MSL + dt * 128:tt * MSL + (dt + 1) * 128]

            def gdst(i):
                tt, dt = divmod(i, 2)
                return gFMb[:, dt * 512 + tt * 128:dt * 512 + (tt + 1) * 128]
            emit_transpose_tiles(p, gsrc, gdst, 8, ident[:, :], ps_t, mm_t, ev_t,
                                 free_base=tposed[0])
            tposed[0] += 1
            p.scalar.wait(ev_t)
            p.scalar.dma_start(
                out=cc["g_in"][:, :].rearrange("(a p) c -> p a c", p=128),
                in_=gFMb[:, :].rearrange("p (a c) -> p a c", c=512)).inc(st_g)
            p.gpsimd.wait(st_g)
            p.gpsimd.collective_compute("AllGather", ALU.bypass,
                                        replica_groups=[list(range(8))],
                                        ins=[cc["g_in"].opt()],
                                        outs=[cc["g_out"].opt()]).inc(cc_g)

            p.tensor.wait(ev_zb)
            for k in range(ND):
                p.tensor.matmul(ps_r[0:1, :], lrw[:, k:k + 1],
                                xn_f[:, cur * ND * 512 + k * 512:cur * ND * 512 + (k + 1) * 512],
                                start=(k == 0), stop=(k == ND - 1)).inc(mm_b)
            p.scalar.wait(mm_b)
            p.scalar.activation(rowsc[0:1, :], ps_r[0:1, :], AF.Sigmoid,
                                bias=lrb[0:1, 0:1], scale=1.0).inc(sch)
            p.vector.wait(sch)
            vchain(p, vch, [
                lambda: p.vector.tensor_scalar(rowsc[0:1, :], rowsc[0:1, :], 1.0, 0.0,
                                               ALU.mult, ALU.add,
                                               accum_out=stat_t[0:1, 0:1]),
                lambda: p.vector.tensor_scalar(stat_t[0:1, 1:2], stat_t[0:1, 0:1],
                                               -LRM / CH, None, ALU.mult),
            ])
            p.tensor.wait(sch)
            p.tensor.wait(vch)
            for k in range(ND):
                p.tensor.matmul(ps_r[0:1, :], fgw[:, k:k + 1],
                                xn_f[:, cur * ND * 512 + k * 512:cur * ND * 512 + (k + 1) * 512],
                                start=(k == 0), stop=(k == ND - 1)).inc(mm_b)
            p.vector.wait(mm_b)
            vchain(p, vch, [
                lambda: p.vector.tensor_copy(rowsc[0:1, :], ps_r[0:1, :]),
                lambda: p.vector.tensor_reduce(stat_t[0:1, 4:5], rowsc[0:1, 0:128], AX.X, ALU.add),
                lambda: p.vector.tensor_reduce(stat_t[0:1, 5:6], rowsc[0:1, 128:256], AX.X, ALU.add),
                lambda: p.vector.tensor_reduce(stat_t[0:1, 6:7], rowsc[0:1, 256:384], AX.X, ALU.add),
                lambda: p.vector.tensor_reduce(stat_t[0:1, 7:8], rowsc[0:1, 384:512], AX.X, ALU.add),
                lambda: p.vector.tensor_scalar(stat_t[0:1, 4:8], stat_t[0:1, 4:8],
                                               1.0 / 128, None, ALU.mult),
            ])
            p.scalar.wait(vch)
            p.scalar.activation(stat_t[0:1, 4:8], stat_t[0:1, 4:8], AF.Sigmoid,
                                bias=lrb[0:1, 1:2], scale=1.0).inc(sch)
            p.vector.wait(sch)
            vchain(p, vch, [
                lambda: p.vector.tensor_reduce(stat_t[0:1, 3:4], stat_t[0:1, 4:8], AX.X, ALU.add),
                lambda: p.vector.tensor_scalar(stat_t[0:1, 3:4], stat_t[0:1, 3:4],
                                               0.25, None, ALU.mult),
            ])
            bcast11(stat_t[0:1, 3:4], scal[:, 0:1])
            bcast11(stat_t[0:1, 1:2], scal[:, 1:2])

            p.tensor.wait(ld_h)

            def mm_w1rt(mt, pslot):
                for tt in range(4):
                    p.tensor.matmul(ps_q[:, pslot * 512:pslot * 512 + MSL],
                                    hfT[:, tt * D + mt * 128:tt * D + (mt + 1) * 128],
                                    gTMb[:, tt * MSL:(tt + 1) * MSL],
                                    start=(tt == 0), stop=(tt == 3)).inc(mm_w)
            wmaster_update(W1RT, ND, mm_w1rt)
            p.vector.wait(ev_w)
            p.vector.wait(mm_p)
            p.vector.tensor_copy(W1RTb[:, :], W1RT[:, :]).inc(vch)

            p.sync.wait(cc_g)
            p.sync.wait(mm_w)
            gview = cc["g_out"]
            gv3 = gview.rearrange("(a p) c -> p a c", p=128)
            for tt in range(4):
                p.sync.dma_start(out=gfT[:, tt * D:(tt + 1) * D],
                                 in_=gview[:, tt * 128:(tt + 1) * 128],
                                 transpose=True).inc(ld_gt)

            p.tensor.wait(ev_zb)
            for grp in range(4):
                gslot = grp % 2
                if grp == 0:
                    p.sync.wait(mm_d)
                else:
                    p.sync.wait_abs(mm_d, mm_d.n - 4)
                p.sync.dma_start(
                    out=gfF[:, gslot * 4 * 512:(gslot + 1) * 4 * 512].rearrange(
                        "p (a c) -> p a c", c=512),
                    in_=gv3[:, grp * 4:(grp + 1) * 4, :]).inc(ld_gf[gslot])
                p.tensor.wait(ld_gf[gslot])
                for mt in range(2):
                    for kk in range(4):
                        k = grp * 4 + kk
                        p.tensor.matmul(ps_z[:, mt * 512:(mt + 1) * 512],
                                        W1Cb[:, k * MSL + mt * 128:k * MSL + (mt + 1) * 128],
                                        gfF[:, (gslot * 4 + kk) * 512:(gslot * 4 + kk + 1) * 512],
                                        start=(k == 0), stop=(k == ND - 1)).inc(mm_d)
            p.scalar.wait(ev_d)
            p.scalar.wait(ev_zb)
            for mt in range(2):
                p.scalar.activation(dsil[:, mt * 512:(mt + 1) * 512],
                                    zf[:, mt * 1024 + roff:mt * 1024 + roff + 512],
                                    AF.Sigmoid).inc(sch)
            p.vector.wait(sch)
            p.vector.wait(mm_w)
            for mt in range(2):
                p.vector.tensor_tensor(dzF[:, mt * 512:(mt + 1) * 512],
                                       zf[:, mt * 1024 + roff:mt * 1024 + roff + 512],
                                       dsil[:, mt * 512:(mt + 1) * 512],
                                       ALU.mult).inc(ev_d)
            p.vector.wait(ev_d)
            for mt in range(2):
                p.vector.tensor_scalar(dzFb[:, mt * 512:(mt + 1) * 512],
                                       dsil[:, mt * 512:(mt + 1) * 512],
                                       -1.0, 1.0, ALU.mult, ALU.add).inc(ev_d)
            p.vector.wait(ev_d)
            for mt in range(2):
                p.vector.tensor_tensor(dzF[:, mt * 512:(mt + 1) * 512],
                                       dzF[:, mt * 512:(mt + 1) * 512],
                                       dzFb[:, mt * 512:(mt + 1) * 512],
                                       ALU.mult).inc(ev_d)
            p.vector.wait(ev_d)
            for mt in range(2):
                p.vector.tensor_tensor(dsil[:, mt * 512:(mt + 1) * 512],
                                       dsil[:, mt * 512:(mt + 1) * 512],
                                       dzF[:, mt * 512:(mt + 1) * 512],
                                       ALU.add).inc(ev_d)
            p.vector.wait(mm_d)
            p.vector.wait(ev_d)
            for mt in range(2):
                p.vector.tensor_tensor(dzF[:, mt * 512:(mt + 1) * 512],
                                       ps_z[:, mt * 512:(mt + 1) * 512],
                                       dsil[:, mt * 512:(mt + 1) * 512],
                                       ALU.mult).inc(ev_d)
            p.vector.wait(ev_d)
            p.vector.tensor_copy(dzFb[:, :], dzF[:, :]).inc(ev_d)
            vchain(p, vch, [
                lambda: p.vector.tensor_reduce(b0scr[:, 0:1], dzF[:, 0:512], AX.X, ALU.add),
                lambda: p.vector.tensor_reduce(b0scr[:, 1:2], dzF[:, 512:1024], AX.X, ALU.add),
                lambda: p.vector.tensor_scalar(b0scr[:, 2:4], b0c[:, 0:2],
                                               scal[:, 0:1], None, ALU.mult),
                lambda: p.vector.scalar_tensor_tensor(b0c[:, 0:2], b0scr[:, 0:2],
                                                      scal[:, 1:2], b0scr[:, 2:4],
                                                      ALU.mult, ALU.add),
            ])
            p.tensor.wait(ev_d)

            def zsrc(i):
                mt, tt = divmod(i, 4)
                return dzFb[:, mt * 512 + tt * 128:mt * 512 + (tt + 1) * 128]

            def zdst(i):
                mt, tt = divmod(i, 4)
                return dzT[:, tt * MSL + mt * 128:tt * MSL + (mt + 1) * 128]
            emit_transpose_tiles(p, zsrc, zdst, 8, ident[:, :], ps_t, mm_t, ev_t,
                                 free_base=tposed[0])
            tposed[0] += 1

            p.tensor.wait(vch)
            p.tensor.wait(ev_g)
            for tt in range(4):
                p.tensor.matmul(ps_r[0:1, 0:MSL], ones_cb[:, 0:1],
                                gTMb[:, tt * MSL:(tt + 1) * MSL],
                                start=(tt == 0), stop=(tt == 3)).inc(mm_b)
            p.vector.wait(mm_b)
            vchain(p, vch, [
                lambda: p.vector.tensor_scalar(rowsc[0:1, 0:MSL], b1r[0:1, :],
                                               stat_t[0:1, 3:4], None, ALU.mult),
                lambda: p.vector.scalar_tensor_tensor(b1r[0:1, :], ps_r[0:1, 0:MSL],
                                                      stat_t[0:1, 1:2],
                                                      rowsc[0:1, 0:MSL],
                                                      ALU.mult, ALU.add),
            ])
            p.tensor.wait(vch)
            p.tensor.matmul(ps_b[:, 0:MSL], ones_r[0:1, :], b1r[0:1, :],
                            start=True, stop=True).inc(mm_b)
            p.vector.wait(mm_b)
            p.vector.wait(ev_pp)
            p.vector.tensor_copy(b1b[:, :], ps_b[:, 0:MSL]).inc(vch)

            p.tensor.wait(ev_t)

            def mm_w0t(dt, pslot):
                for tt in range(4):
                    p.tensor.matmul(ps_q[:, pslot * 512:pslot * 512 + MSL],
                                    xn_t[:, tt * D + dt * 128:tt * D + (dt + 1) * 128],
                                    dzT[:, tt * MSL:(tt + 1) * MSL],
                                    start=(tt == 0), stop=(tt == 3)).inc(mm_w)
            wmaster_update(W0T, ND, mm_w0t)
            p.vector.wait(ev_w)
            p.vector.wait(mm_f)
            p.vector.tensor_copy(W0Tb[:, :], W0T[:, :]).inc(vch)

            if t + 1 < NCH:
                load_chunk(t + 1, prv)
                p.tensor.wait(ld_xn)
                transpose_chunk(prv)

            p.tensor.wait(ld_gt)

            def mm_w1c(dt, pslot):
                for tt in range(4):
                    p.tensor.matmul(ps_q[:, pslot * 512:pslot * 512 + MSL],
                                    gfT[:, tt * D + dt * 128:tt * D + (dt + 1) * 128],
                                    hTms[:, tt * MSL:(tt + 1) * MSL],
                                    start=(tt == 0), stop=(tt == 3)).inc(mm_w)
            p.tensor.wait(mm_d)
            wmaster_update(W1Cb, ND, mm_w1c)

        p.gpsimd.wait(st_o)
        for g4 in range(4):
            p.gpsimd.collective_compute(
                "AllGather", ALU.bypass,
                replica_groups=[list(range(8))],
                ins=[out_d[g4 * 8 * CH:(g4 + 1) * 8 * CH, :].opt()],
                outs=[cc["o_out"][g4:g4 + 1].opt()]).inc(cc_o)

    p.barrier()
    with p.scope():
        x1a = p.sb("t_x1a", (128, 2 * D), F32)
        oget = p.sb("t_oget", (128, 2 * D), F32)
        oidx_s = p.sb("t_oidx", (128, NT * NCORES), mybir.dt.int32)
        g_o = p.sem("t_go")
        ld_i2 = p.sem("t_ldi2")
        p.sync.dma_start(out=oidx_s[:, :], in_=oidx[:, :]).inc(ld_i2)
        p.gpsimd.wait(ld_i2)
        p.gpsimd.wait(cc_o)
        oview = cc["o_out"].rearrange("a r t c -> (a r t) c")
        x1v = p.sem("t_x1v")
        ld_xa = [p.sem("t_ldxa0"), p.sem("t_ldxa1")]
        g_o2 = [p.sem("t_go0", "sw"), p.sem("t_go1", "sw")]
        st_y2 = [p.sem("t_sty0"), p.sem("t_sty1")]
        for L in range(NT):
            slot = L % 2
            if L >= 2:
                p.sync.wait(x1v, L - 1)
                p.sync.wait(st_y2[slot])
                p.gpsimd.wait(x1v, L - 1)
                p.gpsimd.wait(st_y2[slot])
            p.sync.dma_start(out=x1a[:, slot * D:(slot + 1) * D],
                             in_=x_own[L * 128:(L + 1) * 128, :]).inc(ld_xa[slot])
            for r in range(NCORES):
                p.gpsimd.indirect_dma_start(
                    oget[:, slot * D + r * MSL:slot * D + (r + 1) * MSL], None,
                    oview,
                    bass.IndirectOffsetOnAxis(oidx_s[:, L * NCORES + r:L * NCORES + r + 1], 0)
                ).inc(g_o2[slot])
            p.vector.wait(ld_xa[slot])
            p.vector.wait(g_o2[slot])
            p.vector.tensor_tensor(x1a[:, slot * D:(slot + 1) * D],
                                   x1a[:, slot * D:(slot + 1) * D],
                                   oget[:, slot * D:(slot + 1) * D],
                                   ALU.add).inc(x1v)
            p.scalar.wait(x1v)
            p.scalar.dma_start(out=y[L * 128:(L + 1) * 128, :],
                               in_=x1a[:, slot * D:(slot + 1) * D]).inc(st_y2[slot])
        p.scalar.wait(st_y2[0])
        p.scalar.wait(st_y2[1])
def build_l0_attn():
    nc = bass.Bass("TRN2", target_bir_lowering=False, debug=False)
    x_own = nc.dram_tensor("x_own", [TOK, D], F32, kind="ExternalInput").ap()
    x_dsl = nc.dram_tensor("x_dsl", [NCH * CH, MSL], F32, kind="ExternalInput").ap()
    W0T_s = nc.dram_tensor("W0T_s", [D, MSL], F32, kind="ExternalInput").ap()
    W1RT_s = nc.dram_tensor("W1RT_s", [D, MSL], F32, kind="ExternalInput").ap()
    W1C_s = nc.dram_tensor("W1C_s", [D, MSL], F32, kind="ExternalInput").ap()
    b0_2 = nc.dram_tensor("b0_2", [128, 2], F32, kind="ExternalInput").ap()
    b1row = nc.dram_tensor("b1row", [1, MSL], F32, kind="ExternalInput").ap()
    lr_wT = nc.dram_tensor("lr_wT", [D, 1], F32, kind="ExternalInput").ap()
    fg_wT = nc.dram_tensor("fg_wT", [D, 1], F32, kind="ExternalInput").ap()
    lrfgb = nc.dram_tensor("lrfgb", [1, 2], F32, kind="ExternalInput").ap()
    oidx = nc.dram_tensor("oidx", [128, NT * NCORES], mybir.dt.int32,
                          kind="ExternalInput").ap()
    ident_i = nc.dram_tensor("ident", [128, 128], BF16, kind="ExternalInput").ap()
    ones_colb_i = nc.dram_tensor("ones_colb", [128, 1], BF16, kind="ExternalInput").ap()
    ones_row_i = nc.dram_tensor("ones_row", [1, 128], F32, kind="ExternalInput").ap()
    y = nc.dram_tensor("y", [TOK, D], F32, kind="ExternalOutput").ap()
    cc = {
        "st_in": nc.dram_tensor("cc_st_in", [TOK, 2], F32).ap(),
        "st_out": nc.dram_tensor("cc_st_out", [NCORES * TOK, 2], F32).ap(),
        "xn_in": nc.dram_tensor("cc_xn_in", [TOK, D], BF16).ap(),
        "xn_out": nc.dram_tensor("cc_xn_out", [NCORES * TOK, D], BF16).ap(),
        "h_in": nc.dram_tensor("cc_h_in", [2 * 128, 1024], BF16).ap(),
        "h_out": nc.dram_tensor("cc_h_out", [NCORES, 2 * 128, 1024], BF16).ap(),
        "g_in": nc.dram_tensor("cc_g_in", [2 * 128, 512], BF16).ap(),
        "g_out": nc.dram_tensor("cc_g_out", [NCORES * 2 * 128, 512], BF16).ap(),
        "o_out": nc.dram_tensor("cc_o_out", [4, NCORES, 8 * CH, MSL], F32).ap(),
    }
    out_d = nc.dram_tensor("out_d", [NCH * CH, MSL], F32).ap()
    with ExitStack() as ctx:
        p = Prog(nc, ctx)
        emit_l0_attn(p, x_own, x_dsl, W0T_s, W1RT_s, W1C_s, b0_2, b1row, lr_wT,
                     fg_wT, lrfgb, oidx, ident_i, ones_colb_i, ones_row_i, y, cc, out_d)
        p.finalize()
    return nc


def mk_xdsl(x_full, s):
    xr = x_full.reshape(B, NCH, 128, D).transpose(1, 0, 2, 3).reshape(NCH * CH, D)
    return np.ascontiguousarray(xr[:, s * MSL:(s + 1) * MSL])


def mk_oidx(b, half):
    idx = np.empty((128, NT * NCORES), dtype=np.int32)
    for L in range(NT):
        c = 16 * half + L
        for r in range(NCORES):
            base = ((c // 8) * NCORES + r) * (8 * CH) + (c % 8) * CH + b * 128
            idx[:, L * NCORES + r] = base + np.arange(128)
    return idx


def run_l0_attn(x_full, W0, b0, W1, b1, lr_w, lr_b, fg_w, fg_b):
    c = consts()
    ca = attn_consts()
    nc = build_l0_attn()
    W0T = np.ascontiguousarray(W0.T)
    W1T = np.ascontiguousarray(W1.T)
    shards = shard_rows(x_full)
    in_maps = []
    for s in range(NCORES):
        b, half = divmod(s, 2)
        msl = slice(s * MSL, (s + 1) * MSL)
        in_maps.append(dict(
            x_own=shards[s],
            x_dsl=mk_xdsl(x_full, s),
            W0T_s=np.ascontiguousarray(W0T[:, msl]),
            W1RT_s=np.ascontiguousarray(W1T[:, msl]),
            W1C_s=np.ascontiguousarray(W1[:, msl]),
            b0_2=np.ascontiguousarray(b0[msl].reshape(2, 128).T),
            b1row=np.ascontiguousarray(b1[msl].reshape(1, MSL)),
            lr_wT=np.ascontiguousarray(lr_w.reshape(D, 1)),
            fg_wT=np.ascontiguousarray(fg_w.reshape(D, 1)),
            lrfgb=np.array([[float(np.asarray(lr_b).reshape(-1)[0]),
                             float(np.asarray(fg_b).reshape(-1)[0])]], dtype=np.float32),
            oidx=mk_oidx(b, half),
            ident=c["ident"], ones_row=c["ones_row"],
            ones_colb=ca["ones_colb"]))
    res = run_bass_kernel_spmd(nc, in_maps, core_ids=list(range(NCORES)))
    return unshard_rows([res.results[s]["y"] for s in range(NCORES)])


def mk_xnfidx(b):
    idx = np.empty((128, 32), dtype=np.int32)
    for g in range(32):
        r = 2 * b + (1 if g >= 16 else 0)
        lt = g % 16
        idx[:, g] = (r * 16 + lt) * 128 + np.arange(128)
    return idx


def build_fused():
    nc = bass.Bass("TRN2", target_bir_lowering=False, debug=False)
    EI = dict(kind="ExternalInput")
    ap = lambda name, shape, dt=F32, **kw: nc.dram_tensor(name, list(shape), dt, **kw).ap()
    ident_i = ap("ident", (128, 128), BF16, **EI)
    ones_colb_i = ap("ones_colb", (128, 1), BF16, **EI)
    ones_col_i = ap("ones_col", (128, 1), F32, **EI)
    ones_row_i = ap("ones_row", (1, 128), F32, **EI)
    x_own = ap("x_own", (TOK, D), **EI)
    x_dsl = ap("x_dsl", (NCH * CH, MSL), **EI)
    W0T_s = ap("W0T_s", (D, MSL), **EI)
    W1RT_s = ap("W1RT_s", (D, MSL), **EI)
    W1C_s = ap("W1C_s", (D, MSL), **EI)
    b0_2 = ap("b0_2", (128, 2), **EI)
    b1row = ap("b1row", (1, MSL), **EI)
    lr_wT = ap("lr_wT", (D, 1), **EI)
    fg_wT = ap("fg_wT", (D, 1), **EI)
    lrfgb = ap("lrfgb", (1, 2), **EI)
    oidx = ap("oidx", (128, NT * NCORES), mybir.dt.int32, **EI)
    gate_wT = ap("gate_wT", (D, HG), **EI)
    up_wT = ap("up_wT", (D, HG), **EI)
    down_wT = ap("down_wT", (HG, D), **EI)
    gate_s2 = ap("gate_s2", (128, NF_G), **EI)
    up_s2 = ap("up_s2", (128, NF_G), **EI)
    down_s_row = ap("down_s_row", (1, D), **EI)
    WqTl = ap("WqTl", (D, NHL * 128), **EI)
    WkTl = ap("WkTl", (D, NHL * 128), **EI)
    WvTl = ap("WvTl", (D, NHL * 128), **EI)
    WoT = ap("WoT", (D, D), **EI)
    bq2 = ap("bq2", (128, NHL), **EI)
    bk2 = ap("bk2", (128, NHL), **EI)
    bv_row = ap("bv_row", (1, NHL * 128), **EI)
    bo_row = ap("bo_row", (1, D), **EI)
    slopes_row = ap("slopes_row", (1, NHL), **EI)
    kpos = ap("kpos", (128, S // 128), **EI)
    qpos_row = ap("qpos_row", (1, S), **EI)
    ctxidx = ap("ctxidx", (128, NH), mybir.dt.int32, **EI)
    xnfidx = ap("xnfidx", (128, 32), mybir.dt.int32, **EI)
    masktri_i = ap("masktri", (128, 128), BF16, **EI)
    w_fcT = ap("w_fcT", (D, HFC), **EI)
    w_projT = ap("w_projT", (HFC, D), **EI)
    y_out = ap("y", (TOK, D), F32, kind="ExternalOutput")
    x1_d = ap("x1_d", (TOK, D))
    x2_d = ap("x2_d", (TOK, D))
    x3_d = ap("x3_d", (TOK, D))
    SH = dict(addr_space="Shared")
    cc = {
        "st_in": ap("cc_st_in", (TOK, 2)),
        "st_out": ap("cc_st_out", (NCORES * TOK, 2), **SH),
        "xn_in": ap("cc_xn_in", (TOK, D), BF16),
        "xn_out": ap("cc_xn_out", (NCORES * TOK, D), BF16, **SH),
        "h_in": ap("cc_h_in", (2 * 128, 1024), BF16),
        "h_out": ap("cc_h_out", (NCORES, 2 * 128, 1024), BF16, **SH),
        "g_in": ap("cc_g_in", (2 * 128, 512), BF16),
        "g_out": ap("cc_g_out", (NCORES * 2 * 128, 512), BF16, **SH),
        "o_out": ap("cc_o_out", (4, NCORES, 8 * CH, MSL), **SH),
    }
    out_d = ap("out_d", (NCH * CH, MSL))
    twfc_d = nc.dram_tensor("twfc_d", [NF_FC, ND, 128, 128], BF16).ap()
    twpj_d = nc.dram_tensor("twpj_d", [4, NF_FC, 128, 512], BF16).ap()
    xnfm_d = ap("xnfm_d", (D, S), BF16)
    ctx_d = nc.dram_tensor("ctx_d", [NHL, 128, S], BF16)
    cc_att = nc.dram_tensor("cc_att_out", [4, NCORES, 2, 128, S], BF16,
                            addr_space="Shared")
    ctxmy_d = ap("ctxmy_d", (NH, 128, TOK), BF16)
    cc_xnf_in = ap("cc_xnf_in", (TOK, D), BF16)
    cc_xnf_out = ap("cc_xnf_out", (NCORES * TOK, D), BF16, **SH)

    with ExitStack() as ctx:
        p = Prog(nc, ctx)
        with p.scope():
            emit_l0_attn(p, x_own, x_dsl, W0T_s, W1RT_s, W1C_s, b0_2, b1row,
                         lr_wT, fg_wT, lrfgb, oidx, ident_i, ones_colb_i,
                         ones_row_i, x1_d, cc, out_d)
        p.barrier()
        with p.scope():
            emit_l0_ffn(p, x1_d, gate_wT, up_wT, down_wT, gate_s2, up_s2,
                        down_s_row, ident_i, ones_col_i, ones_row_i, x2_d)
        p.barrier()
        with p.scope():
            emit_l1_attn(p, None, x2_d, WqTl, WkTl, WvTl, WoT, bq2, bk2,
                         bv_row, bo_row, slopes_row, kpos, qpos_row, ctxidx,
                         masktri_i, ident_i, ones_colb_i, ones_row_i, x3_d,
                         xnfm_d, ctx_d, cc_att, ctxmy_d,
                         fused=True, cc_xnf_in=cc_xnf_in, cc_xnf_out=cc_xnf_out,
                         xnfidx=xnfidx)
        p.barrier()
        with p.scope():
            emit_l1_ffn(p, x3_d, w_fcT, w_projT, twfc_d, twpj_d, ident_i,
                        ones_col_i, ones_row_i, y_out)
        p.finalize()
    return nc


_FUSED_NC = None


def kernel(L0_attn_x, L0_attn_W0, L0_attn_b0, L0_attn_W1, L0_attn_b1,
           L0_attn_lr_w, L0_attn_lr_b, L0_attn_fg_w, L0_attn_fg_b,
           L0_ffn_gate_w, L0_ffn_up_w, L0_ffn_down_w, L0_ffn_gate_s,
           L0_ffn_up_s, L0_ffn_down_s, L1_attn_Wq, L1_attn_bq, L1_attn_Wk,
           L1_attn_bk, L1_attn_Wv, L1_attn_bv, L1_attn_Wo, L1_attn_bo,
           L1_attn_alibi_slopes, L1_ffn_w_fc, L1_ffn_w_proj):
    global _FUSED_NC
    a = lambda v: np.ascontiguousarray(np.asarray(v, dtype=np.float32))
    x = a(L0_attn_x)
    W0, b0 = a(L0_attn_W0), a(L0_attn_b0)
    W1, b1 = a(L0_attn_W1), a(L0_attn_b1)
    lr_w, fg_w = a(L0_attn_lr_w), a(L0_attn_fg_w)
    Wq, Wk, Wv, Wo = a(L1_attn_Wq), a(L1_attn_Wk), a(L1_attn_Wv), a(L1_attn_Wo)
    bq, bk, bv, bo = a(L1_attn_bq), a(L1_attn_bk), a(L1_attn_bv), a(L1_attn_bo)
    slopes = a(L1_attn_alibi_slopes)
    c = consts()
    ca = attn_consts()
    if _FUSED_NC is None:
        _FUSED_NC = build_fused()
    nc = _FUSED_NC
    W0T = np.ascontiguousarray(W0.T)
    W1T = np.ascontiguousarray(W1.T)
    shards = shard_rows(x)
    shared = dict(
        ident=c["ident"], ones_row=c["ones_row"], ones_colb=ca["ones_colb"],
        ones_col=c["ones_col"],
        masktri=ca["masktri"], kpos=ca["kpos"], qpos_row=ca["qpos_row"],
        gate_wT=np.ascontiguousarray(a(L0_ffn_gate_w).T),
        up_wT=np.ascontiguousarray(a(L0_ffn_up_w).T),
        down_wT=np.ascontiguousarray(a(L0_ffn_down_w).T),
        gate_s2=np.ascontiguousarray(a(L0_ffn_gate_s).reshape(NF_G, 128).T),
        up_s2=np.ascontiguousarray(a(L0_ffn_up_s).reshape(NF_G, 128).T),
        down_s_row=np.ascontiguousarray(a(L0_ffn_down_s).reshape(1, D)),
        WoT=np.ascontiguousarray(Wo.T),
        bo_row=np.ascontiguousarray(bo.reshape(1, D)),
        w_fcT=np.ascontiguousarray(a(L1_ffn_w_fc).T),
        w_projT=np.ascontiguousarray(a(L1_ffn_w_proj).T),
        lr_wT=np.ascontiguousarray(lr_w.reshape(D, 1)),
        fg_wT=np.ascontiguousarray(fg_w.reshape(D, 1)),
        lrfgb=np.array([[float(np.asarray(L0_attn_lr_b).reshape(-1)[0]),
                         float(np.asarray(L0_attn_fg_b).reshape(-1)[0])]],
                       dtype=np.float32),
    )
    in_maps = []
    for s in range(NCORES):
        b, half = divmod(s, 2)
        msl = slice(s * MSL, (s + 1) * MSL)
        hs = slice(half * NHL * 128, (half + 1) * NHL * 128)
        hh = slice(half * NHL, (half + 1) * NHL)
        in_maps.append(dict(
            x_own=shards[s], x_dsl=mk_xdsl(x, s),
            W0T_s=np.ascontiguousarray(W0T[:, msl]),
            W1RT_s=np.ascontiguousarray(W1T[:, msl]),
            W1C_s=np.ascontiguousarray(W1[:, msl]),
            b0_2=np.ascontiguousarray(b0[msl].reshape(2, 128).T),
            b1row=np.ascontiguousarray(b1[msl].reshape(1, MSL)),
            oidx=mk_oidx(b, half),
            WqTl=np.ascontiguousarray(Wq.T[:, hs]),
            WkTl=np.ascontiguousarray(Wk.T[:, hs]),
            WvTl=np.ascontiguousarray(Wv.T[:, hs]),
            bq2=np.ascontiguousarray(bq[hs].reshape(NHL, 128).T),
            bk2=np.ascontiguousarray(bk[hs].reshape(NHL, 128).T),
            bv_row=np.ascontiguousarray(bv[hs].reshape(1, NHL * 128)),
            slopes_row=np.ascontiguousarray(slopes[hh].reshape(1, NHL)),
            ctxidx=mk_ctxidx(b, half),
            xnfidx=mk_xnfidx(b),
            **shared))
    res = run_bass_kernel_spmd(nc, in_maps, core_ids=list(range(NCORES)))
    return unshard_rows([res.results[s]["y"] for s in range(NCORES)])
